# version 01
# baseline (speedup 1.0000x reference)
import os
import numpy as np

import concourse.bass as bass
import concourse.mybir as mybir

F32 = mybir.dt.float32
BF16 = mybir.dt.bfloat16
AL = mybir.AluOpType
AF = mybir.ActivationFunctionType
AX = mybir.AxisListType

P = 128
D = 1024
SEQ = 4096
NT = SEQ // P
H = 8
DH = 64
INNER = 512
IFF = 2816
QB = 16
FQ = 512
LN_EPS = 1e-5
L2_EPS = 1e-12
LOG8 = float(np.log(np.float64(8.0)))


class Ctr:
    def __init__(self, nc, name, step=1):
        self._cm = nc.semaphore(name)
        self.sem = self._cm.__enter__()
        self.n = 0
        self.step = step

    def inc(self, inst, k=1):
        inst.then_inc(self.sem, k * self.step)
        self.n += k * self.step
        return self.n

    def wait(self, eng, n=None):
        eng.wait_ge(self.sem, self.n if n is None else n)


def _layout():
    L = {}
    off = [24576]

    def put(name, shape, dtype, at=None):
        nbytes = int(np.prod(shape[1:])) * (2 if dtype == BF16 else 4)
        if dtype == mybir.dt.uint8:
            nbytes = int(np.prod(shape[1:]))
        o = off[0] if at is None else at
        L[name] = (o, shape, dtype)
        if at is None:
            off[0] += (nbytes + 31) // 32 * 32
        return o

    put("wout_b", [P, 4 * D], BF16)
    put("ident", [P, P], BF16)
    put("ones32", [P, 64], F32)
    put("mu_all", [P, NT], F32)
    put("rstd_all", [P, NT], F32)
    put("smalls", [P, 64], F32)
    put("consts", [P, 8], F32)
    r1 = off[0]
    put("qT", [P, 4 * SEQ], BF16)
    put("kT", [P, 4 * SEQ], BF16)
    put("vaug", [P, NT * 520], BF16)
    r1_end = off[0]
    put("gateT", [P, 8 * IFF], BF16, at=r1)
    put("upT", [P, 8 * IFF], BF16, at=r1 + 8 * IFF * 2)
    assert r1 + 2 * 8 * IFF * 2 <= r1_end
    r2 = off[0]
    put("wqkv_b", [P, 8 * 3 * INNER], BF16, at=r2)
    a = r2 + 8 * 3 * INNER * 2
    for nm, shape, dt in (("lnw_rep", [P, D], F32), ("lnb_rep", [P, D], F32),
                          ("xf", [P, 2 * D], F32), ("xn", [P, D], F32),
                          ("xn2b", [P, D], BF16),
                          ("xn2T", [P, 8 * P], BF16), ("qkb", [P, 2 * INNER], BF16),
                          ("sqs2", [P, D], F32), ("wstage", [P, 768], F32),
                          ("sqs", [P, INNER], F32)):
        n = int(np.prod(shape[1:])) * (2 if dt == BF16 else 4)
        put(nm, shape, dt, at=a)
        a += (n + 31) // 32 * 32
    r2_a_end = a
    put("downT", [P, 22 * D], BF16, at=r2)
    b = r2 + 22 * D * 2
    bsmall = b
    for i in range(2):
        put(f"pt_a{i}", [P, INNER], BF16, at=b); b += INNER * 2
    for i in range(2):
        put(f"pt_b{i}", [P, INNER], BF16, at=b); b += INNER * 2
    put("rsrow", [P, 2 * INNER], F32, at=b); b += 2 * INNER * 4
    put("rsln", [P, 2 * INNER], F32, at=b); b += 2 * INNER * 4
    put("rrep", [64, 2 * INNER], F32, at=b); b += 2 * INNER * 4
    put("oTs", [64, 2 * INNER], BF16, at=b); b += 2 * INNER * 2
    put("maskb", [P, 2 * INNER], BF16, at=b); b += 2 * INNER * 2
    assert b <= r2_a_end + 8192, (b, r2_a_end)
    c = bsmall
    for nm, shape, dt in (("xf_c", [P, 2 * D], F32), ("xo", [P, D], F32),
                          ("xq", [P, D], BF16), ("xqT", [P, 8 * P], BF16),
                          ("oTt", [P, 4 * P], BF16), ("sg_sb", [P, INNER], F32)):
        n = int(np.prod(shape[1:])) * (2 if dt == BF16 else 4)
        put(nm, shape, dt, at=c)
        c += (n + 31) // 32 * 32
    r3 = max(r2_a_end, b, c)
    put("wstg", [P, 1408], F32, at=r3)
    put("h_sb", [P, IFF], F32, at=r3)
    o = r3 + IFF * 4
    put("wqb", [P, IFF], BF16, at=o); o += IFF * 2
    for nm in ("qs_m", "qs_s", "qs_r"):
        put(nm, [P, FQ // QB], F32, at=o); o += (FQ // QB * 4 + 31) // 32 * 32
    for nm in ("qs_a", "qs_u", "qs_g"):
        put(nm, [P, FQ], F32, at=o); o += FQ * 4
    put("hqT", [P, IFF], BF16, at=o); o += IFF * 2
    assert o <= 229376, ("layout overflow", o)
    return L


class MK:
    def __init__(self, debug=None):
        self.debug = debug
        nc = bass.Bass("TRN2", target_bir_lowering=False,
                       detect_race_conditions=False)
        self.nc = nc
        self.x_h = nc.dram_tensor("x", [SEQ, D], F32, kind="ExternalInput").ap()
        self.lnw_h = nc.dram_tensor("ln_w", [D], F32, kind="ExternalInput").ap()
        self.lnb_h = nc.dram_tensor("ln_b", [D], F32, kind="ExternalInput").ap()
        self.wqkv_h = nc.dram_tensor("w_qkv", [D, 3 * INNER], F32, kind="ExternalInput").ap()
        self.wout_h = nc.dram_tensor("w_out", [INNER, D], F32, kind="ExternalInput").ap()
        self.wgate_h = nc.dram_tensor("w_gate", [IFF, D], F32, kind="ExternalInput").ap()
        self.wup_h = nc.dram_tensor("w_up", [IFF, D], F32, kind="ExternalInput").ap()
        self.wdown_h = nc.dram_tensor("w_down", [D, IFF], F32, kind="ExternalInput").ap()
        self.y_h = nc.dram_tensor("y", [SEQ, D], F32, kind="ExternalOutput").ap()
        self.oT_h = nc.dram_tensor("oT_scr", [NT, 4, P, P], BF16, kind="Internal").ap()
        self._handles = {}
        self._layout = _layout()

    def sb(self, name, shape=None, dtype=None):
        off, lshape, ldt = self._layout[name]
        assert shape is None or list(shape) == list(lshape), (name, shape, lshape)
        h = self.nc.alloc_sbuf_tensor_at(name, lshape, ldt, offset=off)
        self._handles[name] = h
        return h.ap()

    def free(self, *names):
        for n in names:
            self._handles.pop(n, None)
            if hasattr(self, n):
                delattr(self, n)

    def build(self):
        nc = self.nc
        self.sX = Ctr(nc, "c_sX", 16)
        self.sW = Ctr(nc, "c_sW", 16)
        self.sO = Ctr(nc, "c_sO", 16)
        self.sY = Ctr(nc, "c_sY", 16)
        self.T = Ctr(nc, "c_T")
        self.V = Ctr(nc, "c_V")
        self.S = Ctr(nc, "c_S")
        self.G = Ctr(nc, "c_G")

        with (nc.psum_tensor("ps_q", [P, INNER], F32) as ps_q,
              nc.psum_tensor("ps_k", [P, INNER], F32) as ps_k,
              nc.psum_tensor("ps_v", [P, INNER], F32) as ps_v,
              nc.psum_tensor("ps_tr", [P, INNER], F32) as ps_tr,
              nc.psum_tensor("ps_oa", [P, INNER], F32) as ps_oa,
              nc.psum_tensor("ps_ob", [P, INNER], F32) as ps_ob,
              nc.psum_tensor("ps_r", [P, INNER], F32) as ps_r,
              nc.Block() as block):
            self.ps = dict(q=ps_q[:], k=ps_k[:], v=ps_v[:], tr=ps_tr[:],
                           oa=ps_oa[:], ob=ps_ob[:], r=ps_r[:])

            @block.gpsimd
            def _(g):
                self.emit_setup()
                self.emit_phaseA()
                self.emit_phaseB()
                self.emit_phaseC()
                self.emit_tail()
        return nc

    def emit_setup(self):
        nc = self.nc
        v, s, te, gp, sy = nc.vector, nc.scalar, nc.tensor, nc.gpsimd, nc.sync
        V, S, T, G, sX, sW = self.V, self.S, self.T, self.G, self.sX, self.sW

        self.wqkv_b = self.sb("wqkv_b", [P, 8 * 3 * INNER], BF16)
        self.wout_b = self.sb("wout_b", [P, 4 * D], BF16)
        self.lnw_rep = self.sb("lnw_rep", [P, D], F32)
        self.lnb_rep = self.sb("lnb_rep", [P, D], F32)
        self.ident = self.sb("ident", [P, P], BF16)
        self.ones32 = self.sb("ones32", [P, 64], F32)
        self.qT = self.sb("qT", [P, 4 * SEQ], BF16)
        self.kT = self.sb("kT", [P, 4 * SEQ], BF16)
        self.vaug = self.sb("vaug", [P, NT * 520], BF16)
        self.mu_all = self.sb("mu_all", [P, NT], F32)
        self.rstd_all = self.sb("rstd_all", [P, NT], F32)
        self.sqs = self.sb("sqs", [P, INNER], F32)
        self.smalls = self.sb("smalls", [P, 64], F32)
        self.xf = self.sb("xf", [P, 2 * D], F32)
        self.xn = self.sb("xn", [P, D], F32)
        self.xn2b = self.sb("xn2b", [P, D], BF16)
        self.xn2T = self.sb("xn2T", [P, 8 * P], BF16)
        self.qkb = self.sb("qkb", [P, 2 * INNER], BF16)
        self.sqs2 = self.sb("sqs2", [P, D], F32)
        self.wstage = self.sb("wstage", [P, 768], F32)

        self.consts = self.sb("consts", [P, 8], F32)
        for i, val in enumerate([LN_EPS, L2_EPS, LOG8]):
            G.inc(gp.memset(self.consts[:, i:i + 1], val))
            nc.const_aps.register(F32, val, self.consts[:, i:i + 1])
        G.wait(s)
        G.wait(v)

        gp.memset(self.ident, 0.0)
        G.inc(gp.affine_select(out=self.ident, in_=self.ident, compare_op=AL.not_equal,
                               fill=1.0, base=0, pattern=[[-1, P]], channel_multiplier=1))
        ones_ap = bass.AP(self._handles["vaug"], 64, [[520 * NT, P], [520, NT], [65, 8]])
        G.inc(gp.memset(ones_ap, 1.0))
        V.inc(v.memset(self.ones32, 1.0))

        sX.inc(sy.dma_start(out=self.lnw_rep,
                            in_=self.lnw_h.rearrange("(o d) -> o d", o=1).broadcast_to([P, D])))
        sX.inc(sy.dma_start(out=self.lnb_rep,
                            in_=self.lnb_h.rearrange("(o d) -> o d", o=1).broadcast_to([P, D])))

        for kc in range(8):
            for hh in range(2):
                V.wait(sy)
                sW.inc(sy.dma_start(out=self.wstage[:, :768],
                                    in_=self.wqkv_h[kc * P:(kc + 1) * P,
                                                    hh * 768:(hh + 1) * 768]))
                sW.wait(v)
                V.inc(v.tensor_copy(
                    self.wqkv_b[:, kc * 1536 + hh * 768: kc * 1536 + (hh + 1) * 768],
                    self.wstage[:, :768]))
        for kc in range(4):
            for hh in range(2):
                V.wait(sy)
                sW.inc(sy.dma_start(out=self.wstage[:, :512],
                                    in_=self.wout_h[kc * P:(kc + 1) * P,
                                                    hh * 512:(hh + 1) * 512]))
                sW.wait(v)
                V.inc(v.tensor_copy(
                    self.wout_b[:, kc * D + hh * 512: kc * D + (hh + 1) * 512],
                    self.wstage[:, :512]))

    def emit_ln_stats(self, x_ap, sq_scr, mu_out, rstd_out, col):
        nc = self.nc
        v, s = nc.vector, nc.scalar
        V, S = self.V, self.S
        sm = self.smalls
        c0, c1, c2, c3 = col, col + 1, col + 2, col + 3
        V.inc(v.tensor_reduce(sm[:, c0:c0 + 1], x_ap, AX.X, AL.add))
        S.inc(s.activation(sq_scr, x_ap, AF.Square, accum_out=sm[:, c1:c1 + 1]))
        V.wait(v)
        V.inc(v.tensor_scalar(mu_out, sm[:, c0:c0 + 1], 1.0 / D, None, AL.mult))
        S.wait(v)
        V.inc(v.tensor_scalar(sm[:, c2:c2 + 1], sm[:, c1:c1 + 1], 1.0 / D, None, AL.mult))
        V.wait(v)
        V.inc(v.tensor_tensor(sm[:, c3:c3 + 1], mu_out, mu_out, AL.mult))
        V.wait(v)
        V.inc(v.tensor_sub(sm[:, c2:c2 + 1], sm[:, c2:c2 + 1], sm[:, c3:c3 + 1]))
        V.wait(s)
        S.inc(s.activation(sm[:, c3:c3 + 1], sm[:, c2:c2 + 1], AF.Ln, bias=LN_EPS))
        S.wait(s)
        S.inc(s.activation(rstd_out, sm[:, c3:c3 + 1], AF.Exp, scale=-0.5))
        S.wait(self.nc.vector)

    def emit_quant(self, x_ap, out_ap, F, scr):
        nc = self.nc
        v, s = nc.vector, nc.scalar
        V, S = self.V, self.S
        nb = F // QB
        m, sc2, r6 = scr["m"], scr["sc2"], scr["r6"]
        a, u, sgn = scr["a"], scr["u"], scr["sgn"]
        x3 = x_ap.rearrange("p (b s) -> p b s", s=QB)
        V.inc(v.tensor_reduce(m[:, :nb], x3, AX.X, AL.max, apply_absolute_value=True))
        V.wait(v)
        v.tensor_scalar(sc2[:, :nb], m[:, :nb], 1.0 / 12.0, 5e-13, AL.mult, AL.max)
        V.inc(v.tensor_scalar(r6[:, :nb], m[:, :nb], 1.0 / 6.0, 1e-12, AL.mult, AL.max))
        V.wait(v)
        V.inc(v.reciprocal(r6[:, :nb], r6[:, :nb]))
        V.wait(s)
        s.activation(a[:, :F], x_ap, AF.Abs)
        S.inc(nc.scalar.sign(sgn[:, :F], x_ap))
        V.wait(v)
        S.wait(v)
        r6b = r6[:, :nb].broadcast_to([P, nb, QB])
        sc2b = sc2[:, :nb].broadcast_to([P, nb, QB])
        a3 = a[:, :F].rearrange("p (b s) -> p b s", s=QB)
        aa, uu = a[:, :F], u[:, :F]

        def st(inst):
            V.inc(inst)
            V.wait(v)

        st(v.tensor_tensor(a3, a3, r6b, AL.mult))
        st(v.tensor_scalar(uu, aa, 5.0, 2.0, AL.is_gt, AL.mult))
        st(v.scalar_tensor_tensor(uu, aa, 3.5, uu, AL.is_gt, AL.add))
        st(v.scalar_tensor_tensor(uu, aa, 2.5, uu, AL.is_gt, AL.add))
        st(v.tensor_scalar(uu, uu, 2.0, None, AL.mult))
        st(v.scalar_tensor_tensor(uu, aa, 1.75, uu, AL.is_gt, AL.add))
        st(v.scalar_tensor_tensor(uu, aa, 1.25, uu, AL.is_gt, AL.add))
        st(v.scalar_tensor_tensor(uu, aa, 0.75, uu, AL.is_gt, AL.add))
        st(v.scalar_tensor_tensor(uu, aa, 0.25, uu, AL.is_gt, AL.add))
        st(v.tensor_tensor(uu, uu, sgn[:, :F], AL.mult))
        st(v.tensor_tensor(out_ap.rearrange("p (b s) -> p b s", s=QB),
                           u[:, :F].rearrange("p (b s) -> p b s", s=QB), sc2b, AL.mult))

    def emit_phaseA(self):
        nc = self.nc
        v, s, te, gp, sy = nc.vector, nc.scalar, nc.tensor, nc.gpsimd, nc.sync
        V, S, T, sX = self.V, self.S, self.T, self.sX
        ps = self.ps
        trb = ps["tr"].bitcast(BF16)

        vfree = [0] * NT
        sfree = [0] * NT

        for t in range(NT):
            par = t % 2
            xt = self.xf[:, par * D:(par + 1) * D]
            if t >= 2:
                sy.wait_ge(V.sem, vfree[t - 2])
                sy.wait_ge(S.sem, sfree[t - 2])
            sX.inc(sy.dma_start(out=xt, in_=self.x_h[t * P:(t + 1) * P, :]))
            sX.wait(v)
            sX.wait(s)
            self.emit_ln_stats(xt, self.sqs2, self.mu_all[:, t:t + 1],
                               self.rstd_all[:, t:t + 1], 0)
            V.inc(v.tensor_scalar(self.xn, xt, self.mu_all[:, t:t + 1],
                                  self.rstd_all[:, t:t + 1], AL.subtract, AL.mult))
            vfree[t] = V.n
            V.wait(v)
            V.wait(s)
            self.emit_ln_stats(self.xn, self.sqs2, self.smalls[:, 8:9],
                               self.smalls[:, 9:10], 10)
            sfree[t] = S.n
            V.inc(v.tensor_scalar(self.sqs2, self.xn, self.smalls[:, 8:9],
                                  self.smalls[:, 9:10], AL.subtract, AL.mult))
            V.wait(v)
            V.inc(v.tensor_tensor(self.sqs2, self.sqs2, self.lnw_rep, AL.mult))
            V.wait(v)
            V.inc(v.tensor_tensor(self.xn2b, self.sqs2, self.lnb_rep, AL.add))
            V.wait(te)
            for j in range(8):
                inst = te.transpose(trb[:, j * P:(j + 1) * P],
                                    self.xn2b[:, j * P:(j + 1) * P], self.ident)
            T.inc(inst)
            T.wait(v)
            V.inc(v.tensor_copy(self.xn2T, trb))
            V.wait(te)
            for bank, psn in enumerate(("q", "k", "v")):
                for kc in range(8):
                    inst = nc.tensor.matmul(
                        ps[psn], self.xn2T[:, kc * P:(kc + 1) * P],
                        self.wqkv_b[:, kc * 1536 + bank * INNER:
                                    kc * 1536 + (bank + 1) * INNER],
                        start=(kc == 0), stop=(kc == 7))
            T.inc(inst)
            T.wait(s)
            T.wait(v)
            sm = self.smalls
            S.inc(s.activation(self.sqs, ps["q"], AF.Square))
            S.wait(v)
            V.inc(v.tensor_reduce(sm[:, 16:24],
                                  self.sqs.rearrange("p (h c) -> p h c", c=DH),
                                  AX.X, AL.add))
            V.wait(s)
            S.inc(s.activation(sm[:, 24:32], sm[:, 16:24], AF.Ln, bias=L2_EPS))
            S.wait(s)
            S.inc(s.activation(sm[:, 32:40], sm[:, 24:32], AF.Exp, scale=-0.5, bias=LOG8))
            S.wait(v)
            V.inc(v.tensor_tensor(self.qkb[:, 0:INNER].rearrange("p (h c) -> p h c", c=DH),
                                  ps["q"].rearrange("p (h c) -> p h c", c=DH),
                                  sm[:, 32:40].broadcast_to([P, H, DH]), AL.mult))
            V.wait(s)
            S.inc(s.activation(self.sqs, ps["k"], AF.Square))
            S.wait(v)
            V.inc(v.tensor_reduce(sm[:, 16:24],
                                  self.sqs.rearrange("p (h c) -> p h c", c=DH),
                                  AX.X, AL.add))
            V.wait(s)
            S.inc(s.activation(sm[:, 24:32], sm[:, 16:24], AF.Ln, bias=L2_EPS))
            S.wait(s)
            S.inc(s.activation(sm[:, 32:40], sm[:, 24:32], AF.Exp, scale=-0.5))
            S.wait(v)
            V.inc(v.tensor_tensor(self.qkb[:, INNER:2 * INNER]
                                  .rearrange("p (h c) -> p h c", c=DH),
                                  ps["k"].rearrange("p (h c) -> p h c", c=DH),
                                  sm[:, 32:40].broadcast_to([P, H, DH]), AL.mult))
            vdst = bass.AP(self._handles["vaug"], t * 520,
                           [[520 * NT, P], [65, 8], [1, DH]])
            V.inc(v.tensor_copy(vdst, ps["v"].rearrange("p (h c) -> p h c", c=DH)))
            V.wait(te)
            for j in range(8):
                inst = te.transpose(trb[:, j * P:(j + 1) * P],
                                    self.qkb[:, j * P:(j + 1) * P], self.ident)
            T.inc(inst)
            T.wait(v)
            qdst = bass.AP(self._handles["qT"], t * P, [[4 * SEQ, P], [SEQ, 4], [1, P]])
            kdst = bass.AP(self._handles["kT"], t * P, [[4 * SEQ, P], [SEQ, 4], [1, P]])
            V.inc(v.tensor_copy(qdst, trb[:, 0:INNER].rearrange("p (c q) -> p c q", q=P)))
            V.inc(v.tensor_copy(kdst, trb[:, INNER:2 * INNER]
                                .rearrange("p (c q) -> p c q", q=P)))
            vfree[t] = V.n
            sfree[t] = S.n


    def emit_phaseB(self):
        nc = self.nc
        v, s, te, gp, sy = nc.vector, nc.scalar, nc.tensor, nc.gpsimd, nc.sync
        V, S, T, sW, sO = self.V, self.S, self.T, self.sW, self.sO
        ps = self.ps
        trb = ps["tr"].bitcast(BF16)

        self.free("wqkv_b", "xn", "xn2T", "wstage", "lnw_rep", "lnb_rep",
                  "xf", "sqs2", "qkb", "xn2b")
        self.downT = self.sb("downT", [P, 22 * D], BF16)
        self.wstg = self.sb("wstg", [P, 1408], F32)
        self.wqb = self.sb("wqb", [P, IFF], BF16)
        self.qs_m = self.sb("qs_m", [P, FQ // QB], F32)
        self.qs_s = self.sb("qs_s", [P, FQ // QB], F32)
        self.qs_r = self.sb("qs_r", [P, FQ // QB], F32)
        self.qs_a = self.sb("qs_a", [P, FQ], F32)
        self.qs_u = self.sb("qs_u", [P, FQ], F32)
        self.qs_g = self.sb("qs_g", [P, FQ], F32)
        self.qscr = dict(m=self.qs_m, sc2=self.qs_s, r6=self.qs_r,
                         a=self.qs_a, u=self.qs_u, sgn=self.qs_g)
        self.pt_a = [self.sb(f"pt_a{i}", [P, INNER], BF16) for i in range(2)]
        self.pt_b = [self.sb(f"pt_b{i}", [P, INNER], BF16) for i in range(2)]
        self.rsrow = self.sb("rsrow", [P, 2 * INNER], F32)
        self.maskb = self.sb("maskb", [P, 2 * INNER], BF16)
        gp.memset(self.maskb, 1.0)
        self.G.inc(gp.affine_select(out=self.maskb, in_=self.maskb, compare_op=AL.is_ge,
                                    fill=0.0, base=-INNER, pattern=[[1, 2 * INNER]],
                                    channel_multiplier=-1))
        self.G.wait(v)
        self.G.wait(s)
        self.rsln = self.sb("rsln", [P, 2 * INNER], F32)
        self.rrep = self.sb("rrep", [64, 2 * INNER], F32)
        self.oTs = self.sb("oTs", [64, 2 * INNER], BF16)

        for dc in range(8):
            V.wait(sy)
            S.wait(sy)
            for hh in range(2):
                if hh:
                    V.wait(sy)
                    S.wait(sy)
                sW.inc(sy.dma_start(out=self.wstg,
                                    in_=self.wdown_h[dc * P:(dc + 1) * P,
                                                     hh * 1408:(hh + 1) * 1408]))
                sW.wait(v)
                sW.wait(s)
                for c0, cw in ((0, 512), (512, 512), (1024, 384)):
                    self.emit_quant(self.wstg[:, c0:c0 + cw],
                                    self.wqb[:, hh * 1408 + c0: hh * 1408 + c0 + cw],
                                    cw, self.qscr)
            V.wait(te)
            for grp in ((0, 8), (8, 16), (16, 22)):
                g0, g1 = grp
                for j in range(g0, g1):
                    inst = te.transpose(trb[:, (j - g0) * P:(j - g0 + 1) * P],
                                        self.wqb[:, j * P:(j + 1) * P], self.ident)
                T.inc(inst)
                T.wait(v)
                dst = bass.AP(self._handles["downT"], dc * P + g0 * D,
                              [[22 * D, P], [D, g1 - g0], [1, P]])
                V.inc(v.tensor_copy(dst, trb[:, 0:(g1 - g0) * P]
                                    .rearrange("p (c q) -> p c q", q=P)))
                V.wait(te)

        sa_banks = [ps["q"], ps["k"]]
        sb_banks = [ps["v"], ps["tr"]]
        pta, ptb = self.pt_a, self.pt_b
        exp_done = {}
        av_done = {}
        first_chunk = True
        for hp in range(4):
            for ci in range(8):
                nj = 4 * ci + 4
                qs = self.qT[:, hp * SEQ + ci * INNER: hp * SEQ + (ci + 1) * INNER]
                S.wait(te)
                V.wait(te)
                for j in range(nj):
                    par = j % 2
                    ks = self.kT[:, hp * SEQ + j * P: hp * SEQ + (j + 1) * P]
                    if (hp, ci, j - 2) in exp_done:
                        te.wait_ge(S.sem, exp_done[(hp, ci, j - 2)])
                    nc.tensor.matmul(sa_banks[par], ks[0:64, :], qs[0:64, :],
                                     start=True, stop=True, tile_position=(0, 0))
                    inst = nc.tensor.matmul(sb_banks[par], ks[64:128, :], qs[64:128, :],
                                            start=True, stop=True, tile_position=(64, 0))
                    T.inc(inst)
                    tS = T.n
                    s.wait_ge(T.sem, tS)
                    if (hp, ci, j - 2) in av_done:
                        s.wait_ge(T.sem, av_done[(hp, ci, j - 2)])
                    s.activation(pta[par], sa_banks[par], AF.Exp)
                    S.inc(s.activation(ptb[par], sb_banks[par], AF.Exp))
                    exp_done[(hp, ci, j)] = S.n
                    diag = (j >= 4 * ci)
                    if diag:
                        r = j * P - ci * INNER
                        msl = self.maskb[:, INNER - r: 2 * INNER - r]
                        S.wait(v)
                        v.tensor_tensor(pta[par], pta[par], msl, AL.mult)
                        V.inc(v.tensor_tensor(ptb[par], ptb[par], msl, AL.mult))
                        V.wait(te)
                    else:
                        te.wait_ge(S.sem, exp_done[(hp, ci, j)])
                    va = self.vaug[:, j * 520 + (2 * hp) * 65:
                                   j * 520 + (2 * hp) * 65 + 65]
                    vb = self.vaug[:, j * 520 + (2 * hp + 1) * 65:
                                   j * 520 + (2 * hp + 1) * 65 + 65]
                    nc.tensor.matmul(ps["oa"][0:65, :], va, pta[par],
                                     start=(j == 0), stop=(j == nj - 1))
                    inst = nc.tensor.matmul(ps["ob"][0:65, :], vb, ptb[par],
                                            start=(j == 0), stop=(j == nj - 1))
                    T.inc(inst)
                    av_done[(hp, ci, j)] = T.n
                T.wait(v)
                V.inc(v.tensor_copy(self.rsrow[64:65, 0:INNER], ps["oa"][64:65, :]))
                V.inc(v.tensor_copy(self.rsrow[64:65, INNER:], ps["ob"][64:65, :]))
                V.wait(s)
                S.inc(s.activation(self.rsln[64:65, :], self.rsrow[64:65, :], AF.Ln))
                S.wait(s)
                S.inc(s.activation(self.rsrow[64:65, :], self.rsln[64:65, :],
                                   AF.Exp, scale=-1.0))
                S.wait(te)
                nc.tensor.matmul(ps["r"][0:64, :], self.ones32[64:65, :],
                                 self.rsrow[64:65, 0:INNER], start=True, stop=True)
                inst = nc.tensor.matmul(ps["q"][0:64, :], self.ones32[64:65, :],
                                        self.rsrow[64:65, INNER:], start=True, stop=True)
                T.inc(inst)
                T.wait(v)
                V.inc(v.tensor_copy(self.rrep[:, 0:INNER], ps["r"][0:64, :]))
                V.inc(v.tensor_copy(self.rrep[:, INNER:], ps["q"][0:64, :]))
                V.wait(v)
                v.wait_ge(self.sO.sem, self.sO.n)
                V.inc(v.tensor_tensor(self.oTs[:, 0:INNER], ps["oa"][0:64, :],
                                      self.rrep[:, 0:INNER], AL.mult))
                V.inc(v.tensor_tensor(self.oTs[:, INNER:], ps["ob"][0:64, :],
                                      self.rrep[:, INNER:], AL.mult))
                V.wait(sy)
                base = (4 * ci * 4 + hp) * P * P
                dsta = bass.AP(self.oT_h.tensor, base, [[P, 64], [4 * P * P, 4], [1, P]])
                dstb = bass.AP(self.oT_h.tensor, base + 64 * P,
                               [[P, 64], [4 * P * P, 4], [1, P]])
                self.sO.inc(sy.dma_start(
                    out=dsta, in_=self.oTs[:, 0:INNER].rearrange("p (t q) -> p t q", q=P)))
                self.sO.inc(sy.dma_start(
                    out=dstb, in_=self.oTs[:, INNER:].rearrange("p (t q) -> p t q", q=P)))

    def emit_phaseC(self):
        nc = self.nc
        v, s, te, gp, sy = nc.vector, nc.scalar, nc.tensor, nc.gpsimd, nc.sync
        V, S, T, sW, sX, sY = self.V, self.S, self.T, self.sW, self.sX, self.sY
        ps = self.ps
        trb = ps["tr"].bitcast(BF16)

        self.free("qT", "kT", "vaug", "pt_a0", "pt_a1", "pt_b0", "pt_b1",
                  "rsrow", "rsln", "rrep", "oTs", "maskb")
        self.gateT = self.sb("gateT", [P, 8 * IFF], BF16)
        self.upT = self.sb("upT", [P, 8 * IFF], BF16)

        for name, src in (("gateT", self.wgate_h), ("upT", self.wup_h)):
            dstT = self._handles[name]
            for rc in range(22):
                V.wait(sy)
                S.wait(sy)
                sW.inc(sy.dma_start(out=self.wstg[:, :D], in_=src[rc * P:(rc + 1) * P, :]))
                sW.wait(v)
                sW.wait(s)
                for half in range(2):
                    self.emit_quant(self.wstg[:, half * 512:(half + 1) * 512],
                                    self.wqb[:, half * 512:(half + 1) * 512], 512, self.qscr)
                V.wait(te)
                for j in range(8):
                    inst = te.transpose(trb[:, j * P:(j + 1) * P],
                                        self.wqb[:, j * P:(j + 1) * P], self.ident)
                T.inc(inst)
                T.wait(v)
                dst = bass.AP(dstT, rc * P, [[8 * IFF, P], [IFF, 8], [1, P]])
                V.inc(v.tensor_copy(dst, trb[:, 0:D].rearrange("p (c q) -> p c q", q=P)))
                V.wait(te)

        self.free("wstg")
        self.xf_c = self.sb("xf_c", [P, 2 * D], F32)
        self.xo = self.sb("xo", [P, D], F32)
        self.xq = self.sb("xq", [P, D], BF16)
        self.xqT = self.sb("xqT", [P, 8 * P], BF16)
        self.h_sb = self.sb("h_sb", [P, IFF], F32)
        self.hqT = self.sb("hqT", [P, IFF], BF16)
        self.oTt = self.sb("oTt", [P, 4 * P], BF16)
        self.sg_sb = self.sb("sg_sb", [P, INNER], F32)

        NCH = 6
        vfree = {}
        sfree = {}
        for t in range(NT):
            par = t % 2
            xt = self.xf_c[:, par * D:(par + 1) * D]
            ot = self.oTt[:, 0:4 * P]
            if t >= 2:
                sy.wait_ge(V.sem, vfree[t - 2])
                sy.wait_ge(S.sem, sfree[t - 2])
            sX.inc(sy.dma_start(out=xt, in_=self.x_h[t * P:(t + 1) * P, :]))
            if t == 0:
                self.sO.wait(sy)
            T.wait(sy)
            sX.inc(sy.dma_start(out=ot.rearrange("p (c q) -> p c q", q=P),
                                in_=self.oT_h[t].rearrange("c p q -> p c q")))
            sX.wait(v)
            sX.wait(s)
            sX.wait(te)
            V.inc(v.tensor_scalar(self.xo, xt, self.mu_all[:, t:t + 1],
                                  self.rstd_all[:, t:t + 1], AL.subtract, AL.mult))
            V.wait(v)
            V.wait(s)
            for half in range(2):
                self.emit_quant(self.xo[:, half * 512:(half + 1) * 512],
                                self.xq[:, half * 512:(half + 1) * 512], 512, self.qscr)
            V.wait(te)
            for j in range(8):
                inst = te.transpose(trb[:, j * P:(j + 1) * P],
                                    self.xq[:, j * P:(j + 1) * P], self.ident)
            T.inc(inst)
            T.wait(v)
            V.inc(v.tensor_copy(self.xqT, trb))
            V.wait(te)
            for ic in range(NCH):
                w = INNER if ic < 5 else (IFF - 5 * INNER)
                c0 = ic * INNER
                S.wait(te)
                V.wait(te)
                for kc in range(8):
                    nc.tensor.matmul(ps["q"][:, 0:w],
                                     self.xqT[:, kc * P:(kc + 1) * P],
                                     self.gateT[:, kc * IFF + c0: kc * IFF + c0 + w],
                                     start=(kc == 0), stop=(kc == 7))
                for kc in range(8):
                    inst = nc.tensor.matmul(ps["k"][:, 0:w],
                                            self.xqT[:, kc * P:(kc + 1) * P],
                                            self.upT[:, kc * IFF + c0: kc * IFF + c0 + w],
                                            start=(kc == 0), stop=(kc == 7))
                T.inc(inst)
                tGU = T.n
                s.wait_ge(T.sem, tGU)
                S.inc(s.activation(self.sg_sb[:, 0:w], ps["q"][:, 0:w],
                                   AF.Silu))
                S.wait(v)
                v.wait_ge(T.sem, tGU)
                V.inc(v.tensor_tensor(self.h_sb[:, c0:c0 + w], ps["k"][:, 0:w],
                                      self.sg_sb[:, 0:w], AL.mult))
            V.wait(v)
            V.wait(s)
            for c0 in range(0, IFF, 512):
                cw = min(512, IFF - c0)
                self.emit_quant(self.h_sb[:, c0:c0 + cw],
                                self.wqb[:, c0:c0 + cw], cw, self.qscr)
            for grp in ((0, 8), (8, 16), (16, 22)):
                g0, g1 = grp
                V.wait(te)
                for j in range(g0, g1):
                    inst = te.transpose(trb[:, (j - g0) * P:(j - g0 + 1) * P],
                                        self.wqb[:, j * P:(j + 1) * P], self.ident)
                T.inc(inst)
                T.wait(v)
                V.inc(v.tensor_copy(self.hqT[:, g0 * P:g1 * P], trb[:, 0:(g1 - g0) * P]))
                V.wait(te)
            for nb, bank in enumerate(("oa", "ob")):
                for cb in range(4):
                    nc.tensor.matmul(
                        ps[bank], ot[:, cb * P:(cb + 1) * P],
                        self.wout_b[:, cb * D + nb * INNER: cb * D + (nb + 1) * INNER],
                        start=(cb == 0), stop=False)
                for kb in range(22):
                    inst = nc.tensor.matmul(
                        ps[bank], self.hqT[:, kb * P:(kb + 1) * P],
                        self.downT[:, kb * D + nb * INNER: kb * D + (nb + 1) * INNER],
                        start=False, stop=(kb == 21))
            T.inc(inst)
            T.wait(v)
            v.wait_ge(self.sY.sem, self.sY.n)
            V.inc(v.tensor_tensor(self.xo[:, 0:INNER], ps["oa"], xt[:, 0:INNER], AL.add))
            V.inc(v.tensor_tensor(self.xo[:, INNER:D], ps["ob"], xt[:, INNER:D], AL.add))
            vfree[t] = V.n
            sfree[t] = S.n
            V.wait(sy)
            sY.inc(sy.dma_start(out=self.y_h[t * P:(t + 1) * P, :], in_=self.xo))
            V.wait(te)

    def emit_tail(self):
        gp = self.nc.gpsimd
        self.sY.wait(gp)
        self.sO.wait(gp)
        self.sX.wait(gp)
        self.sW.wait(gp)
        self.T.wait(gp)
        self.V.wait(gp)
        self.S.wait(gp)


def build(debug=None):
    return MK(debug=debug).build()


def kernel(L0_attn_x, L0_attn_ln_w, L0_attn_ln_b, L0_attn_w_qkv, L0_attn_w_out,
           L0_ffn_w_gate, L0_ffn_w_up, L0_ffn_w_down):
    from concourse.bass_utils import run_bass_kernel_spmd
    x = np.ascontiguousarray(np.asarray(L0_attn_x, dtype=np.float32))
    lnw = np.ascontiguousarray(np.asarray(L0_attn_ln_w, dtype=np.float32))
    lnb = np.ascontiguousarray(np.asarray(L0_attn_ln_b, dtype=np.float32))
    wqkv = np.ascontiguousarray(np.asarray(L0_attn_w_qkv, dtype=np.float32))
    wout = np.ascontiguousarray(np.asarray(L0_attn_w_out, dtype=np.float32))
    wg = np.ascontiguousarray(np.asarray(L0_ffn_w_gate, dtype=np.float32))
    wu = np.ascontiguousarray(np.asarray(L0_ffn_w_up, dtype=np.float32))
    wd = np.ascontiguousarray(np.asarray(L0_ffn_w_down, dtype=np.float32))
    nc = build()
    in_maps = [dict(x=np.ascontiguousarray(x[b]), ln_w=lnw, ln_b=lnb, w_qkv=wqkv,
                    w_out=wout, w_gate=wg, w_up=wu, w_down=wd) for b in range(8)]
    res = run_bass_kernel_spmd(nc, in_maps, core_ids=list(range(8)))
    return np.stack([np.asarray(res.results[b]["y"]) for b in range(8)], axis=0)


# version 02
# speedup vs baseline: 1.1535x; 1.1535x over previous
import numpy as np

import concourse.bass as bass
import concourse.mybir as mybir

F32 = mybir.dt.float32
BF16 = mybir.dt.bfloat16
AL = mybir.AluOpType
AF = mybir.ActivationFunctionType
AX = mybir.AxisListType

P = 128
D = 1024
SEQ = 4096
NT = SEQ // P
H = 8
DH = 64
INNER = 512
IFF = 2816
QB = 16
LN_EPS = 1e-5
L2_EPS = 1e-12
LOG8 = float(np.log(np.float64(8.0)))
ARENA = 24576
PART_SZ = 229376


class Ctr:
    def __init__(self, nc, name, step=1):
        self._cm = nc.semaphore(name)
        self.sem = self._cm.__enter__()
        self.n = 0
        self.step = step

    def inc(self, inst, k=1):
        inst.then_inc(self.sem, k * self.step)
        self.n += k * self.step
        return self.n

    def wait(self, eng, n=None):
        v = self.n if n is None else n
        if v > 0:
            eng.wait_ge(self.sem, v)


def _layout():
    L = {}

    def put(name, off, shape, dtype):
        L[name] = (off, shape, dtype)
        nb = int(np.prod(shape[1:])) * (2 if dtype == BF16 else 4)
        return off + (nb + 31) // 32 * 32

    o = ARENA
    o = put("ident", o, [P, P], BF16)
    o = put("ones65", o, [P, 64], F32)
    o = put("mu_all", o, [P, NT], F32)
    o = put("rstd_all", o, [P, NT], F32)
    o = put("smalls", o, [P, 64], F32)
    o = put("consts", o, [P, 8], F32)
    r1 = o
    e = put("gateT", r1, [P, 8 * IFF], BF16)
    e = put("upT", e, [P, 8 * IFF], BF16)
    e = put("qT", r1, [P, 4 * SEQ], BF16)
    e = put("kT", e, [P, 4 * SEQ], BF16)
    r1_end = put("vaug", e, [P, NT * 520], BF16)
    r2 = r1_end
    e = put("downT", r2, [P, 22 * D], BF16)
    e = put("wqkv_b", r2, [P, 8 * 3 * INNER], BF16)
    e = put("lnw_rep", e, [P, D], F32)
    e = put("lnb_rep", e, [P, D], F32)
    assert e <= r2 + 45056
    e = put("oT_sb", r2, [P, 4 * SEQ], BF16)
    e = put("wout_b", e, [P, 4 * D], BF16)
    assert e <= r2 + 45056
    r3 = r2 + 45056
    e = put("xf", r3, [P, 2 * D], F32)
    e = put("xo", e, [P, D], F32)
    e = put("xq", e, [P, D], BF16)
    e = put("xqT", e, [P, 8 * P], BF16)
    e = put("hq", e, [P, INNER], BF16)
    e = put("hqT", e, [P, INNER], BF16)
    e = put("sg_sb", e, [P, INNER], F32)
    e = put("hch", e, [P, 2 * INNER], F32)
    e = put("wstg", e, [P, 2 * 1408], F32)
    e = put("qs_m", e, [P, 64], F32)
    e = put("qs_s", e, [P, 64], F32)
    e = put("qs_r", e, [P, 64], F32)
    e = put("qs_a", e, [P, D], F32)
    e = put("qs_u", e, [P, D], F32)
    e = put("qs_g", e, [P, D], F32)
    e = put("wqb", e, [P, 1408], BF16)
    c1_end = e
    assert c1_end <= PART_SZ - 288, c1_end
    a0 = L["xo"][0]
    e = put("xn", a0, [P, D], F32)
    e = put("sqs2", e, [P, D], F32)
    e = put("xn2b", e, [P, D], BF16)
    e = put("xn2T", e, [P, 8 * P], BF16)
    e = put("qkb", e, [P, 2 * INNER], BF16)
    e = put("sqs", e, [P, INNER], F32)
    e = put("wkstg", e, [P, 2 * 768], F32)
    assert e <= c1_end
    b0 = r3
    e = put("pt_a0", b0, [P, INNER], BF16)
    e = put("pt_a1", e, [P, INNER], BF16)
    e = put("pt_b0", e, [P, INNER], BF16)
    e = put("pt_b1", e, [P, INNER], BF16)
    e = put("rsrow", e, [P, 2 * INNER], F32)
    e = put("rsln", e, [P, 2 * INNER], F32)
    e = put("rrep", e, [64, 2 * INNER], F32)
    e = put("oTs", e, [64, INNER], BF16)
    e = put("maskb", e, [P, 2 * INNER], BF16)
    assert e <= c1_end
    e = put("pbuf", b0, [P, 2 * D], F32)
    e = put("obuf", e, [P, 2 * D], F32)
    assert e <= c1_end
    return L


class MK2:
    def __init__(self):
        nc = bass.Bass("TRN2", target_bir_lowering=False,
                       detect_race_conditions=False)
        self.nc = nc
        self.x_h = nc.dram_tensor("x", [SEQ, D], F32, kind="ExternalInput").ap()
        self.lnw_h = nc.dram_tensor("ln_w", [D], F32, kind="ExternalInput").ap()
        self.lnb_h = nc.dram_tensor("ln_b", [D], F32, kind="ExternalInput").ap()
        self.wqkv_h = nc.dram_tensor("w_qkv", [D, 3 * INNER], F32, kind="ExternalInput").ap()
        self.wout_h = nc.dram_tensor("w_out", [INNER, D], F32, kind="ExternalInput").ap()
        self.wgate_h = nc.dram_tensor("w_gate", [IFF, D], F32, kind="ExternalInput").ap()
        self.wup_h = nc.dram_tensor("w_up", [IFF, D], F32, kind="ExternalInput").ap()
        self.wdown_h = nc.dram_tensor("w_down", [D, IFF], F32, kind="ExternalInput").ap()
        self.y_h = nc.dram_tensor("y", [SEQ, D], F32, kind="ExternalOutput").ap()
        self.part_h = nc.dram_tensor("part_scr", [SEQ, D], F32, kind="Internal").ap()
        self._handles = {}
        self._L = _layout()

    def sb(self, name):
        off, shape, dt = self._L[name]
        h = self.nc.alloc_sbuf_tensor_at(name, shape, dt, offset=off)
        self._handles[name] = h
        setattr(self, name, h.ap())
        return h.ap()

    def build(self):
        nc = self.nc
        self.sX = Ctr(nc, "c_sX", 16)
        self.sW = Ctr(nc, "c_sW", 16)
        self.sP = Ctr(nc, "c_sP", 16)
        self.sO = Ctr(nc, "c_sO", 16)
        self.sY = Ctr(nc, "c_sY", 16)
        self.T = Ctr(nc, "c_T")
        self.V = Ctr(nc, "c_V")
        self.S = Ctr(nc, "c_S")
        self.G = Ctr(nc, "c_G")

        with (nc.psum_tensor("pq", [P, INNER], F32) as pq,
              nc.psum_tensor("pk", [P, INNER], F32) as pk,
              nc.psum_tensor("pv", [P, INNER], F32) as pv,
              nc.psum_tensor("pr", [P, INNER], F32) as pr,
              nc.psum_tensor("poa", [P, INNER], F32) as poa,
              nc.psum_tensor("pob", [P, INNER], F32) as pob,
              nc.psum_tensor("ptA", [P, INNER], F32) as ptA,
              nc.psum_tensor("ptB", [P, INNER], F32) as ptB,
              nc.Block() as block):
            self.ps = dict(q=pq[:], k=pk[:], v=pv[:], r=pr[:],
                           oa=poa[:], ob=pob[:], tA=ptA[:], tB=ptB[:])

            @block.gpsimd
            def _(g):
                self.emit_setup()
                self.emit_prep()
                self.emit_C1()
                self.emit_A()
                self.emit_B()
                self.emit_C2()
                self.emit_tail()
        return nc

    def emit_setup(self):
        nc = self.nc
        v, s, gp, sy = nc.vector, nc.scalar, nc.gpsimd, nc.sync
        V, S, G = self.V, self.S, self.G
        for nm in ("ident", "ones65", "mu_all", "rstd_all", "smalls", "consts",
                   "gateT", "upT", "downT", "xf", "xo", "xq", "xqT", "hq", "hqT",
                   "sg_sb", "hch", "wstg", "qs_m", "qs_s", "qs_r", "qs_a", "qs_u",
                   "qs_g", "wqb"):
            self.sb(nm)
        self.qscr = dict(m=self.qs_m, sc2=self.qs_s, r6=self.qs_r,
                         a=self.qs_a, u=self.qs_u, sgn=self.qs_g)
        for i, val in enumerate([LN_EPS, L2_EPS, LOG8]):
            G.inc(gp.memset(self.consts[:, i:i + 1], val))
            nc.const_aps.register(F32, val, self.consts[:, i:i + 1])
        gp.memset(self.ident, 0.0)
        G.inc(gp.affine_select(out=self.ident, in_=self.ident, compare_op=AL.not_equal,
                               fill=1.0, base=0, pattern=[[-1, P]], channel_multiplier=1))
        V.inc(v.memset(self.ones65, 1.0))
        G.wait(s)
        G.wait(v)
        G.wait(nc.tensor)

    def emit_quant(self, x_ap, out_ap, F, gp_assist=True):
        nc = self.nc
        v, s, gp = nc.vector, nc.scalar, nc.gpsimd
        V, S, G = self.V, self.S, self.G
        scr = self.qscr
        nb = F // QB
        m, sc2, r6 = scr["m"], scr["sc2"], scr["r6"]
        a, u, sgn = scr["a"], scr["u"], scr["sgn"]
        x3 = x_ap.rearrange("p (b s) -> p b s", s=QB)
        V.wait(s)
        G.wait(s)
        s.activation(a[:, :F], x_ap, AF.Abs)
        S.inc(s.activation(sgn[:, :F], x_ap, AF.Sign))
        G.wait(v)
        V.inc(v.tensor_reduce(m[:, :nb], x3, AX.X, AL.max, apply_absolute_value=True))
        V.wait(v)
        v.tensor_scalar(sc2[:, :nb], m[:, :nb], 1.0 / 12.0, 5e-13, AL.mult, AL.max)
        V.inc(v.tensor_scalar(r6[:, :nb], m[:, :nb], 1.0 / 6.0, 1e-12, AL.mult, AL.max))
        V.wait(v)
        V.inc(v.reciprocal(r6[:, :nb], r6[:, :nb]))
        r6b = r6[:, :nb].broadcast_to([P, nb, QB])
        sc2b = sc2[:, :nb].broadcast_to([P, nb, QB])
        a3 = a[:, :F].rearrange("p (b s) -> p b s", s=QB)
        aa, uu = a[:, :F], u[:, :F]
        if gp_assist:
            S.wait(gp)
            V.wait(gp)
            G.inc(gp.tensor_tensor(a3, a3, r6b, AL.mult))
            G.wait(v)
        else:
            S.wait(v)
            V.wait(v)
            V.inc(v.tensor_tensor(a3, a3, r6b, AL.mult))
            V.wait(v)

        def st(inst):
            V.inc(inst)
            V.wait(v)

        st(v.tensor_scalar(uu, aa, 5.0, 2.0, AL.is_gt, AL.mult))
        st(v.scalar_tensor_tensor(uu, aa, 3.5, uu, AL.is_gt, AL.add))
        st(v.scalar_tensor_tensor(uu, aa, 2.5, uu, AL.is_gt, AL.add))
        st(v.tensor_scalar(uu, uu, 2.0, None, AL.mult))
        st(v.scalar_tensor_tensor(uu, aa, 1.75, uu, AL.is_gt, AL.add))
        st(v.scalar_tensor_tensor(uu, aa, 1.25, uu, AL.is_gt, AL.add))
        st(v.scalar_tensor_tensor(uu, aa, 0.75, uu, AL.is_gt, AL.add))
        st(v.scalar_tensor_tensor(uu, aa, 0.25, uu, AL.is_gt, AL.add))
        if gp_assist:
            V.wait(gp)
            G.inc(gp.tensor_tensor(uu, uu, sgn[:, :F], AL.mult))
            G.wait(gp)
            G.inc(gp.tensor_tensor(out_ap.rearrange("p (b s) -> p b s", s=QB),
                                   u[:, :F].rearrange("p (b s) -> p b s", s=QB),
                                   sc2b, AL.mult))
        else:
            st(v.tensor_tensor(uu, uu, sgn[:, :F], AL.mult))
            st(v.tensor_tensor(out_ap.rearrange("p (b s) -> p b s", s=QB),
                               u[:, :F].rearrange("p (b s) -> p b s", s=QB),
                               sc2b, AL.mult))
        return self.G.n if gp_assist else self.V.n

    def emit_prep(self):
        nc = self.nc
        v, s, te, gp, sy = nc.vector, nc.scalar, nc.tensor, nc.gpsimd, nc.sync
        V, S, T, G, sW, sX = self.V, self.S, self.T, self.G, self.sW, self.sX
        ps = self.ps
        trA = ps["tA"].bitcast(BF16)

        sm = self.smalls
        stat_x_done = [0] * NT

        def emit_stats(t):
            par = t % 2
            xt = self.xf[:, par * D:(par + 1) * D]
            if t >= 2:
                sy.wait_ge(V.sem, stat_x_done[t - 2])
                sy.wait_ge(S.sem, self._stat_s_done[t - 2])
            sX.inc(sy.dma_start(out=xt, in_=self.x_h[t * P:(t + 1) * P, :]))
            sX.wait(v)
            sX.wait(s)
            V.inc(v.tensor_reduce(sm[:, 0:1], xt, AX.X, AL.add))
            stat_x_done[t] = V.n
            S.inc(s.activation(self.qs_a, xt, AF.Square, accum_out=sm[:, 1:2]))
            self._stat_s_done[t] = S.n
            V.wait(v)
            V.inc(v.tensor_scalar(self.mu_all[:, t:t + 1], sm[:, 0:1], 1.0 / D,
                                  None, AL.mult))
            S.wait(v)
            V.inc(v.tensor_scalar(sm[:, 2:3], sm[:, 1:2], 1.0 / D, None, AL.mult))
            V.wait(v)
            V.inc(v.tensor_tensor(sm[:, 3:4], self.mu_all[:, t:t + 1],
                                  self.mu_all[:, t:t + 1], AL.mult))
            V.wait(v)
            V.inc(v.tensor_sub(sm[:, 2:3], sm[:, 2:3], sm[:, 3:4]))
            V.wait(s)
            S.inc(s.activation(sm[:, 3:4], sm[:, 2:3], AF.Ln, bias=LN_EPS))
            S.wait(s)
            S.inc(s.activation(self.rstd_all[:, t:t + 1], sm[:, 3:4], AF.Exp,
                               scale=-0.5))
            S.wait(v)

        self._stat_s_done = [0] * NT
        for t in range(NT):
            emit_stats(t)
        self._stats_v_final = self.V.n
        self._stats_s_final = self.S.n

        wdma_done = [0, 0]

        def prep_gateup(name, src_h):
            dstT = self._handles[name]
            for rc in range(22):
                par = rc % 2
                stg = self.wstg[:, par * 1408: par * 1408 + D]
                if len(self._wq_hist) >= 2:
                    hv, hs, hg = self._wq_hist[-2]
                    sy.wait_ge(V.sem, hv)
                    sy.wait_ge(S.sem, hs)
                    sy.wait_ge(G.sem, hg)
                sW.inc(sy.dma_start(out=stg, in_=src_h[rc * P:(rc + 1) * P, :]))
                sW.wait(v)
                sW.wait(s)
                sW.wait(gp)
                self.T.wait(gp, getattr(self, "_wq_tm", 0))
                gdone = self.emit_quant(stg, self.wqb[:, :D], D)
                self._wq_hist.append((self.V.n, self.S.n, self.G.n))
                G.wait(te, gdone)
                T.wait(te, self._wq_trfree)
                te.wait_ge(V.sem, self._wq_evac)
                for j in range(8):
                    inst = te.transpose(trA[:, j * P:(j + 1) * P],
                                        self.wqb[:, j * P:(j + 1) * P], self.ident)
                T.inc(inst)
                tm = self.T.n
                self._wq_tm = tm
                v.wait_ge(self.T.sem, tm)
                dst = bass.AP(dstT, rc * P, [[8 * IFF, P], [IFF, 8], [1, P]])
                V.inc(v.tensor_copy(dst, trA[:, 0:D].rearrange("p (c q) -> p c q", q=P)))
                self._wq_evac = self.V.n

        self._wq_hist = []
        self._wq_evac = 0
        self._wq_trfree = 0
        prep_gateup("gateT", self.wgate_h)
        prep_gateup("upT", self.wup_h)

        for dc in range(8):
            for hh in range(2):
                idx = dc * 2 + hh
                par = idx % 2
                stg = self.wstg[:, par * 1408:(par + 1) * 1408]
                if len(self._wq_hist) >= 2:
                    hv, hs, hg = self._wq_hist[-2]
                    sy.wait_ge(V.sem, hv)
                    sy.wait_ge(S.sem, hs)
                    sy.wait_ge(G.sem, hg)
                sW.inc(sy.dma_start(out=stg,
                                    in_=self.wdown_h[dc * P:(dc + 1) * P,
                                                     hh * 1408:(hh + 1) * 1408]))
                sW.wait(v)
                sW.wait(s)
                sW.wait(gp)
                self.T.wait(gp, getattr(self, "_wq_tm", 0))
                gdone = 0
                for c0, cw in ((0, 704), (704, 704)):
                    gdone = self.emit_quant(stg[:, c0:c0 + cw],
                                            self.wqb[:, c0:c0 + cw], cw)
                self._wq_hist.append((self.V.n, self.S.n, self.G.n))
                G.wait(te, gdone)
                te.wait_ge(V.sem, self._wq_evac)
                for grp in ((0, 8), (8, 11)):
                    g0, g1 = grp
                    for j in range(g0, g1):
                        inst = te.transpose(trA[:, (j - g0) * P:(j - g0 + 1) * P],
                                            self.wqb[:, j * P:(j + 1) * P], self.ident)
                    T.inc(inst)
                    tm = self.T.n
                    self._wq_tm = tm
                    v.wait_ge(self.T.sem, tm)
                    kb0 = hh * 11 + g0
                    dst = bass.AP(self._handles["downT"], dc * P + kb0 * D,
                                  [[22 * D, P], [D, g1 - g0], [1, P]])
                    V.inc(v.tensor_copy(dst, trA[:, 0:(g1 - g0) * P]
                                        .rearrange("p (c q) -> p c q", q=P)))
                    self._wq_evac = self.V.n
                    te.wait_ge(V.sem, self._wq_evac)

    def emit_C1(self):
        nc = self.nc
        v, s, te, gp, sy = nc.vector, nc.scalar, nc.tensor, nc.gpsimd, nc.sync
        V, S, T, G, sX, sP = self.V, self.S, self.T, self.G, self.sX, self.sP
        ps = self.ps
        trA = ps["tA"].bitcast(BF16)
        trB = ps["tB"].bitcast(BF16)
        NCH = 6
        x_users = {}
        part_done = {}
        down_groups = {}

        for t in range(NT):
            par = t % 2
            xt = self.xf[:, par * D:(par + 1) * D]
            if t >= 2:
                sy.wait_ge(V.sem, x_users.get(t - 2, 0))
            else:
                sy.wait_ge(V.sem, self._stats_v_final)
                sy.wait_ge(S.sem, self._stats_s_final)
            sX.inc(sy.dma_start(out=xt, in_=self.x_h[t * P:(t + 1) * P, :]))
            sX.wait(v)
            sX.wait(s)
            sX.wait(gp)
            V.inc(v.tensor_scalar(self.xo, xt, self.mu_all[:, t:t + 1],
                                  self.rstd_all[:, t:t + 1], AL.subtract, AL.mult))
            V.wait(v)
            V.wait(s)
            V.wait(gp)
            self.T.wait(gp, getattr(self, "_c1_tmx", 0))
            gq = self.emit_quant(self.xo, self.xq, D)
            G.wait(te, gq)
            te.wait_ge(self.V.sem, getattr(self, "_c1_xqT_evac", self._wq_evac))
            for j in range(8):
                inst = te.transpose(trA[:, j * P:(j + 1) * P],
                                    self.xq[:, j * P:(j + 1) * P], self.ident)
            T.inc(inst)
            tmx = self.T.n
            self._c1_tmx = tmx
            v.wait_ge(self.T.sem, tmx)
            V.inc(v.tensor_copy(self.xqT, trA))
            self._c1_xqT_evac = self.V.n
            te.wait_ge(self.V.sem, self._c1_xqT_evac)
            hm_done = {}
            silu_done = {}
            hq_g = {}
            hqT_evac = {}
            for ic in range(NCH):
                w = INNER if ic < 5 else (IFF - 5 * INNER)
                c0 = ic * INNER
                gb, ub = (("q", "k") if ic % 2 == 0 else ("v", "r"))
                if ic >= 2:
                    te.wait_ge(self.S.sem, silu_done[ic - 2])
                    te.wait_ge(self.V.sem, hm_done[ic - 2])
                for kc in range(8):
                    nc.tensor.matmul(ps[gb][:, 0:w],
                                     self.xqT[:, kc * P:(kc + 1) * P],
                                     self.gateT[:, kc * IFF + c0: kc * IFF + c0 + w],
                                     start=(kc == 0), stop=(kc == 7))
                for kc in range(8):
                    inst = nc.tensor.matmul(ps[ub][:, 0:w],
                                            self.xqT[:, kc * P:(kc + 1) * P],
                                            self.upT[:, kc * IFF + c0: kc * IFF + c0 + w],
                                            start=(kc == 0), stop=(kc == 7))
                T.inc(inst)
                tGU = self.T.n
                s.wait_ge(self.T.sem, tGU)
                S.inc(s.activation(self.sg_sb[:, 0:w], ps[gb][:, 0:w], AF.Silu))
                silu_done[ic] = self.S.n
                hpar = ic % 2
                hc = self.hch[:, hpar * INNER: hpar * INNER + w]
                v.wait_ge(self.T.sem, tGU)
                S.wait(v)
                V.inc(v.tensor_tensor(hc, ps[ub][:, 0:w], self.sg_sb[:, 0:w], AL.mult))
                hm_done[ic] = self.V.n
                self.T.wait(gp, getattr(self, "_c1_tmh", 0))
                gq2 = self.emit_quant(hc, self.hq[:, 0:w], w)
                hq_g[ic] = gq2
                ntr = (w + P - 1) // P
                G.wait(te, gq2)
                if ic >= 1:
                    te.wait_ge(self.V.sem, hqT_evac[ic - 1])
                for j in range(ntr):
                    inst = te.transpose(trB[:, j * P:(j + 1) * P],
                                        self.hq[:, j * P:(j + 1) * P], self.ident)
                T.inc(inst)
                tmh = self.T.n
                self._c1_tmh = tmh
                v.wait_ge(self.T.sem, tmh)
                V.inc(v.tensor_copy(self.hqT[:, 0:ntr * P], trB[:, 0:ntr * P]))
                hqT_evac[ic] = self.V.n
                te.wait_ge(self.V.sem, hqT_evac[ic])
                if ic == 0 and t > 0:
                    te.wait_ge(self.V.sem, part_done[t - 1])
                for nb, bank in enumerate(("oa", "ob")):
                    for kb in range(ntr):
                        kbg = ic * 4 + kb
                        inst = nc.tensor.matmul(
                            ps[bank], self.hqT[:, kb * P:(kb + 1) * P],
                            self.downT[:, kbg * D + nb * INNER:
                                       kbg * D + (nb + 1) * INNER],
                            start=(ic == 0 and kb == 0),
                            stop=(ic == NCH - 1 and kb == ntr - 1))
                T.inc(inst)
            tdown = self.T.n
            v.wait_ge(self.T.sem, tdown)
            v.wait_ge(self.sP.sem, self.sP.n)
            V.inc(v.tensor_tensor(self.xo[:, 0:INNER], ps["oa"], xt[:, 0:INNER], AL.add))
            V.inc(v.tensor_tensor(self.xo[:, INNER:D], ps["ob"], xt[:, INNER:D], AL.add))
            part_done[t] = self.V.n
            x_users[t] = self.V.n
            sy.wait_ge(self.V.sem, part_done[t])
            sP.inc(sy.dma_start(out=self.part_h[t * P:(t + 1) * P, :], in_=self.xo))

    def emit_A(self):
        nc = self.nc
        v, s, te, gp, sy = nc.vector, nc.scalar, nc.tensor, nc.gpsimd, nc.sync
        V, S, T, G, sX, sW = self.V, self.S, self.T, self.G, self.sX, self.sW
        ps = self.ps
        trA = ps["tA"].bitcast(BF16)
        trB = ps["tB"].bitcast(BF16)
        for nm in ("qT", "kT", "vaug", "wqkv_b", "lnw_rep", "lnb_rep", "xn",
                   "sqs2", "xn2b", "xn2T", "qkb", "sqs", "wkstg"):
            self.sb(nm)
        sm = self.smalls

        ones_ap = bass.AP(self._handles["vaug"], 64, [[520 * NT, P], [520, NT], [65, 8]])
        G.inc(gp.memset(ones_ap, 1.0))
        sX.inc(sy.dma_start(out=self.lnw_rep,
                            in_=self.lnw_h.rearrange("(o d) -> o d", o=1).broadcast_to([P, D])))
        sX.inc(sy.dma_start(out=self.lnb_rep,
                            in_=self.lnb_h.rearrange("(o d) -> o d", o=1).broadcast_to([P, D])))
        for kc in range(8):
            for hh in range(2):
                idx = kc * 2 + hh
                par = idx % 2
                stg = self.wkstg[:, par * 768:(par + 1) * 768]
                if idx >= 2:
                    sy.wait_ge(V.sem, self._wk_v[idx - 2])
                sW.inc(sy.dma_start(out=stg, in_=self.wqkv_h[kc * P:(kc + 1) * P,
                                                             hh * 768:(hh + 1) * 768]))
                sW.wait(v)
                self._wk_v = getattr(self, "_wk_v", [0] * 16)
                V.inc(v.tensor_copy(
                    self.wqkv_b[:, kc * 1536 + hh * 768: kc * 1536 + (hh + 1) * 768],
                    stg))
                self._wk_v[idx] = self.V.n

        x_users = {}
        s_users = {}
        evac_qkv = {}
        evac_tr = {}
        c1_v_final = self.V.n
        c1_s_final = self.S.n
        c1_g_final = self.G.n
        for t in range(NT):
            par = t % 2
            xt = self.xf[:, par * D:(par + 1) * D]
            if t >= 2:
                sy.wait_ge(V.sem, x_users.get(t - 2, 0))
                sy.wait_ge(S.sem, s_users.get(t - 2, 0))
            else:
                sy.wait_ge(V.sem, c1_v_final)
                sy.wait_ge(S.sem, c1_s_final)
                sy.wait_ge(G.sem, c1_g_final)
            sX.inc(sy.dma_start(out=xt, in_=self.x_h[t * P:(t + 1) * P, :]))
            sX.wait(v)
            sX.wait(s)
            V.inc(v.tensor_scalar(self.xn, xt, self.mu_all[:, t:t + 1],
                                  self.rstd_all[:, t:t + 1], AL.subtract, AL.mult))
            x_users[t] = self.V.n
            V.wait(v)
            V.wait(s)
            V.inc(v.tensor_reduce(sm[:, 8:9], self.xn, AX.X, AL.add))
            S.inc(s.activation(self.sqs2, self.xn, AF.Square, accum_out=sm[:, 9:10]))
            s_users[t] = self.S.n
            V.wait(v)
            V.inc(v.tensor_scalar(sm[:, 10:11], sm[:, 8:9], 1.0 / D, None, AL.mult))
            S.wait(v)
            V.inc(v.tensor_scalar(sm[:, 11:12], sm[:, 9:10], 1.0 / D, None, AL.mult))
            V.wait(v)
            V.inc(v.tensor_tensor(sm[:, 12:13], sm[:, 10:11], sm[:, 10:11], AL.mult))
            V.wait(v)
            V.inc(v.tensor_sub(sm[:, 11:12], sm[:, 11:12], sm[:, 12:13]))
            V.wait(s)
            S.inc(s.activation(sm[:, 12:13], sm[:, 11:12], AF.Ln, bias=LN_EPS))
            S.wait(s)
            S.inc(s.activation(sm[:, 13:14], sm[:, 12:13], AF.Exp, scale=-0.5))
            S.wait(v)
            V.inc(v.tensor_scalar(self.sqs2, self.xn, sm[:, 10:11], sm[:, 13:14],
                                  AL.subtract, AL.mult))
            V.wait(v)
            V.inc(v.tensor_tensor(self.sqs2, self.sqs2, self.lnw_rep, AL.mult))
            V.wait(v)
            V.inc(v.tensor_tensor(self.xn2b, self.sqs2, self.lnb_rep, AL.add))
            vm_xn2b = self.V.n
            te.wait_ge(self.V.sem, vm_xn2b)
            for j in range(8):
                inst = te.transpose(trA[:, j * P:(j + 1) * P],
                                    self.xn2b[:, j * P:(j + 1) * P], self.ident)
            T.inc(inst)
            tm1 = self.T.n
            v.wait_ge(self.T.sem, tm1)
            V.inc(v.tensor_copy(self.xn2T, trA))
            vm_xn2T = self.V.n
            banks = ("q", "k", "v") if par == 0 else ("oa", "ob", "r")
            te.wait_ge(self.V.sem, vm_xn2T)
            if t >= 2:
                te.wait_ge(self.V.sem, evac_qkv[t - 2])
            for bank_i, psn in enumerate(banks):
                for kc in range(8):
                    inst = nc.tensor.matmul(
                        ps[psn], self.xn2T[:, kc * P:(kc + 1) * P],
                        self.wqkv_b[:, kc * 1536 + bank_i * INNER:
                                    kc * 1536 + (bank_i + 1) * INNER],
                        start=(kc == 0), stop=(kc == 7))
            T.inc(inst)
            tqkv = self.T.n
            s.wait_ge(self.T.sem, tqkv)
            v.wait_ge(self.T.sem, tqkv)
            V.wait(s)
            S.inc(s.activation(self.sqs, ps[banks[0]], AF.Square))
            S.wait(v)
            V.inc(v.tensor_reduce(sm[:, 16:24],
                                  self.sqs.rearrange("p (h c) -> p h c", c=DH),
                                  AX.X, AL.add))
            V.wait(s)
            S.inc(s.activation(sm[:, 24:32], sm[:, 16:24], AF.Ln, bias=L2_EPS))
            S.wait(s)
            S.inc(s.activation(sm[:, 32:40], sm[:, 24:32], AF.Exp, scale=-0.5, bias=LOG8))
            S.wait(v)
            V.inc(v.tensor_tensor(self.qkb[:, 0:INNER].rearrange("p (h c) -> p h c", c=DH),
                                  ps[banks[0]].rearrange("p (h c) -> p h c", c=DH),
                                  sm[:, 32:40].broadcast_to([P, H, DH]), AL.mult))
            V.wait(s)
            S.inc(s.activation(self.sqs, ps[banks[1]], AF.Square))
            S.wait(v)
            V.inc(v.tensor_reduce(sm[:, 16:24],
                                  self.sqs.rearrange("p (h c) -> p h c", c=DH),
                                  AX.X, AL.add))
            V.wait(s)
            S.inc(s.activation(sm[:, 24:32], sm[:, 16:24], AF.Ln, bias=L2_EPS))
            S.wait(s)
            S.inc(s.activation(sm[:, 32:40], sm[:, 24:32], AF.Exp, scale=-0.5))
            s_users[t] = self.S.n
            S.wait(v)
            V.inc(v.tensor_tensor(self.qkb[:, INNER:2 * INNER]
                                  .rearrange("p (h c) -> p h c", c=DH),
                                  ps[banks[1]].rearrange("p (h c) -> p h c", c=DH),
                                  sm[:, 32:40].broadcast_to([P, H, DH]), AL.mult))
            vdst = bass.AP(self._handles["vaug"], t * 520,
                           [[520 * NT, P], [65, 8], [1, DH]])
            V.inc(v.tensor_copy(vdst, ps[banks[2]].rearrange("p (h c) -> p h c", c=DH)))
            evac_qkv[t] = self.V.n
            vm_qkb = self.V.n
            te.wait_ge(self.V.sem, vm_qkb)
            if t >= 1:
                te.wait_ge(self.V.sem, evac_tr[t - 1])
            for j in range(8):
                inst = te.transpose(trB[:, j * P:(j + 1) * P],
                                    self.qkb[:, j * P:(j + 1) * P], self.ident)
            T.inc(inst)
            tm2 = self.T.n
            v.wait_ge(self.T.sem, tm2)
            qdst = bass.AP(self._handles["qT"], t * P, [[4 * SEQ, P], [SEQ, 4], [1, P]])
            kdst = bass.AP(self._handles["kT"], t * P, [[4 * SEQ, P], [SEQ, 4], [1, P]])
            V.inc(v.tensor_copy(qdst, trB[:, 0:INNER].rearrange("p (c q) -> p c q", q=P)))
            V.inc(v.tensor_copy(kdst, trB[:, INNER:2 * INNER]
                                .rearrange("p (c q) -> p c q", q=P)))
            evac_tr[t] = self.V.n
            x_users[t] = self.V.n

    def emit_B(self):
        nc = self.nc
        v, s, te, gp, sy = nc.vector, nc.scalar, nc.tensor, nc.gpsimd, nc.sync
        V, S, T, G, sO = self.V, self.S, self.T, self.G, self.sO
        ps = self.ps
        for nm in ("oT_sb", "pt_a0", "pt_a1", "pt_b0", "pt_b1", "rsrow", "rsln",
                   "rrep", "oTs", "maskb"):
            self.sb(nm)
        self.T.wait(gp)
        self.V.wait(gp)
        self.S.wait(gp)
        gp.memset(self.maskb, 1.0)
        G.inc(gp.affine_select(out=self.maskb, in_=self.maskb, compare_op=AL.is_ge,
                               fill=0.0, base=-INNER, pattern=[[1, 2 * INNER]],
                               channel_multiplier=-1))
        G.wait(v)
        G.wait(te)
        V.wait(te)
        S.wait(te)
        vm_b0 = self.V.n
        sa_banks = [ps["q"], ps["k"]]
        sb_banks = [ps["v"], ps["r"]]
        pta = [self.pt_a0, self.pt_a1]
        ptb = [self.pt_b0, self.pt_b1]
        trR = [ps["tA"], ps["tB"]]

        vm_epi = vm_b0
        self._b_last_av = 0
        for hp in range(4):
            for ci in range(8):
                nj = 4 * ci + 4
                qs = self.qT[:, hp * SEQ + ci * INNER: hp * SEQ + (ci + 1) * INNER]
                tm_sp = {}
                sm_exp = {}
                vm_msk = {}
                tm_av = {}

                def emit_spair(j):
                    par = j % 2
                    ks = self.kT[:, hp * SEQ + j * P: hp * SEQ + (j + 1) * P]
                    if j >= 2:
                        te.wait_ge(self.S.sem, sm_exp[j - 2])
                    elif (hp, ci) != (0, 0):
                        te.wait_ge(self.S.sem, self._b_last_exp)
                    nc.tensor.matmul(sa_banks[par], ks[0:64, :], qs[0:64, :],
                                     start=True, stop=True, tile_position=(0, 0))
                    inst = nc.tensor.matmul(sb_banks[par], ks[64:128, :], qs[64:128, :],
                                            start=True, stop=True, tile_position=(64, 0))
                    T.inc(inst)
                    tm_sp[j] = self.T.n
                    s.wait_ge(self.T.sem, tm_sp[j])
                    if j >= 2:
                        s.wait_ge(self.T.sem, tm_av[j - 2])
                    elif self._b_last_av:
                        s.wait_ge(self.T.sem, self._b_last_av)
                    s.activation(pta[par], sa_banks[par], AF.Exp)
                    S.inc(s.activation(ptb[par], sb_banks[par], AF.Exp))
                    sm_exp[j] = self.S.n
                    if j >= 4 * ci:
                        r = j * P - ci * INNER
                        msl = self.maskb[:, INNER - r: 2 * INNER - r]
                        v.wait_ge(self.S.sem, sm_exp[j])
                        v.tensor_tensor(pta[par], pta[par], msl, AL.mult)
                        V.inc(v.tensor_tensor(ptb[par], ptb[par], msl, AL.mult))
                        vm_msk[j] = self.V.n

                def emit_av(j):
                    par = j % 2
                    if j in vm_msk:
                        te.wait_ge(self.V.sem, vm_msk[j])
                    else:
                        te.wait_ge(self.S.sem, sm_exp[j])
                    if j == 0:
                        te.wait_ge(self.V.sem, vm_epi)
                    va = self.vaug[:, j * 520 + (2 * hp) * 65:
                                   j * 520 + (2 * hp) * 65 + 65]
                    vb = self.vaug[:, j * 520 + (2 * hp + 1) * 65:
                                   j * 520 + (2 * hp + 1) * 65 + 65]
                    nc.tensor.matmul(ps["oa"][0:65, :], va, pta[par],
                                     start=(j == 0), stop=(j == nj - 1))
                    inst = nc.tensor.matmul(ps["ob"][0:65, :], vb, ptb[par],
                                            start=(j == 0), stop=(j == nj - 1))
                    T.inc(inst)
                    tm_av[j] = self.T.n

                emit_spair(0)
                for j in range(nj):
                    if j + 1 < nj:
                        emit_spair(j + 1)
                    emit_av(j)
                self._b_last_exp = sm_exp[nj - 1]
                self._b_last_av = tm_av[nj - 1]
                v.wait_ge(self.T.sem, tm_av[nj - 1])
                V.inc(v.tensor_copy(self.rsrow[64:65, 0:INNER], ps["oa"][64:65, :]))
                V.inc(v.tensor_copy(self.rsrow[64:65, INNER:], ps["ob"][64:65, :]))
                V.wait(s)
                S.inc(s.activation(self.rsln[64:65, :], self.rsrow[64:65, :], AF.Ln))
                S.wait(s)
                S.inc(s.activation(self.rsrow[64:65, :], self.rsln[64:65, :],
                                   AF.Exp, scale=-1.0))
                sm_rec = self.S.n
                te.wait_ge(self.S.sem, sm_rec)
                nc.tensor.matmul(trR[0][0:64, :], self.ones65[64:65, 0:64],
                                 self.rsrow[64:65, 0:INNER], start=True, stop=True)
                inst = nc.tensor.matmul(trR[1][0:64, :], self.ones65[64:65, 0:64],
                                        self.rsrow[64:65, INNER:], start=True, stop=True)
                T.inc(inst)
                tm_rep = self.T.n
                v.wait_ge(self.T.sem, tm_rep)
                V.inc(v.tensor_copy(self.rrep[:, 0:INNER], trR[0][0:64, :]))
                V.inc(v.tensor_copy(self.rrep[:, INNER:], trR[1][0:64, :]))
                V.wait(v)
                coff = hp * SEQ + ci * INNER
                V.inc(v.tensor_tensor(
                    self.oT_sb[0:64, coff:coff + INNER], ps["oa"][0:64, :],
                    self.rrep[:, 0:INNER], AL.mult))
                v.wait_ge(self.sO.sem, self.sO.n)
                V.inc(v.tensor_tensor(self.oTs[:, 0:INNER], ps["ob"][0:64, :],
                                      self.rrep[:, INNER:], AL.mult))
                vm_epi = self.V.n
                sy.wait_ge(self.V.sem, vm_epi)
                sO.inc(sy.dma_start(out=self.oT_sb[64:128, coff:coff + INNER],
                                    in_=self.oTs[:, 0:INNER]))

    def emit_C2(self):
        nc = self.nc
        v, s, te, gp, sy = nc.vector, nc.scalar, nc.tensor, nc.gpsimd, nc.sync
        V, S, T, sP, sY, sW = self.V, self.S, self.T, self.sP, self.sY, self.sW
        ps = self.ps
        for nm in ("pbuf", "obuf", "wout_b"):
            self.sb(nm)
        self.V.wait(sy)
        self.S.wait(sy)
        self.T.wait(sy)
        for kc in range(4):
            for hh in range(2):
                idx = kc * 2 + hh
                par = idx % 2
                stg = self.obuf[:, par * D: par * D + INNER]
                if idx >= 2:
                    sy.wait_ge(V.sem, self._wo_v[idx - 2])
                sW.inc(sy.dma_start(out=stg, in_=self.wout_h[kc * P:(kc + 1) * P,
                                                             hh * 512:(hh + 1) * 512]))
                sW.wait(v)
                self._wo_v = getattr(self, "_wo_v", [0] * 8)
                V.inc(v.tensor_copy(
                    self.wout_b[:, kc * D + hh * 512: kc * D + (hh + 1) * 512], stg))
                self._wo_v[idx] = self.V.n
        self.sO.wait(sy)
        self.sO.wait(te)
        V.wait(te)
        out_done = {}
        res_done = {}
        for t in range(NT):
            par = t % 2
            pt_ = self.pbuf[:, par * D:(par + 1) * D]
            ot_ = self.obuf[:, par * D:(par + 1) * D]
            if t >= 2:
                sy.wait_ge(V.sem, res_done[t - 2])
            sP.inc(sy.dma_start(out=pt_, in_=self.part_h[t * P:(t + 1) * P, :]))
            bpair = ("oa", "ob") if par == 0 else ("q", "k")
            if t >= 2:
                te.wait_ge(self.V.sem, res_done[t - 2])
            for nb, bank in enumerate(bpair):
                for cb in range(4):
                    inst = nc.tensor.matmul(
                        ps[bank], self.oT_sb[:, cb * SEQ + t * P: cb * SEQ + (t + 1) * P],
                        self.wout_b[:, cb * D + nb * INNER: cb * D + (nb + 1) * INNER],
                        start=(cb == 0), stop=(cb == 3))
            T.inc(inst)
            tm = self.T.n
            v.wait_ge(self.T.sem, tm)
            sP.wait(v)
            v.wait_ge(self.sY.sem, self.sY.n)
            V.inc(v.tensor_tensor(ot_[:, 0:INNER], ps[bpair[0]], pt_[:, 0:INNER], AL.add))
            V.inc(v.tensor_tensor(ot_[:, INNER:D], ps[bpair[1]], pt_[:, INNER:D], AL.add))
            res_done[t] = self.V.n
            sy.wait_ge(self.V.sem, res_done[t])
            sY.inc(sy.dma_start(out=self.y_h[t * P:(t + 1) * P, :], in_=ot_))
            out_done[t] = self.sY.n

    def emit_tail(self):
        gp = self.nc.gpsimd
        for c in (self.sY, self.sP, self.sO, self.sX, self.sW,
                  self.T, self.V, self.S, self.G):
            c.wait(gp)


def build():
    return MK2().build()


def kernel(L0_attn_x, L0_attn_ln_w, L0_attn_ln_b, L0_attn_w_qkv, L0_attn_w_out,
           L0_ffn_w_gate, L0_ffn_w_up, L0_ffn_w_down):
    from concourse.bass_utils import run_bass_kernel_spmd
    x = np.ascontiguousarray(np.asarray(L0_attn_x, dtype=np.float32))
    args = dict(ln_w=np.ascontiguousarray(np.asarray(L0_attn_ln_w, dtype=np.float32)),
                ln_b=np.ascontiguousarray(np.asarray(L0_attn_ln_b, dtype=np.float32)),
                w_qkv=np.ascontiguousarray(np.asarray(L0_attn_w_qkv, dtype=np.float32)),
                w_out=np.ascontiguousarray(np.asarray(L0_attn_w_out, dtype=np.float32)),
                w_gate=np.ascontiguousarray(np.asarray(L0_ffn_w_gate, dtype=np.float32)),
                w_up=np.ascontiguousarray(np.asarray(L0_ffn_w_up, dtype=np.float32)),
                w_down=np.ascontiguousarray(np.asarray(L0_ffn_w_down, dtype=np.float32)))
    nc = build()
    in_maps = [dict(x=np.ascontiguousarray(x[b]), **args) for b in range(8)]
    res = run_bass_kernel_spmd(nc, in_maps, core_ids=list(range(8)))
    return np.stack([np.asarray(res.results[b]["y"]) for b in range(8)], axis=0)


if __name__ == "__main__":
    nc = build()
    print("instructions:", sum(1 for _ in nc.all_instructions()))


# version 09
# speedup vs baseline: 1.3806x; 1.1969x over previous
import numpy as np

import concourse.bass as bass
import concourse.mybir as mybir

F32 = mybir.dt.float32
BF16 = mybir.dt.bfloat16
AL = mybir.AluOpType
AF = mybir.ActivationFunctionType
AX = mybir.AxisListType

P = 128
D = 1024
SEQ = 4096
NT = SEQ // P
H = 8
DH = 64
INNER = 512
IFF = 2816
QB = 16
LN_EPS = 1e-5
L2_EPS = 1e-12
LOG8 = float(np.log(np.float64(8.0)))
ARENA = 24576
PART_SZ = 229376


class Ctr:
    def __init__(self, nc, name, step=1):
        self._cm = nc.semaphore(name)
        self.sem = self._cm.__enter__()
        self.n = 0
        self.step = step

    def inc(self, inst, k=1):
        inst.then_inc(self.sem, k * self.step)
        self.n += k * self.step
        return self.n

    def wait(self, eng, n=None):
        v = self.n if n is None else n
        if v > 0:
            eng.wait_ge(self.sem, v)


def _layout():
    L = {}

    def put(name, off, shape, dtype):
        L[name] = (off, shape, dtype)
        nb = int(np.prod(shape[1:])) * (2 if dtype == BF16 else 4)
        return off + (nb + 31) // 32 * 32

    o = ARENA
    o = put("ident", o, [P, P], BF16)
    o = put("ones65", o, [P, 64], F32)
    o = put("mu_all", o, [P, NT], F32)
    o = put("rstd_all", o, [P, NT], F32)
    o = put("smalls", o, [P, 64], F32)
    o = put("consts", o, [P, 8], F32)
    r1 = o
    e = put("gateT", r1, [P, 8 * IFF], BF16)
    e = put("upT", e, [P, 8 * IFF], BF16)
    e = put("qT", r1, [P, 4 * SEQ], BF16)
    e = put("kT", e, [P, 4 * SEQ], BF16)
    r1_end = put("vaug", e, [P, NT * 520], BF16)
    r2 = r1_end
    e = put("downT", r2, [P, 22 * D], BF16)
    e = put("wqkv_b", r2, [P, 8 * 3 * INNER], BF16)
    e = put("lnw_rep", e, [P, D], F32)
    e = put("lnb_rep", e, [P, D], F32)
    assert e <= r2 + 45056
    e = put("oT_sb", r2, [P, 4 * SEQ], BF16)
    e = put("wout_b", e, [P, 4 * D], BF16)
    assert e <= r2 + 45056
    r3 = r2 + 45056
    e = put("xf", r3, [P, 2 * D], F32)
    e = put("xo", e, [P, D], F32)
    e = put("xq", e, [P, D], BF16)
    e = put("xqT", e, [P, 8 * P], BF16)
    e = put("hq", e, [P, 2 * INNER], BF16)
    e = put("hqT", e, [P, 2 * INNER], BF16)
    wr = e
    e = put("wstg", e, [P, 2 * 1408], F32)
    e2 = put("sg_sb", wr, [P, 2 * INNER], F32)
    e2 = put("hch", e2, [P, 2 * INNER], F32)
    assert e2 <= e
    e = put("qs_m", e, [P, 2 * 64], F32)
    e = put("qs_s", e, [P, 2 * 64], F32)
    e = put("qs_r", e, [P, 2 * 64], F32)
    e = put("qs_a", e, [P, 2 * D], F32)
    e = put("qs_u", e, [P, 2 * D], F32)
    e = put("qs_g", e, [P, 2 * D], BF16)
    e = put("wqb", e, [P, 2 * D], BF16)
    c1_end = e
    assert c1_end <= PART_SZ - 288, c1_end
    a0 = L["xo"][0]
    e = put("xn", a0, [P, D], F32)
    e = put("sqs2", e, [P, D], F32)
    e = put("xn2b", e, [P, D], BF16)
    e = put("xn2T", e, [P, 8 * P], BF16)
    e = put("qkb", e, [P, 2 * INNER], BF16)
    e = put("sqs", e, [P, INNER], F32)
    e = put("wkstg", e, [P, 2 * 768], F32)
    assert e <= c1_end
    b0 = r3
    e = put("pt_a0", b0, [P, INNER], BF16)
    e = put("pt_a1", e, [P, INNER], BF16)
    e = put("pt_b0", e, [P, INNER], BF16)
    e = put("pt_b1", e, [P, INNER], BF16)
    e = put("rsrow", e, [P, 2 * INNER], F32)
    e = put("rsln", e, [P, 2 * INNER], F32)
    e = put("rrep", e, [64, 2 * INNER], F32)
    e = put("oTs", e, [64, INNER], BF16)
    e = put("maskb", e, [P, 2 * INNER], BF16)
    assert e <= c1_end
    e = put("pbuf", b0, [P, 2 * D], F32)
    e = put("obuf", e, [P, 2 * D], F32)
    assert e <= c1_end
    return L


class MK2:
    def __init__(self):
        nc = bass.Bass("TRN2", target_bir_lowering=False,
                       detect_race_conditions=False)
        self.nc = nc
        self.x_h = nc.dram_tensor("x", [SEQ, D], F32, kind="ExternalInput").ap()
        self.lnw_h = nc.dram_tensor("ln_w", [D], F32, kind="ExternalInput").ap()
        self.lnb_h = nc.dram_tensor("ln_b", [D], F32, kind="ExternalInput").ap()
        self.wqkv_h = nc.dram_tensor("w_qkv", [D, 3 * INNER], F32, kind="ExternalInput").ap()
        self.wout_h = nc.dram_tensor("w_out", [INNER, D], F32, kind="ExternalInput").ap()
        self.wgate_h = nc.dram_tensor("w_gate", [IFF, D], F32, kind="ExternalInput").ap()
        self.wup_h = nc.dram_tensor("w_up", [IFF, D], F32, kind="ExternalInput").ap()
        self.wdown_h = nc.dram_tensor("w_down", [D, IFF], F32, kind="ExternalInput").ap()
        self.y_h = nc.dram_tensor("y", [SEQ, D], F32, kind="ExternalOutput").ap()
        self.part_h = nc.dram_tensor("part_scr", [SEQ, D], F32, kind="Internal").ap()
        self._handles = {}
        self._L = _layout()

    def sb(self, name):
        off, shape, dt = self._L[name]
        h = self.nc.alloc_sbuf_tensor_at(name, shape, dt, offset=off)
        self._handles[name] = h
        setattr(self, name, h.ap())
        return h.ap()

    def vfence(self):
        v = self.nc.vector
        self.V.wait(v)
        self.V.inc(v.memset(self.smalls[:, 62:63], 0.0))
        return self.V.n

    def sfence(self):
        s = self.nc.scalar
        self.S.wait(s)
        self.S.inc(s.activation(self.smalls[:, 63:64], self.smalls[:, 62:63],
                                AF.Abs))
        return self.S.n

    def dma_echo(self, ctr, n):
        gp = self.nc.gpsimd
        ctr.wait(gp, n)
        self.G.inc(gp.memset(self.smalls[:, 61:62], 0.0))
        return self.G.n

    def build(self):
        nc = self.nc
        self.sX = Ctr(nc, "c_sX", 16)
        self.sW = Ctr(nc, "c_sW", 16)
        self.sP = Ctr(nc, "c_sP", 16)
        self.sO = Ctr(nc, "c_sO", 16)
        self.sY = Ctr(nc, "c_sY", 16)
        self.T = Ctr(nc, "c_T")
        self.V = Ctr(nc, "c_V")
        self.S = Ctr(nc, "c_S")
        self.G = Ctr(nc, "c_G")

        with (nc.psum_tensor("pq", [P, INNER], F32) as pq,
              nc.psum_tensor("pk", [P, INNER], F32) as pk,
              nc.psum_tensor("pv", [P, INNER], F32) as pv,
              nc.psum_tensor("pr", [P, INNER], F32) as pr,
              nc.psum_tensor("poa", [P, INNER], F32) as poa,
              nc.psum_tensor("pob", [P, INNER], F32) as pob,
              nc.psum_tensor("ptA", [P, INNER], F32) as ptA,
              nc.psum_tensor("ptB", [P, INNER], F32) as ptB,
              nc.Block() as block):
            self.ps = dict(q=pq[:], k=pk[:], v=pv[:], r=pr[:],
                           oa=poa[:], ob=pob[:], tA=ptA[:], tB=ptB[:])

            @block.gpsimd
            def _(g):
                self.emit_setup()
                self.emit_prep()
                self.emit_C1()
                self.emit_A()
                self.emit_B()
                self.emit_C2()
                self.emit_tail()
        return nc

    def emit_setup(self):
        nc = self.nc
        v, s, gp, sy = nc.vector, nc.scalar, nc.gpsimd, nc.sync
        V, S, G = self.V, self.S, self.G
        for nm in ("ident", "ones65", "mu_all", "rstd_all", "smalls", "consts",
                   "gateT", "upT", "downT", "xf", "xo", "xq", "xqT", "hq", "hqT",
                   "sg_sb", "hch", "wstg", "qs_m", "qs_s", "qs_r", "qs_a", "qs_u",
                   "qs_g", "wqb"):
            self.sb(nm)
        self.qscr = dict(m=self.qs_m, sc2=self.qs_s, r6=self.qs_r,
                         a=self.qs_a, u=self.qs_u, sgn=self.qs_g)
        for i, val in enumerate([LN_EPS, L2_EPS, LOG8]):
            G.inc(gp.memset(self.consts[:, i:i + 1], val))
            nc.const_aps.register(F32, val, self.consts[:, i:i + 1])
        G.inc(gp.memset(self.ident, 0.0))
        G.wait(gp)
        G.inc(gp.affine_select(out=self.ident, in_=self.ident, compare_op=AL.not_equal,
                               fill=1.0, base=0, pattern=[[-1, P]], channel_multiplier=1))
        V.inc(v.memset(self.ones65, 1.0))
        G.wait(s)
        G.wait(v)
        G.wait(nc.tensor)

    def emit_qhead(self, x_ap, F):
        nc = self.nc
        v, s, gp = nc.vector, nc.scalar, nc.gpsimd
        V, S, G = self.V, self.S, self.G
        pp = self._qn & 1
        self._qn += 1
        r = self._qrec[pp]
        nb = F // QB
        o64, oD = pp * 64, pp * D
        m = self.qs_m[:, o64:o64 + nb]
        sc2 = self.qs_s[:, o64:o64 + nb]
        r6 = self.qs_r[:, o64:o64 + nb]
        a = self.qs_a[:, oD:oD + F]
        x3 = x_ap.rearrange("p (b s) -> p b s", s=QB)
        a3 = a.rearrange("p (b s) -> p b s", s=QB)
        V.wait(s, r["v"])
        G.wait(s, r["g"])
        s.activation(a, x_ap, AF.Abs)
        S.inc(s.activation(self.qs_g[:, oD:oD + F], x_ap, AF.Sign))
        s_hi = S.n
        G.wait(v, r["g"])
        V.wait(v)
        V.inc(v.tensor_reduce(m, x3, AX.X, AL.max, apply_absolute_value=True))
        V.wait(v)
        v.tensor_scalar(sc2, m, 1.0 / 12.0, 5e-13, AL.mult, AL.max)
        V.inc(v.tensor_scalar(r6, m, 1.0 / 6.0, 1e-12, AL.mult, AL.max))
        V.wait(v)
        V.inc(v.reciprocal(r6, r6))
        v_r6 = V.n
        S.wait(gp, s_hi)
        V.wait(gp, v_r6)
        G.inc(gp.tensor_tensor(a3, a3, r6.broadcast_to([P, nb, QB]), AL.mult))
        return dict(pp=pp, F=F, s_hi=s_hi, g_am=self.G.n)

    def emit_qtail(self, tok, out_ap):
        nc = self.nc
        v, gp = nc.vector, nc.gpsimd
        V, G = self.V, self.G
        pp, F = tok["pp"], tok["F"]
        nb = F // QB
        o64, oD = pp * 64, pp * D
        sc2 = self.qs_s[:, o64:o64 + nb]
        a = self.qs_a[:, oD:oD + F]
        u = self.qs_u[:, oD:oD + F]
        sgn = self.qs_g[:, oD:oD + F]
        G.wait(v, tok["g_am"])

        def st(inst):
            V.inc(inst)
            V.wait(v)

        st(v.tensor_scalar(u, a, 5.0, 2.0, AL.is_gt, AL.mult))
        st(v.scalar_tensor_tensor(u, a, 3.5, u, AL.is_gt, AL.add))
        st(v.scalar_tensor_tensor(u, a, 2.5, u, AL.is_gt, AL.add))
        st(v.tensor_scalar(u, u, 2.0, None, AL.mult))
        st(v.scalar_tensor_tensor(u, a, 1.75, u, AL.is_gt, AL.add))
        st(v.scalar_tensor_tensor(u, a, 1.25, u, AL.is_gt, AL.add))
        st(v.scalar_tensor_tensor(u, a, 0.75, u, AL.is_gt, AL.add))
        st(v.scalar_tensor_tensor(u, a, 0.25, u, AL.is_gt, AL.add))
        v_st = V.n
        V.wait(gp, v_st)
        G.inc(gp.tensor_tensor(u, u, sgn, AL.mult))
        G.wait(gp)
        G.inc(gp.tensor_tensor(out_ap.rearrange("p (b s) -> p b s", s=QB),
                               u.rearrange("p (b s) -> p b s", s=QB),
                               sc2.broadcast_to([P, nb, QB]), AL.mult))
        self._qrec[pp] = dict(v=v_st, s=tok["s_hi"], g=self.G.n)
        return self.G.n

    def emit_prep(self):
        nc = self.nc
        v, s, te, gp, sy = nc.vector, nc.scalar, nc.tensor, nc.gpsimd, nc.sync
        V, S, T, G, sW, sX = self.V, self.S, self.T, self.G, self.sW, self.sX
        ps = self.ps
        trA = ps["tA"].bitcast(BF16)
        self._qn = 0
        self._qrec = {0: dict(v=0, s=0, g=0), 1: dict(v=0, s=0, g=0)}
        self._tr_hist = []
        self._wq_evac = 0
        jobs = []
        for name, src_h in (("gateT", self.wgate_h), ("upT", self.wup_h)):
            for rc in range(22):
                dst = bass.AP(self._handles[name], rc * P,
                              [[8 * IFF, P], [IFF, 8], [1, P]])
                jobs.append((src_h[rc * P:(rc + 1) * P, :], D, [(0, D, dst)]))
        for dc in range(8):
            for hh in range(2):
                chains = []
                for c0, w in ((0, 768), (768, 640)):
                    kb0 = hh * 11 + c0 // P
                    dst = bass.AP(self._handles["downT"], dc * P + kb0 * D,
                                  [[22 * D, P], [D, w // P], [1, P]])
                    chains.append((c0, w, dst))
                jobs.append((self.wdown_h[dc * P:(dc + 1) * P,
                                          hh * 1408:(hh + 1) * 1408],
                             1408, chains))
        jrec = {}
        swm = {}

        def issue_dma(j):
            src_ap, width, _ = jobs[j]
            if j >= 2:
                sy.wait_ge(V.sem, jrec[j - 2][0])
                sy.wait_ge(S.sem, jrec[j - 2][1])
            sW.inc(sy.dma_start(out=self.wstg[:, (j % 2) * 1408:
                                              (j % 2) * 1408 + width],
                                in_=src_ap))
            swm[j] = sW.n

        pend = None

        def flush_pend():
            nonlocal pend
            if pend is None:
                return
            tok, pp, w, dst = pend
            pend = None
            tpm = self._tr_hist[-2] if len(self._tr_hist) >= 2 else 0
            self.T.wait(gp, tpm)
            gd = self.emit_qtail(tok, self.wqb[:, pp * D:pp * D + w])
            ntr = w // P
            G.wait(te, gd)
            te.wait_ge(V.sem, self._wq_evac)
            for i in range(ntr):
                inst = te.transpose(trA[:, i * P:(i + 1) * P],
                                    self.wqb[:, pp * D + i * P:
                                             pp * D + (i + 1) * P],
                                    self.ident)
            T.inc(inst)
            self._tr_hist.append(self.T.n)
            v.wait_ge(self.T.sem, self.T.n)
            V.inc(v.tensor_copy(dst, trA[:, 0:ntr * P]
                                .rearrange("p (c q) -> p c q", q=P)))
            self._wq_evac = self.V.n

        issue_dma(0)
        echo0 = self.dma_echo(sW, swm[0])
        for j, (src_ap, width, chains) in enumerate(jobs):
            if j + 1 < len(jobs):
                issue_dma(j + 1)
            stg = self.wstg[:, (j % 2) * 1408:(j % 2) * 1408 + width]
            for c0, w, dst in chains:
                pp = self._qn & 1
                sW.wait(v, swm[j])
                sW.wait(s, swm[j])
                if j == 0:
                    G.wait(v, echo0)
                    G.wait(s, echo0)
                tok = self.emit_qhead(stg[:, c0:c0 + w], w)
                flush_pend()
                pend = (tok, pp, w, dst)
            jrec[j] = (self.V.n, self.S.n)
        flush_pend()

    def emit_C1(self):
        nc = self.nc
        v, s, te, gp, sy = nc.vector, nc.scalar, nc.tensor, nc.gpsimd, nc.sync
        V, S, T, G, sX, sP = self.V, self.S, self.T, self.G, self.sX, self.sP
        ps = self.ps
        trA = ps["tA"].bitcast(BF16)
        trB = ps["tB"].bitcast(BF16)
        NCH = 6
        sm = self.smalls
        x_users = {}
        part_done = {}
        p_dma = {}
        sxm = {}
        silu_hist = []
        hm_hist = []
        trb_hist = []
        st_rec = {0: dict(v=0, s=0), 1: dict(v=0, s=0)}
        xq_s_hi = 0
        tmx_prev = 0
        xqT_evac = self._wq_evac
        hqT_evac = 0
        pend = None

        def issue_x_dma(t):
            if t >= NT:
                return
            if t >= 2:
                sy.wait_ge(V.sem, x_users[t - 2])
            sX.inc(sy.dma_start(out=self.xf[:, (t % 2) * D:(t % 2 + 1) * D],
                                in_=self.x_h[t * P:(t + 1) * P, :]))
            sxm[t] = sX.n

        def emit_stats_xn(t):
            nonlocal xq_s_hi
            par = t % 2
            so = par * 8
            xt = self.xf[:, par * D:(par + 1) * D]
            r = st_rec[par]
            sX.wait(v, sxm[t])
            sX.wait(s, sxm[t])
            S.wait(v, r["s"])
            V.wait(s, r["v"])
            V.inc(v.tensor_reduce(sm[:, so:so + 1], xt, AX.X, AL.add))
            V.wait(v)
            S.inc(s.activation(self.xo, xt, AF.Square,
                               accum_out=sm[:, so + 1:so + 2]))
            s_sq = S.n
            V.inc(v.tensor_scalar(self.mu_all[:, t:t + 1], sm[:, so:so + 1],
                                  1.0 / D, None, AL.mult))
            v.wait_ge(self.S.sem, s_sq)
            V.inc(v.tensor_scalar(sm[:, so + 2:so + 3], sm[:, so + 1:so + 2],
                                  1.0 / D, None, AL.mult))
            V.wait(v)
            V.inc(v.tensor_tensor(sm[:, so + 3:so + 4], self.mu_all[:, t:t + 1],
                                  self.mu_all[:, t:t + 1], AL.mult))
            V.wait(v)
            V.inc(v.tensor_sub(sm[:, so + 2:so + 3], sm[:, so + 2:so + 3],
                               sm[:, so + 3:so + 4]))
            v_var = self.V.n
            s.wait_ge(self.V.sem, v_var)
            S.inc(s.activation(sm[:, so + 3:so + 4], sm[:, so + 2:so + 3],
                               AF.Ln, bias=LN_EPS))
            S.wait(s)
            S.inc(s.activation(self.rstd_all[:, t:t + 1], sm[:, so + 3:so + 4],
                               AF.Exp, scale=-0.5))
            s_exp = S.n
            st_rec[par] = dict(v=v_var, s=s_exp)
            v.wait_ge(self.S.sem, s_exp)
            V.inc(v.tensor_scalar(self.xo, xt, self.mu_all[:, t:t + 1],
                                  self.rstd_all[:, t:t + 1],
                                  AL.subtract, AL.mult))
            return self.V.n

        def emit_trax(gq, t):
            nonlocal tmx_prev, xqT_evac
            G.wait(te, gq)
            te.wait_ge(self.V.sem, xqT_evac)
            for j in range(8):
                inst = te.transpose(trA[:, j * P:(j + 1) * P],
                                    self.xq[:, j * P:(j + 1) * P], self.ident)
            T.inc(inst)
            tmx_prev = self.T.n
            v.wait_ge(self.T.sem, tmx_prev)
            V.inc(v.tensor_copy(self.xqT, trA))
            xqT_evac = self.vfence()
            te.wait_ge(self.V.sem, xqT_evac)

        tgu_d = {}

        def emit_gu(ic, nlag):
            w = INNER if ic < 5 else (IFF - 5 * INNER)
            c0 = ic * INNER
            gb, ub = (("q", "k") if ic % 2 == 0 else ("v", "r"))
            if len(silu_hist) >= nlag:
                te.wait_ge(self.S.sem, silu_hist[-nlag])
                te.wait_ge(self.V.sem, hm_hist[-nlag])
            for kc in range(8):
                nc.tensor.matmul(ps[gb][:, 0:w],
                                 self.xqT[:, kc * P:(kc + 1) * P],
                                 self.gateT[:, kc * IFF + c0:
                                            kc * IFF + c0 + w],
                                 start=(kc == 0), stop=(kc == 7))
            for kc in range(8):
                inst = nc.tensor.matmul(ps[ub][:, 0:w],
                                        self.xqT[:, kc * P:(kc + 1) * P],
                                        self.upT[:, kc * IFF + c0:
                                                 kc * IFF + c0 + w],
                                        start=(kc == 0), stop=(kc == 7))
            T.inc(inst)
            tgu_d[ic] = self.T.n

        issue_x_dma(0)
        issue_x_dma(1)
        v_xn = emit_stats_xn(0)
        V.wait(s, v_xn)
        tokXQ = self.emit_qhead(self.xo, D)
        xq_s_hi = tokXQ["s_hi"]
        self.T.wait(gp, tmx_prev)
        gq = self.emit_qtail(tokXQ, self.xq)
        emit_trax(gq, 0)
        emit_gu(0, nlag=2)

        for t in range(NT):
            def flush_pending(t=t):
                nonlocal pend, hqT_evac
                if pend is None:
                    return
                tok2, icp, w2, ic2 = pend
                pend = None
                tbm = trb_hist[-2] if len(trb_hist) >= 2 else 0
                self.T.wait(gp, tbm)
                gq2 = self.emit_qtail(
                    tok2, self.hq[:, icp * INNER:icp * INNER + w2])
                ntr = (w2 + P - 1) // P
                G.wait(te, gq2)
                te.wait_ge(self.V.sem, hqT_evac)
                for j2 in range(ntr):
                    inst2 = te.transpose(trB[:, j2 * P:(j2 + 1) * P],
                                         self.hq[:, icp * INNER + j2 * P:
                                                 icp * INNER + (j2 + 1) * P],
                                         self.ident)
                T.inc(inst2)
                trb_hist.append(self.T.n)
                v.wait_ge(self.T.sem, self.T.n)
                V.inc(v.tensor_copy(self.hqT[:, icp * INNER:
                                             icp * INNER + ntr * P],
                                    trB[:, 0:ntr * P]))
                hqT_evac = self.vfence()
                te.wait_ge(self.V.sem, hqT_evac)
                if ic2 == 0 and t > 0:
                    te.wait_ge(self.V.sem, part_done[t - 1])
                for nb_, bank in enumerate(("oa", "ob")):
                    for kb in range(ntr):
                        kbg = ic2 * 4 + kb
                        inst2 = nc.tensor.matmul(
                            ps[bank],
                            self.hqT[:, icp * INNER + kb * P:
                                     icp * INNER + (kb + 1) * P],
                            self.downT[:, kbg * D + nb_ * INNER:
                                       kbg * D + (nb_ + 1) * INNER],
                            start=(ic2 == 0 and kb == 0),
                            stop=(ic2 == NCH - 1 and kb == ntr - 1))
                T.inc(inst2)

            par = t % 2
            xt = self.xf[:, par * D:(par + 1) * D]
            for ic in range(NCH):
                w = INNER if ic < 5 else (IFF - 5 * INNER)
                c0 = ic * INNER
                icpar = ic % 2
                gb, ub = (("q", "k") if icpar == 0 else ("v", "r"))
                if ic + 1 < NCH:
                    emit_gu(ic + 1, nlag=1)
                tgu = tgu_d[ic]
                s.wait_ge(self.T.sem, tgu)
                if len(hm_hist) >= 2:
                    s.wait_ge(self.V.sem, hm_hist[-2])
                sg = self.sg_sb[:, icpar * INNER: icpar * INNER + w]
                S.inc(s.activation(sg, ps[gb][:, 0:w], AF.Silu))
                silu_hist.append(self.S.n)
                hc = self.hch[:, icpar * INNER: icpar * INNER + w]
                v.wait_ge(self.T.sem, tgu)
                v.wait_ge(self.S.sem, silu_hist[-1])
                if ic == 0 and t >= 1:
                    sP.wait(v, p_dma[t - 1])
                V.inc(v.tensor_tensor(hc, ps[ub][:, 0:w], sg, AL.mult))
                hm_hist.append(self.V.n)
                V.wait(s, hm_hist[-1])
                tok = self.emit_qhead(hc, w)
                flush_pending()
                pend = (tok, icpar, w, ic)
            if t + 1 < NT:
                v_xn = emit_stats_xn(t + 1)
                V.wait(s, v_xn)
                tokXQ = self.emit_qhead(self.xo, D)
                xq_s_hi_t = tokXQ["s_hi"]
            s_hi5 = pend[0]["s_hi"]
            flush_pending()
            tdown = self.T.n
            v.wait_ge(self.T.sem, tdown)
            v.wait_ge(self.S.sem, s_hi5)
            V.inc(v.tensor_tensor(self.hch[:, 0:INNER], ps["oa"],
                                  xt[:, 0:INNER], AL.add))
            V.inc(v.tensor_tensor(self.hch[:, INNER:D], ps["ob"],
                                  xt[:, INNER:D], AL.add))
            part_done[t] = self.V.n
            x_users[t] = self.V.n
            sy.wait_ge(self.V.sem, part_done[t])
            sP.inc(sy.dma_start(out=self.part_h[t * P:(t + 1) * P, :],
                                in_=self.hch[:, 0:D]))
            p_dma[t] = self.sP.n
            issue_x_dma(t + 2)
            if t + 1 < NT:
                xq_s_hi = xq_s_hi_t
                self.T.wait(gp, tmx_prev)
                gq = self.emit_qtail(tokXQ, self.xq)
                emit_trax(gq, t + 1)
                emit_gu(0, nlag=2)

    def emit_A(self):
        nc = self.nc
        v, s, te, gp, sy = nc.vector, nc.scalar, nc.tensor, nc.gpsimd, nc.sync
        V, S, T, G, sX, sW = self.V, self.S, self.T, self.G, self.sX, self.sW
        ps = self.ps
        trA = ps["tA"].bitcast(BF16)
        trB = ps["tB"].bitcast(BF16)
        for nm in ("qT", "kT", "vaug", "wqkv_b", "lnw_rep", "lnb_rep", "xn",
                   "sqs2", "xn2b", "xn2T", "qkb", "sqs", "wkstg"):
            self.sb(nm)
        sm = self.smalls

        ones_ap = bass.AP(self._handles["vaug"], 64, [[520 * NT, P], [520, NT], [65, 8]])
        G.inc(gp.memset(ones_ap, 1.0))
        sX.inc(sy.dma_start(out=self.lnw_rep,
                            in_=self.lnw_h.rearrange("(o d) -> o d", o=1).broadcast_to([P, D])))
        sX.inc(sy.dma_start(out=self.lnb_rep,
                            in_=self.lnb_h.rearrange("(o d) -> o d", o=1).broadcast_to([P, D])))
        for kc in range(8):
            for hh in range(2):
                idx = kc * 2 + hh
                par = idx % 2
                stg = self.wkstg[:, par * 768:(par + 1) * 768]
                if idx >= 2:
                    sy.wait_ge(V.sem, self._wk_v[idx - 2])
                sW.inc(sy.dma_start(out=stg, in_=self.wqkv_h[kc * P:(kc + 1) * P,
                                                             hh * 768:(hh + 1) * 768]))
                sW.wait(v)
                if idx == 0:
                    G.wait(v, self.dma_echo(sW, self.sW.n))
                self._wk_v = getattr(self, "_wk_v", [0] * 16)
                V.inc(v.tensor_copy(
                    self.wqkv_b[:, kc * 1536 + hh * 768: kc * 1536 + (hh + 1) * 768],
                    stg))
                self._wk_v[idx] = self.V.n
        for idx in (16, 17):
            kc, hh = 0, idx - 16
            par = idx % 2
            stg = self.wkstg[:, par * 768:(par + 1) * 768]
            sy.wait_ge(V.sem, self._wk_v[idx - 2])
            sW.inc(sy.dma_start(out=stg, in_=self.wqkv_h[kc * P:(kc + 1) * P,
                                                         hh * 768:(hh + 1) * 768]))
            sW.wait(v)
            V.inc(v.tensor_copy(
                self.wqkv_b[:, kc * 1536 + hh * 768: kc * 1536 + (hh + 1) * 768],
                stg))
        x_users = {}
        s_users = {}
        evac_qkv = {}
        evac_tr = {}
        c1_v_final = self.V.n
        c1_s_final = self.S.n
        c1_g_final = self.G.n
        for t in range(NT):
            par = t % 2
            xt = self.xf[:, par * D:(par + 1) * D]
            if t >= 2:
                sy.wait_ge(V.sem, x_users.get(t - 2, 0))
                sy.wait_ge(S.sem, s_users.get(t - 2, 0))
            else:
                sy.wait_ge(V.sem, c1_v_final)
                sy.wait_ge(S.sem, c1_s_final)
                sy.wait_ge(G.sem, c1_g_final)
            sX.inc(sy.dma_start(out=xt, in_=self.x_h[t * P:(t + 1) * P, :]))
            sX.wait(v)
            sX.wait(s)
            V.inc(v.tensor_scalar(self.xn, xt, self.mu_all[:, t:t + 1],
                                  self.rstd_all[:, t:t + 1], AL.subtract, AL.mult))
            x_users[t] = self.V.n
            V.wait(v)
            V.wait(s)
            V.inc(v.tensor_reduce(sm[:, 8:9], self.xn, AX.X, AL.add))
            S.inc(s.activation(self.sqs2, self.xn, AF.Square, accum_out=sm[:, 9:10]))
            s_users[t] = self.sfence()
            V.wait(v)
            V.inc(v.tensor_scalar(sm[:, 10:11], sm[:, 8:9], 1.0 / D, None, AL.mult))
            S.wait(v)
            V.inc(v.tensor_scalar(sm[:, 11:12], sm[:, 9:10], 1.0 / D, None, AL.mult))
            V.wait(v)
            V.inc(v.tensor_tensor(sm[:, 12:13], sm[:, 10:11], sm[:, 10:11], AL.mult))
            V.wait(v)
            V.inc(v.tensor_sub(sm[:, 11:12], sm[:, 11:12], sm[:, 12:13]))
            self.vfence()
            V.wait(s)
            S.inc(s.activation(sm[:, 12:13], sm[:, 11:12], AF.Ln, bias=LN_EPS))
            S.wait(s)
            S.inc(s.activation(sm[:, 13:14], sm[:, 12:13], AF.Exp, scale=-0.5))
            self.sfence()
            S.wait(v)
            V.inc(v.tensor_scalar(self.sqs2, self.xn, sm[:, 10:11], sm[:, 13:14],
                                  AL.subtract, AL.mult))
            V.wait(v)
            V.inc(v.tensor_tensor(self.sqs2, self.sqs2, self.lnw_rep, AL.mult))
            V.wait(v)
            V.inc(v.tensor_tensor(self.xn2b, self.sqs2, self.lnb_rep, AL.add))
            vm_xn2b = self.vfence()
            te.wait_ge(self.V.sem, vm_xn2b)
            for j in range(8):
                inst = te.transpose(trA[:, j * P:(j + 1) * P],
                                    self.xn2b[:, j * P:(j + 1) * P], self.ident)
            T.inc(inst)
            tm1 = self.T.n
            v.wait_ge(self.T.sem, tm1)
            V.inc(v.tensor_copy(self.xn2T, trA))
            vm_xn2T = self.vfence()
            banks = ("q", "k", "v") if par == 0 else ("oa", "ob", "r")
            te.wait_ge(self.V.sem, vm_xn2T)
            if t >= 2:
                te.wait_ge(self.V.sem, evac_qkv[t - 2])
            for bank_i, psn in enumerate(banks):
                for kc in range(8):
                    inst = nc.tensor.matmul(
                        ps[psn], self.xn2T[:, kc * P:(kc + 1) * P],
                        self.wqkv_b[:, kc * 1536 + bank_i * INNER:
                                    kc * 1536 + (bank_i + 1) * INNER],
                        start=(kc == 0), stop=(kc == 7))
            T.inc(inst)
            tqkv = self.T.n
            s.wait_ge(self.T.sem, tqkv)
            v.wait_ge(self.T.sem, tqkv)
            V.inc(v.tensor_copy(self.sqs, ps[banks[0]]))
            V.wait(v)
            V.inc(v.tensor_tensor(self.sqs, self.sqs, self.sqs, AL.mult))
            V.wait(v)
            V.inc(v.tensor_reduce(sm[:, 16:24],
                                  self.sqs.rearrange("p (h c) -> p h c", c=DH),
                                  AX.X, AL.add))
            self.vfence()
            V.wait(s)
            S.inc(s.activation(sm[:, 24:32], sm[:, 16:24], AF.Ln, bias=L2_EPS))
            S.wait(s)
            S.inc(s.activation(sm[:, 32:40], sm[:, 24:32], AF.Exp, scale=-0.5, bias=LOG8))
            self.sfence()
            S.wait(v)
            V.inc(v.tensor_tensor(self.qkb[:, 0:INNER].rearrange("p (h c) -> p h c", c=DH),
                                  ps[banks[0]].rearrange("p (h c) -> p h c", c=DH),
                                  sm[:, 32:40].broadcast_to([P, H, DH]), AL.mult))
            V.inc(v.tensor_copy(self.sqs, ps[banks[1]]))
            V.wait(v)
            V.inc(v.tensor_tensor(self.sqs, self.sqs, self.sqs, AL.mult))
            V.wait(v)
            V.inc(v.tensor_reduce(sm[:, 16:24],
                                  self.sqs.rearrange("p (h c) -> p h c", c=DH),
                                  AX.X, AL.add))
            self.vfence()
            V.wait(s)
            S.inc(s.activation(sm[:, 24:32], sm[:, 16:24], AF.Ln, bias=L2_EPS))
            S.wait(s)
            S.inc(s.activation(sm[:, 32:40], sm[:, 24:32], AF.Exp, scale=-0.5))
            s_users[t] = self.sfence()
            S.wait(v)
            V.inc(v.tensor_tensor(self.qkb[:, INNER:2 * INNER]
                                  .rearrange("p (h c) -> p h c", c=DH),
                                  ps[banks[1]].rearrange("p (h c) -> p h c", c=DH),
                                  sm[:, 32:40].broadcast_to([P, H, DH]), AL.mult))
            vdst = bass.AP(self._handles["vaug"], t * 520,
                           [[520 * NT, P], [65, 8], [1, DH]])
            V.inc(v.tensor_copy(vdst, ps[banks[2]].rearrange("p (h c) -> p h c", c=DH)))
            evac_qkv[t] = self.V.n
            vm_qkb = self.vfence()
            te.wait_ge(self.V.sem, vm_qkb)
            if t >= 1:
                te.wait_ge(self.V.sem, evac_tr[t - 1])
            for j in range(8):
                inst = te.transpose(trB[:, j * P:(j + 1) * P],
                                    self.qkb[:, j * P:(j + 1) * P], self.ident)
            T.inc(inst)
            tm2 = self.T.n
            v.wait_ge(self.T.sem, tm2)
            qdst = bass.AP(self._handles["qT"], t * P, [[4 * SEQ, P], [SEQ, 4], [1, P]])
            kdst = bass.AP(self._handles["kT"], t * P, [[4 * SEQ, P], [SEQ, 4], [1, P]])
            V.inc(v.tensor_copy(qdst, trB[:, 0:INNER].rearrange("p (c q) -> p c q", q=P)))
            V.inc(v.tensor_copy(kdst, trB[:, INNER:2 * INNER]
                                .rearrange("p (c q) -> p c q", q=P)))
            evac_tr[t] = self.V.n
            x_users[t] = self.V.n

    def emit_B(self):
        nc = self.nc
        v, s, te, gp, sy = nc.vector, nc.scalar, nc.tensor, nc.gpsimd, nc.sync
        V, S, T, G, sO = self.V, self.S, self.T, self.G, self.sO
        ps = self.ps
        for nm in ("oT_sb", "pt_a0", "pt_a1", "pt_b0", "pt_b1", "rsrow", "rsln",
                   "rrep", "oTs", "maskb"):
            self.sb(nm)
        self.T.wait(gp)
        self.V.wait(gp)
        self.S.wait(gp)
        G.inc(gp.memset(self.maskb, 1.0))
        G.wait(gp)
        G.inc(gp.affine_select(out=self.maskb, in_=self.maskb, compare_op=AL.is_ge,
                               fill=0.0, base=-INNER, pattern=[[1, 2 * INNER]],
                               channel_multiplier=-1))
        G.wait(v)
        G.wait(te)
        self.vfence()
        V.wait(te)
        S.wait(te)
        vm_b0 = self.V.n
        sa_banks = [ps["q"], ps["k"]]
        sb_banks = [ps["v"], ps["r"]]
        pta = [self.pt_a0, self.pt_a1]
        ptb = [self.pt_b0, self.pt_b1]
        trR = [ps["tA"], ps["tB"]]

        vm_epi = vm_b0
        self._b_pend_odd = None
        self._b_last_av = 0
        for hp in range(4):
            for ci in range(8):
                nj = 4 * ci + 4
                qs = self.qT[:, hp * SEQ + ci * INNER: hp * SEQ + (ci + 1) * INNER]
                tm_sp = {}
                sm_exp = {}
                vm_msk = {}
                tm_av = {}

                def emit_spair(j):
                    par = j % 2
                    ks = self.kT[:, hp * SEQ + j * P: hp * SEQ + (j + 1) * P]
                    if j >= 2:
                        te.wait_ge(self.S.sem, sm_exp[j - 2])
                    elif (hp, ci) != (0, 0):
                        te.wait_ge(self.S.sem, self._b_last_exp)
                    nc.tensor.matmul(sa_banks[par], ks[0:64, :], qs[0:64, :],
                                     start=True, stop=True, tile_position=(0, 0))
                    inst = nc.tensor.matmul(sb_banks[par], ks[64:128, :], qs[64:128, :],
                                            start=True, stop=True, tile_position=(64, 0))
                    T.inc(inst)
                    tm_sp[j] = self.T.n
                    s.wait_ge(self.T.sem, tm_sp[j])
                    if j >= 2:
                        s.wait_ge(self.T.sem, tm_av[j - 2])
                    elif self._b_last_av:
                        s.wait_ge(self.T.sem, self._b_last_av)
                    s.activation(pta[par], sa_banks[par], AF.Exp)
                    S.inc(s.activation(ptb[par], sb_banks[par], AF.Exp))
                    sm_exp[j] = self.sfence() if j == nj - 1 else self.S.n
                    if j >= 4 * ci:
                        r = j * P - ci * INNER
                        msl = self.maskb[:, INNER - r: 2 * INNER - r]
                        v.wait_ge(self.S.sem, sm_exp[j])
                        v.tensor_tensor(pta[par], pta[par], msl, AL.mult)
                        V.inc(v.tensor_tensor(ptb[par], ptb[par], msl, AL.mult))
                        vm_msk[j] = self.vfence()

                def emit_av(j):
                    par = j % 2
                    if j in vm_msk:
                        te.wait_ge(self.V.sem, vm_msk[j])
                    else:
                        te.wait_ge(self.S.sem, sm_exp[j])
                    if j == 0:
                        te.wait_ge(self.V.sem, vm_epi)
                    va = self.vaug[:, j * 520 + (2 * hp) * 65:
                                   j * 520 + (2 * hp) * 65 + 65]
                    vb = self.vaug[:, j * 520 + (2 * hp + 1) * 65:
                                   j * 520 + (2 * hp + 1) * 65 + 65]
                    nc.tensor.matmul(ps["oa"][0:65, :], va, pta[par],
                                     start=(j == 0), stop=(j == nj - 1))
                    inst = nc.tensor.matmul(ps["ob"][0:65, :], vb, ptb[par],
                                            start=(j == 0), stop=(j == nj - 1))
                    T.inc(inst)
                    tm_av[j] = self.T.n

                emit_spair(0)
                if self._b_pend_odd is not None:
                    coff_p, vm_p = self._b_pend_odd
                    self._b_pend_odd = None
                    sy.wait_ge(self.V.sem, vm_p)
                    sy.wait_ge(self.T.sem, tm_sp[0])
                    sO.inc(sy.dma_start(
                        out=self.oT_sb[64:128, coff_p:coff_p + INNER],
                        in_=self.oTs[:, 0:INNER]))
                for j in range(nj):
                    if j + 1 < nj:
                        emit_spair(j + 1)
                    emit_av(j)
                self._b_last_exp = sm_exp[nj - 1]
                self._b_last_av = tm_av[nj - 1]
                v.wait_ge(self.T.sem, tm_av[nj - 1])
                V.inc(v.tensor_copy(self.rsrow[64:65, 0:INNER], ps["oa"][64:65, :]))
                V.inc(v.tensor_copy(self.rsrow[64:65, INNER:], ps["ob"][64:65, :]))
                self.vfence()
                V.wait(s)
                S.inc(s.activation(self.rsln[64:65, :], self.rsrow[64:65, :], AF.Ln))
                S.wait(s)
                S.inc(s.activation(self.rsrow[64:65, :], self.rsln[64:65, :],
                                   AF.Exp, scale=-1.0))
                sm_rec = self.sfence()
                te.wait_ge(self.S.sem, sm_rec)
                nc.tensor.matmul(trR[0][0:64, :], self.ones65[64:65, 0:64],
                                 self.rsrow[64:65, 0:INNER], start=True, stop=True)
                inst = nc.tensor.matmul(trR[1][0:64, :], self.ones65[64:65, 0:64],
                                        self.rsrow[64:65, INNER:], start=True, stop=True)
                T.inc(inst)
                tm_rep = self.T.n
                v.wait_ge(self.T.sem, tm_rep)
                V.inc(v.tensor_copy(self.rrep[:, 0:INNER], trR[0][0:64, :]))
                V.inc(v.tensor_copy(self.rrep[:, INNER:], trR[1][0:64, :]))
                V.wait(v)
                coff = hp * SEQ + ci * INNER
                V.inc(v.tensor_tensor(
                    self.oT_sb[0:64, coff:coff + INNER], ps["oa"][0:64, :],
                    self.rrep[:, 0:INNER], AL.mult))
                v.wait_ge(self.sO.sem, self.sO.n)
                V.inc(v.tensor_tensor(self.oTs[:, 0:INNER], ps["ob"][0:64, :],
                                      self.rrep[:, INNER:], AL.mult))
                vm_epi = self.V.n
                self._b_pend_odd = (coff, vm_epi)
        coff_p, vm_p = self._b_pend_odd
        self._b_pend_odd = None
        gp.wait_ge(self.V.sem, vm_p)
        G.inc(gp.memset(self.smalls[:, 61:62], 0.0))
        sy.wait_ge(G.sem, self.G.n)
        sO.inc(sy.dma_start(out=self.oT_sb[64:128, coff_p:coff_p + INNER],
                            in_=self.oTs[:, 0:INNER]))

    def emit_C2(self):
        nc = self.nc
        v, s, te, gp, sy = nc.vector, nc.scalar, nc.tensor, nc.gpsimd, nc.sync
        V, S, T, sP, sY, sW = self.V, self.S, self.T, self.sP, self.sY, self.sW
        ps = self.ps
        for nm in ("pbuf", "obuf", "wout_b"):
            self.sb(nm)
        self.V.wait(sy)
        self.S.wait(sy)
        self.T.wait(sy)
        for kc in range(4):
            for hh in range(2):
                idx = kc * 2 + hh
                par = idx % 2
                stg = self.obuf[:, par * D: par * D + INNER]
                if idx >= 2:
                    sy.wait_ge(V.sem, self._wo_v[idx - 2])
                sW.inc(sy.dma_start(out=stg, in_=self.wout_h[kc * P:(kc + 1) * P,
                                                             hh * 512:(hh + 1) * 512]))
                sW.wait(v)
                if idx == 0:
                    self.G.wait(v, self.dma_echo(sW, self.sW.n))
                self._wo_v = getattr(self, "_wo_v", [0] * 8)
                V.inc(v.tensor_copy(
                    self.wout_b[:, kc * D + hh * 512: kc * D + (hh + 1) * 512], stg))
                self._wo_v[idx] = self.V.n
        for idx in (8, 9):
            kc, hh = 0, idx - 8
            par = idx % 2
            stg = self.obuf[:, par * D: par * D + INNER]
            sy.wait_ge(V.sem, self._wo_v[idx - 2])
            sW.inc(sy.dma_start(out=stg, in_=self.wout_h[kc * P:(kc + 1) * P,
                                                         hh * 512:(hh + 1) * 512]))
            sW.wait(v)
            V.inc(v.tensor_copy(
                self.wout_b[:, kc * D + hh * 512: kc * D + (hh + 1) * 512], stg))
        self.sO.wait(sy)
        self.sO.wait(te)
        self.vfence()
        V.wait(te)
        out_done = {}
        res_done = {}
        for t in range(NT):
            par = t % 2
            pt_ = self.pbuf[:, par * D:(par + 1) * D]
            ot_ = self.obuf[:, par * D:(par + 1) * D]
            if t >= 2:
                sy.wait_ge(V.sem, res_done[t - 2])
            sP.inc(sy.dma_start(out=pt_, in_=self.part_h[t * P:(t + 1) * P, :]))
            bpair = ("oa", "ob") if par == 0 else ("q", "k")
            if t >= 2:
                te.wait_ge(self.V.sem, res_done[t - 2])
            for nb, bank in enumerate(bpair):
                for cb in range(4):
                    inst = nc.tensor.matmul(
                        ps[bank], self.oT_sb[:, cb * SEQ + t * P: cb * SEQ + (t + 1) * P],
                        self.wout_b[:, cb * D + nb * INNER: cb * D + (nb + 1) * INNER],
                        start=(cb == 0), stop=(cb == 3))
            T.inc(inst)
            tm = self.T.n
            v.wait_ge(self.T.sem, tm)
            sP.wait(v)
            v.wait_ge(self.sY.sem, self.sY.n)
            if t >= 1:
                sy.wait_ge(self.V.sem, res_done[t - 1])
                sy.wait_ge(self.T.sem, tm)
                po_ = self.obuf[:, ((t - 1) % 2) * D:((t - 1) % 2 + 1) * D]
                sY.inc(sy.dma_start(out=self.y_h[(t - 1) * P:t * P, :], in_=po_))
                out_done[t - 1] = self.sY.n
            V.inc(v.tensor_tensor(ot_[:, 0:INNER], ps[bpair[0]], pt_[:, 0:INNER], AL.add))
            V.inc(v.tensor_tensor(ot_[:, INNER:D], ps[bpair[1]], pt_[:, INNER:D], AL.add))
            res_done[t] = self.V.n
        t = NT - 1
        gp = nc.gpsimd
        gp.wait_ge(self.V.sem, res_done[t])
        self.G.inc(gp.memset(self.smalls[:, 61:62], 0.0))
        sy.wait_ge(self.G.sem, self.G.n)
        sY.inc(sy.dma_start(out=self.y_h[t * P:(t + 1) * P, :],
                            in_=self.obuf[:, (t % 2) * D:(t % 2 + 1) * D]))
        out_done[t] = self.sY.n

    def emit_tail(self):
        gp = self.nc.gpsimd
        for c in (self.sY, self.sP, self.sO, self.sX, self.sW,
                  self.T, self.V, self.S, self.G):
            c.wait(gp)


def build():
    return MK2().build()


def kernel(L0_attn_x, L0_attn_ln_w, L0_attn_ln_b, L0_attn_w_qkv, L0_attn_w_out,
           L0_ffn_w_gate, L0_ffn_w_up, L0_ffn_w_down):
    from concourse.bass_utils import run_bass_kernel_spmd
    x = np.ascontiguousarray(np.asarray(L0_attn_x, dtype=np.float32))
    args = dict(ln_w=np.ascontiguousarray(np.asarray(L0_attn_ln_w, dtype=np.float32)),
                ln_b=np.ascontiguousarray(np.asarray(L0_attn_ln_b, dtype=np.float32)),
                w_qkv=np.ascontiguousarray(np.asarray(L0_attn_w_qkv, dtype=np.float32)),
                w_out=np.ascontiguousarray(np.asarray(L0_attn_w_out, dtype=np.float32)),
                w_gate=np.ascontiguousarray(np.asarray(L0_ffn_w_gate, dtype=np.float32)),
                w_up=np.ascontiguousarray(np.asarray(L0_ffn_w_up, dtype=np.float32)),
                w_down=np.ascontiguousarray(np.asarray(L0_ffn_w_down, dtype=np.float32)))
    nc = build()
    in_maps = [dict(x=np.ascontiguousarray(x[b]), **args) for b in range(8)]
    res = run_bass_kernel_spmd(nc, in_maps, core_ids=list(range(8)))
    return np.stack([np.asarray(res.results[b]["y"]) for b in range(8)], axis=0)


if __name__ == "__main__":
    nc = build()
    print("instructions:", sum(1 for _ in nc.all_instructions()))


# version 10
# speedup vs baseline: 1.3980x; 1.0126x over previous
import numpy as np

import concourse.bass as bass
import concourse.mybir as mybir

F32 = mybir.dt.float32
BF16 = mybir.dt.bfloat16
AL = mybir.AluOpType
AF = mybir.ActivationFunctionType
AX = mybir.AxisListType

P = 128
D = 1024
SEQ = 4096
NT = SEQ // P
H = 8
DH = 64
INNER = 512
IFF = 2816
QB = 16
LN_EPS = 1e-5
L2_EPS = 1e-12
LOG8 = float(np.log(np.float64(8.0)))
ARENA = 24576
PART_SZ = 229376


class Ctr:
    def __init__(self, nc, name, step=1):
        self._cm = nc.semaphore(name)
        self.sem = self._cm.__enter__()
        self.n = 0
        self.step = step

    def inc(self, inst, k=1):
        inst.then_inc(self.sem, k * self.step)
        self.n += k * self.step
        return self.n

    def wait(self, eng, n=None):
        v = self.n if n is None else n
        if v > 0:
            eng.wait_ge(self.sem, v)


def _layout():
    L = {}

    def put(name, off, shape, dtype):
        L[name] = (off, shape, dtype)
        nb = int(np.prod(shape[1:])) * (2 if dtype == BF16 else 4)
        return off + (nb + 31) // 32 * 32

    o = ARENA
    o = put("ident", o, [P, P], BF16)
    o = put("ones65", o, [P, 64], F32)
    o = put("mu_all", o, [P, NT], F32)
    o = put("rstd_all", o, [P, NT], F32)
    o = put("smalls", o, [P, 64], F32)
    o = put("consts", o, [P, 8], F32)
    r1 = o
    e = put("gateT", r1, [P, 8 * IFF], BF16)
    e = put("upT", e, [P, 8 * IFF], BF16)
    e = put("qT", r1, [P, 4 * SEQ], BF16)
    e = put("kT", e, [P, 4 * SEQ], BF16)
    r1_end = put("vaug", e, [P, NT * 520], BF16)
    r2 = r1_end
    e = put("downT", r2, [P, 22 * D], BF16)
    e = put("wqkv_b", r2, [P, 8 * 3 * INNER], BF16)
    e = put("lnw_rep", e, [P, D], F32)
    e = put("lnb_rep", e, [P, D], F32)
    assert e <= r2 + 45056
    e = put("oT_sb", r2, [P, 4 * SEQ], BF16)
    e = put("wout_b", e, [P, 4 * D], BF16)
    assert e <= r2 + 45056
    r3 = r2 + 45056
    e = put("xf", r3, [P, 2 * D], F32)
    e = put("xo", e, [P, D], F32)
    e = put("xq", e, [P, D], BF16)
    e = put("xqT", e, [P, 8 * P], BF16)
    e = put("hq", e, [P, 2 * INNER], BF16)
    e = put("hqT", e, [P, 2 * INNER], BF16)
    wr = e
    e = put("wstg", e, [P, 2 * 1408], F32)
    e2 = put("sg_sb", wr, [P, 2 * INNER], F32)
    e2 = put("hch", e2, [P, 2 * INNER], F32)
    assert e2 <= e
    e = put("qs_m", e, [P, 2 * 64], F32)
    e = put("qs_s", e, [P, 2 * 64], F32)
    e = put("qs_r", e, [P, 2 * 64], F32)
    e = put("qs_a", e, [P, 2 * D], F32)
    e = put("qs_u", e, [P, 2 * D], F32)
    e = put("qs_g", e, [P, 2 * D], BF16)
    e = put("wqb", e, [P, 2 * D], BF16)
    c1_end = e
    assert c1_end <= PART_SZ - 288, c1_end
    a0 = L["xo"][0]
    e = put("xn", a0, [P, D], F32)
    e = put("sqs2", e, [P, D], F32)
    e = put("xn2b", e, [P, D], BF16)
    e = put("xn2T", e, [P, 8 * P], BF16)
    e = put("qkb", e, [P, 2 * INNER], BF16)
    e = put("sqs", e, [P, INNER], F32)
    e = put("wkstg", e, [P, 2 * 768], F32)
    assert e <= c1_end
    b0 = r3
    e = put("pt_a0", b0, [P, INNER], BF16)
    e = put("pt_a1", e, [P, INNER], BF16)
    e = put("pt_b0", e, [P, INNER], BF16)
    e = put("pt_b1", e, [P, INNER], BF16)
    e = put("rsrow", e, [P, 2 * INNER], F32)
    e = put("rsln", e, [P, 2 * INNER], F32)
    e = put("rrep", e, [64, 2 * INNER], F32)
    e = put("oTs", e, [64, INNER], BF16)
    e = put("maskb", e, [P, 2 * INNER], BF16)
    assert e <= c1_end
    e = put("pbuf", b0, [P, 2 * D], F32)
    e = put("obuf", e, [P, 2 * D], F32)
    assert e <= c1_end
    return L


class MK2:
    def __init__(self):
        nc = bass.Bass("TRN2", target_bir_lowering=False,
                       detect_race_conditions=False)
        self.nc = nc
        self.x_h = nc.dram_tensor("x", [SEQ, D], F32, kind="ExternalInput").ap()
        self.lnw_h = nc.dram_tensor("ln_w", [D], F32, kind="ExternalInput").ap()
        self.lnb_h = nc.dram_tensor("ln_b", [D], F32, kind="ExternalInput").ap()
        self.wqkv_h = nc.dram_tensor("w_qkv", [D, 3 * INNER], F32, kind="ExternalInput").ap()
        self.wout_h = nc.dram_tensor("w_out", [INNER, D], F32, kind="ExternalInput").ap()
        self.wgate_h = nc.dram_tensor("w_gate", [IFF, D], F32, kind="ExternalInput").ap()
        self.wup_h = nc.dram_tensor("w_up", [IFF, D], F32, kind="ExternalInput").ap()
        self.wdown_h = nc.dram_tensor("w_down", [D, IFF], F32, kind="ExternalInput").ap()
        self.y_h = nc.dram_tensor("y", [SEQ, D], F32, kind="ExternalOutput").ap()
        self.part_h = nc.dram_tensor("part_scr", [SEQ, D], F32, kind="Internal").ap()
        self._handles = {}
        self._L = _layout()

    def sb(self, name):
        off, shape, dt = self._L[name]
        h = self.nc.alloc_sbuf_tensor_at(name, shape, dt, offset=off)
        self._handles[name] = h
        setattr(self, name, h.ap())
        return h.ap()

    def vfence(self):
        v = self.nc.vector
        self.V.wait(v)
        self.V.inc(v.memset(self.smalls[:, 62:63], 0.0))
        return self.V.n

    def sfence(self):
        s = self.nc.scalar
        self.S.wait(s)
        self.S.inc(s.activation(self.smalls[:, 63:64], self.smalls[:, 62:63],
                                AF.Abs))
        return self.S.n

    def dma_echo(self, ctr, n):
        gp = self.nc.gpsimd
        ctr.wait(gp, n)
        self.G.inc(gp.memset(self.smalls[:, 61:62], 0.0))
        return self.G.n

    def build(self):
        nc = self.nc
        self.sX = Ctr(nc, "c_sX", 16)
        self.sW = Ctr(nc, "c_sW", 16)
        self.sP = Ctr(nc, "c_sP", 16)
        self.sO = Ctr(nc, "c_sO", 16)
        self.sY = Ctr(nc, "c_sY", 16)
        self.T = Ctr(nc, "c_T")
        self.V = Ctr(nc, "c_V")
        self.S = Ctr(nc, "c_S")
        self.G = Ctr(nc, "c_G")

        with (nc.psum_tensor("pq", [P, INNER], F32) as pq,
              nc.psum_tensor("pk", [P, INNER], F32) as pk,
              nc.psum_tensor("pv", [P, INNER], F32) as pv,
              nc.psum_tensor("pr", [P, INNER], F32) as pr,
              nc.psum_tensor("poa", [P, INNER], F32) as poa,
              nc.psum_tensor("pob", [P, INNER], F32) as pob,
              nc.psum_tensor("ptA", [P, INNER], F32) as ptA,
              nc.psum_tensor("ptB", [P, INNER], F32) as ptB,
              nc.Block() as block):
            self.ps = dict(q=pq[:], k=pk[:], v=pv[:], r=pr[:],
                           oa=poa[:], ob=pob[:], tA=ptA[:], tB=ptB[:])

            @block.gpsimd
            def _(g):
                self.emit_setup()
                self.emit_prep()
                self.emit_C1()
                self.emit_A()
                self.emit_B()
                self.emit_C2()
                self.emit_tail()
        return nc

    def emit_setup(self):
        nc = self.nc
        v, s, gp, sy = nc.vector, nc.scalar, nc.gpsimd, nc.sync
        V, S, G = self.V, self.S, self.G
        for nm in ("ident", "ones65", "mu_all", "rstd_all", "smalls", "consts",
                   "gateT", "upT", "downT", "xf", "xo", "xq", "xqT", "hq", "hqT",
                   "sg_sb", "hch", "wstg", "qs_m", "qs_s", "qs_r", "qs_a", "qs_u",
                   "qs_g", "wqb"):
            self.sb(nm)
        self.qscr = dict(m=self.qs_m, sc2=self.qs_s, r6=self.qs_r,
                         a=self.qs_a, u=self.qs_u, sgn=self.qs_g)
        for i, val in enumerate([LN_EPS, L2_EPS, LOG8]):
            G.inc(gp.memset(self.consts[:, i:i + 1], val))
            nc.const_aps.register(F32, val, self.consts[:, i:i + 1])
        G.inc(gp.memset(self.ident, 0.0))
        G.wait(gp)
        G.inc(gp.affine_select(out=self.ident, in_=self.ident, compare_op=AL.not_equal,
                               fill=1.0, base=0, pattern=[[-1, P]], channel_multiplier=1))
        V.inc(v.memset(self.ones65, 1.0))
        G.wait(s)
        G.wait(v)
        G.wait(nc.tensor)

    def emit_qhead(self, x_ap, F):
        nc = self.nc
        v, s, gp = nc.vector, nc.scalar, nc.gpsimd
        V, S, G = self.V, self.S, self.G
        pp = self._qn & 1
        self._qn += 1
        r = self._qrec[pp]
        nb = F // QB
        o64, oD = pp * 64, pp * D
        m = self.qs_m[:, o64:o64 + nb]
        sc2 = self.qs_s[:, o64:o64 + nb]
        r6 = self.qs_r[:, o64:o64 + nb]
        a = self.qs_a[:, oD:oD + F]
        x3 = x_ap.rearrange("p (b s) -> p b s", s=QB)
        a3 = a.rearrange("p (b s) -> p b s", s=QB)
        V.wait(s, r["v"])
        G.wait(s, r["g"])
        s.activation(a, x_ap, AF.Abs)
        S.inc(s.activation(self.qs_g[:, oD:oD + F], x_ap, AF.Sign))
        s_hi = S.n
        G.wait(v, r["g"])
        V.wait(v)
        V.inc(v.tensor_reduce(m, x3, AX.X, AL.max, apply_absolute_value=True))
        V.wait(v)
        v.tensor_scalar(sc2, m, 1.0 / 12.0, 5e-13, AL.mult, AL.max)
        V.inc(v.tensor_scalar(r6, m, 1.0 / 6.0, 1e-12, AL.mult, AL.max))
        V.wait(v)
        V.inc(v.reciprocal(r6, r6))
        v_r6 = V.n
        S.wait(gp, s_hi)
        V.wait(gp, v_r6)
        G.inc(gp.tensor_tensor(a3, a3, r6.broadcast_to([P, nb, QB]), AL.mult))
        return dict(pp=pp, F=F, s_hi=s_hi, g_am=self.G.n)

    def emit_qtail(self, tok, out_ap):
        nc = self.nc
        v, gp = nc.vector, nc.gpsimd
        V, G = self.V, self.G
        pp, F = tok["pp"], tok["F"]
        nb = F // QB
        o64, oD = pp * 64, pp * D
        sc2 = self.qs_s[:, o64:o64 + nb]
        a = self.qs_a[:, oD:oD + F]
        u = self.qs_u[:, oD:oD + F]
        sgn = self.qs_g[:, oD:oD + F]
        G.wait(v, tok["g_am"])

        def st(inst):
            V.inc(inst)
            V.wait(v)

        st(v.tensor_scalar(u, a, 5.0, 2.0, AL.is_gt, AL.mult))
        st(v.scalar_tensor_tensor(u, a, 3.5, u, AL.is_gt, AL.add))
        st(v.scalar_tensor_tensor(u, a, 2.5, u, AL.is_gt, AL.add))
        st(v.tensor_scalar(u, u, 2.0, None, AL.mult))
        st(v.scalar_tensor_tensor(u, a, 1.75, u, AL.is_gt, AL.add))
        st(v.scalar_tensor_tensor(u, a, 1.25, u, AL.is_gt, AL.add))
        st(v.scalar_tensor_tensor(u, a, 0.75, u, AL.is_gt, AL.add))
        st(v.scalar_tensor_tensor(u, a, 0.25, u, AL.is_gt, AL.add))
        v_st = V.n
        V.wait(gp, v_st)
        G.inc(gp.tensor_tensor(u, u, sgn, AL.mult))
        G.wait(gp)
        G.inc(gp.tensor_tensor(out_ap.rearrange("p (b s) -> p b s", s=QB),
                               u.rearrange("p (b s) -> p b s", s=QB),
                               sc2.broadcast_to([P, nb, QB]), AL.mult))
        self._qrec[pp] = dict(v=v_st, s=tok["s_hi"], g=self.G.n)
        return self.G.n

    def emit_prep(self):
        nc = self.nc
        v, s, te, gp, sy = nc.vector, nc.scalar, nc.tensor, nc.gpsimd, nc.sync
        V, S, T, G, sW, sX = self.V, self.S, self.T, self.G, self.sW, self.sX
        ps = self.ps
        trA = ps["tA"].bitcast(BF16)
        self._qn = 0
        self._qrec = {0: dict(v=0, s=0, g=0), 1: dict(v=0, s=0, g=0)}
        self._tr_hist = []
        self._wq_evac = 0
        jobs = []
        for name, src_h in (("gateT", self.wgate_h), ("upT", self.wup_h)):
            for rc in range(22):
                dst = bass.AP(self._handles[name], rc * P,
                              [[8 * IFF, P], [IFF, 8], [1, P]])
                jobs.append((src_h[rc * P:(rc + 1) * P, :], D, [(0, D, dst)]))
        for dc in range(8):
            for hh in range(2):
                chains = []
                for c0, w in ((0, 768), (768, 640)):
                    kb0 = hh * 11 + c0 // P
                    dst = bass.AP(self._handles["downT"], dc * P + kb0 * D,
                                  [[22 * D, P], [D, w // P], [1, P]])
                    chains.append((c0, w, dst))
                jobs.append((self.wdown_h[dc * P:(dc + 1) * P,
                                          hh * 1408:(hh + 1) * 1408],
                             1408, chains))
        jrec = {}
        swm = {}
        sm = self.smalls
        sxm = {}
        st_rec = {0: dict(v=0, s=0), 1: dict(v=0, s=0)}
        stat_users = {}

        def issue_stat_dma(t):
            if t >= NT:
                return
            if t >= 2:
                sy.wait_ge(V.sem, stat_users[t - 2][0])
                sy.wait_ge(S.sem, stat_users[t - 2][1])
            sX.inc(sy.dma_start(out=self.xf[:, (t % 2) * D:(t % 2 + 1) * D],
                                in_=self.x_h[t * P:(t + 1) * P, :]))
            sxm[t] = sX.n

        def emit_stats_tile(t):
            par = t % 2
            so = par * 8
            xt = self.xf[:, par * D:(par + 1) * D]
            r = st_rec[par]
            sX.wait(v, sxm[t])
            sX.wait(s, sxm[t])
            S.wait(v, r["s"])
            V.wait(s, r["v"])
            V.inc(v.tensor_reduce(sm[:, so:so + 1], xt, AX.X, AL.add))
            V.wait(v)
            S.inc(s.activation(self.xo, xt, AF.Square,
                               accum_out=sm[:, so + 1:so + 2]))
            s_sq = S.n
            V.inc(v.tensor_scalar(self.mu_all[:, t:t + 1], sm[:, so:so + 1],
                                  1.0 / D, None, AL.mult))
            v.wait_ge(self.S.sem, s_sq)
            V.inc(v.tensor_scalar(sm[:, so + 2:so + 3], sm[:, so + 1:so + 2],
                                  1.0 / D, None, AL.mult))
            V.wait(v)
            V.inc(v.tensor_tensor(sm[:, so + 3:so + 4], self.mu_all[:, t:t + 1],
                                  self.mu_all[:, t:t + 1], AL.mult))
            V.wait(v)
            V.inc(v.tensor_sub(sm[:, so + 2:so + 3], sm[:, so + 2:so + 3],
                               sm[:, so + 3:so + 4]))
            v_var = self.vfence()
            s.wait_ge(self.V.sem, v_var)
            S.inc(s.activation(sm[:, so + 3:so + 4], sm[:, so + 2:so + 3],
                               AF.Ln, bias=LN_EPS))
            S.wait(s)
            S.inc(s.activation(self.rstd_all[:, t:t + 1], sm[:, so + 3:so + 4],
                               AF.Exp, scale=-0.5))
            s_exp = self.sfence()
            st_rec[par] = dict(v=v_var, s=s_exp)
            stat_users[t] = (v_var, s_exp)

        def issue_dma(j):
            src_ap, width, _ = jobs[j]
            if j >= 2:
                sy.wait_ge(V.sem, jrec[j - 2][0])
                sy.wait_ge(S.sem, jrec[j - 2][1])
            sW.inc(sy.dma_start(out=self.wstg[:, (j % 2) * 1408:
                                              (j % 2) * 1408 + width],
                                in_=src_ap))
            swm[j] = sW.n

        pend = None

        def flush_pend():
            nonlocal pend
            if pend is None:
                return
            tok, pp, w, dst = pend
            pend = None
            tpm = self._tr_hist[-2] if len(self._tr_hist) >= 2 else 0
            self.T.wait(gp, tpm)
            gd = self.emit_qtail(tok, self.wqb[:, pp * D:pp * D + w])
            ntr = w // P
            G.wait(te, gd)
            te.wait_ge(V.sem, self._wq_evac)
            for i in range(ntr):
                inst = te.transpose(trA[:, i * P:(i + 1) * P],
                                    self.wqb[:, pp * D + i * P:
                                             pp * D + (i + 1) * P],
                                    self.ident)
            T.inc(inst)
            self._tr_hist.append(self.T.n)
            v.wait_ge(self.T.sem, self.T.n)
            V.inc(v.tensor_copy(dst, trA[:, 0:ntr * P]
                                .rearrange("p (c q) -> p c q", q=P)))
            self._wq_evac = self.V.n

        issue_dma(0)
        echo0 = self.dma_echo(sW, swm[0])
        issue_stat_dma(0)
        issue_stat_dma(1)
        stat_t = 0
        for j, (src_ap, width, chains) in enumerate(jobs):
            if j + 1 < len(jobs):
                issue_dma(j + 1)
            if j % 2 == 1 and stat_t < NT:
                emit_stats_tile(stat_t)
                issue_stat_dma(stat_t + 2)
                stat_t += 1
            stg = self.wstg[:, (j % 2) * 1408:(j % 2) * 1408 + width]
            for c0, w, dst in chains:
                pp = self._qn & 1
                sW.wait(v, swm[j])
                sW.wait(s, swm[j])
                if j == 0:
                    G.wait(v, echo0)
                    G.wait(s, echo0)
                tok = self.emit_qhead(stg[:, c0:c0 + w], w)
                flush_pend()
                pend = (tok, pp, w, dst)
            jrec[j] = (self.V.n, self.S.n)
        flush_pend()
        while stat_t < NT:
            emit_stats_tile(stat_t)
            issue_stat_dma(stat_t + 2)
            stat_t += 1
        self._stats_v_final = self.V.n
        self._stats_s_final = self.S.n

    def emit_C1(self):
        nc = self.nc
        v, s, te, gp, sy = nc.vector, nc.scalar, nc.tensor, nc.gpsimd, nc.sync
        V, S, T, G, sX, sP = self.V, self.S, self.T, self.G, self.sX, self.sP
        ps = self.ps
        trA = ps["tA"].bitcast(BF16)
        trB = ps["tB"].bitcast(BF16)
        NCH = 6
        sm = self.smalls
        x_users = {}
        part_done = {}
        p_dma = {}
        sxm = {}
        silu_hist = []
        hm_hist = []
        trb_hist = []
        xq_s_hi = self._stats_s_final
        tmx_prev = 0
        xqT_evac = self._wq_evac
        hqT_evac = 0
        pend = None

        def issue_x_dma(t):
            if t >= NT:
                return
            if t <= 1:
                sy.wait_ge(V.sem, self._stats_v_final)
                sy.wait_ge(S.sem, self._stats_s_final)
            else:
                sy.wait_ge(V.sem, x_users[t - 2])
            sX.inc(sy.dma_start(out=self.xf[:, (t % 2) * D:(t % 2 + 1) * D],
                                in_=self.x_h[t * P:(t + 1) * P, :]))
            sxm[t] = sX.n

        def emit_stats_xn(t):
            par = t % 2
            xt = self.xf[:, par * D:(par + 1) * D]
            sX.wait(v, sxm[t])
            v.wait_ge(self.S.sem, xq_s_hi)
            V.inc(v.tensor_scalar(self.xo, xt, self.mu_all[:, t:t + 1],
                                  self.rstd_all[:, t:t + 1],
                                  AL.subtract, AL.mult))
            return self.vfence()

        def emit_trax(gq, t):
            nonlocal tmx_prev, xqT_evac
            G.wait(te, gq)
            te.wait_ge(self.V.sem, xqT_evac)
            for j in range(8):
                inst = te.transpose(trA[:, j * P:(j + 1) * P],
                                    self.xq[:, j * P:(j + 1) * P], self.ident)
            T.inc(inst)
            tmx_prev = self.T.n
            v.wait_ge(self.T.sem, tmx_prev)
            V.inc(v.tensor_copy(self.xqT, trA))
            xqT_evac = self.vfence()
            te.wait_ge(self.V.sem, xqT_evac)

        tgu_d = {}

        def emit_gu(ic, nlag):
            w = INNER if ic < 5 else (IFF - 5 * INNER)
            c0 = ic * INNER
            gb, ub = (("q", "k") if ic % 2 == 0 else ("v", "r"))
            if len(silu_hist) >= nlag:
                te.wait_ge(self.S.sem, silu_hist[-nlag])
                te.wait_ge(self.V.sem, hm_hist[-nlag])
            for kc in range(8):
                nc.tensor.matmul(ps[gb][:, 0:w],
                                 self.xqT[:, kc * P:(kc + 1) * P],
                                 self.gateT[:, kc * IFF + c0:
                                            kc * IFF + c0 + w],
                                 start=(kc == 0), stop=(kc == 7))
            for kc in range(8):
                inst = nc.tensor.matmul(ps[ub][:, 0:w],
                                        self.xqT[:, kc * P:(kc + 1) * P],
                                        self.upT[:, kc * IFF + c0:
                                                 kc * IFF + c0 + w],
                                        start=(kc == 0), stop=(kc == 7))
            T.inc(inst)
            tgu_d[ic] = self.T.n

        issue_x_dma(0)
        issue_x_dma(1)
        v_xn = emit_stats_xn(0)
        V.wait(s, v_xn)
        tokXQ = self.emit_qhead(self.xo, D)
        xq_s_hi = tokXQ["s_hi"]
        self.T.wait(gp, tmx_prev)
        gq = self.emit_qtail(tokXQ, self.xq)
        emit_trax(gq, 0)
        emit_gu(0, nlag=2)

        for t in range(NT):
            def flush_pending(t=t):
                nonlocal pend, hqT_evac
                if pend is None:
                    return
                tok2, icp, w2, ic2 = pend
                pend = None
                tbm = trb_hist[-2] if len(trb_hist) >= 2 else 0
                self.T.wait(gp, tbm)
                gq2 = self.emit_qtail(
                    tok2, self.hq[:, icp * INNER:icp * INNER + w2])
                ntr = (w2 + P - 1) // P
                G.wait(te, gq2)
                te.wait_ge(self.V.sem, hqT_evac)
                for j2 in range(ntr):
                    inst2 = te.transpose(trB[:, j2 * P:(j2 + 1) * P],
                                         self.hq[:, icp * INNER + j2 * P:
                                                 icp * INNER + (j2 + 1) * P],
                                         self.ident)
                T.inc(inst2)
                trb_hist.append(self.T.n)
                v.wait_ge(self.T.sem, self.T.n)
                V.inc(v.tensor_copy(self.hqT[:, icp * INNER:
                                             icp * INNER + ntr * P],
                                    trB[:, 0:ntr * P]))
                hqT_evac = self.vfence()
                te.wait_ge(self.V.sem, hqT_evac)
                if ic2 == 0 and t > 0:
                    te.wait_ge(self.V.sem, part_done[t - 1])
                for nb_, bank in enumerate(("oa", "ob")):
                    for kb in range(ntr):
                        kbg = ic2 * 4 + kb
                        inst2 = nc.tensor.matmul(
                            ps[bank],
                            self.hqT[:, icp * INNER + kb * P:
                                     icp * INNER + (kb + 1) * P],
                            self.downT[:, kbg * D + nb_ * INNER:
                                       kbg * D + (nb_ + 1) * INNER],
                            start=(ic2 == 0 and kb == 0),
                            stop=(ic2 == NCH - 1 and kb == ntr - 1))
                T.inc(inst2)

            par = t % 2
            xt = self.xf[:, par * D:(par + 1) * D]
            for ic in range(NCH):
                w = INNER if ic < 5 else (IFF - 5 * INNER)
                c0 = ic * INNER
                icpar = ic % 2
                gb, ub = (("q", "k") if icpar == 0 else ("v", "r"))
                if ic + 1 < NCH:
                    emit_gu(ic + 1, nlag=1)
                tgu = tgu_d[ic]
                s.wait_ge(self.T.sem, tgu)
                if len(hm_hist) >= 2:
                    s.wait_ge(self.V.sem, hm_hist[-2])
                sg = self.sg_sb[:, icpar * INNER: icpar * INNER + w]
                S.inc(s.activation(sg, ps[gb][:, 0:w], AF.Silu))
                silu_hist.append(self.S.n)
                hc = self.hch[:, icpar * INNER: icpar * INNER + w]
                v.wait_ge(self.T.sem, tgu)
                v.wait_ge(self.S.sem, silu_hist[-1])
                if ic == 0 and t >= 1:
                    sP.wait(v, p_dma[t - 1])
                V.inc(v.tensor_tensor(hc, ps[ub][:, 0:w], sg, AL.mult))
                hm_hist.append(self.V.n)
                V.wait(s, hm_hist[-1])
                tok = self.emit_qhead(hc, w)
                flush_pending()
                pend = (tok, icpar, w, ic)
            if t + 1 < NT:
                v_xn = emit_stats_xn(t + 1)
                V.wait(s, v_xn)
                tokXQ = self.emit_qhead(self.xo, D)
                xq_s_hi_t = tokXQ["s_hi"]
            s_hi5 = pend[0]["s_hi"]
            flush_pending()
            tdown = self.T.n
            v.wait_ge(self.T.sem, tdown)
            v.wait_ge(self.S.sem, s_hi5)
            V.inc(v.tensor_tensor(self.hch[:, 0:INNER], ps["oa"],
                                  xt[:, 0:INNER], AL.add))
            V.inc(v.tensor_tensor(self.hch[:, INNER:D], ps["ob"],
                                  xt[:, INNER:D], AL.add))
            part_done[t] = self.V.n
            x_users[t] = self.V.n
            sy.wait_ge(self.V.sem, part_done[t])
            sP.inc(sy.dma_start(out=self.part_h[t * P:(t + 1) * P, :],
                                in_=self.hch[:, 0:D]))
            p_dma[t] = self.sP.n
            issue_x_dma(t + 2)
            if t + 1 < NT:
                xq_s_hi = xq_s_hi_t
                self.T.wait(gp, tmx_prev)
                gq = self.emit_qtail(tokXQ, self.xq)
                emit_trax(gq, t + 1)
                emit_gu(0, nlag=2)

    def emit_A(self):
        nc = self.nc
        v, s, te, gp, sy = nc.vector, nc.scalar, nc.tensor, nc.gpsimd, nc.sync
        V, S, T, G, sX, sW = self.V, self.S, self.T, self.G, self.sX, self.sW
        ps = self.ps
        trA = ps["tA"].bitcast(BF16)
        trB = ps["tB"].bitcast(BF16)
        for nm in ("qT", "kT", "vaug", "wqkv_b", "lnw_rep", "lnb_rep", "xn",
                   "sqs2", "xn2b", "xn2T", "qkb", "sqs", "wkstg"):
            self.sb(nm)
        sm = self.smalls

        ones_ap = bass.AP(self._handles["vaug"], 64, [[520 * NT, P], [520, NT], [65, 8]])
        G.inc(gp.memset(ones_ap, 1.0))
        sX.inc(sy.dma_start(out=self.lnw_rep,
                            in_=self.lnw_h.rearrange("(o d) -> o d", o=1).broadcast_to([P, D])))
        sX.inc(sy.dma_start(out=self.lnb_rep,
                            in_=self.lnb_h.rearrange("(o d) -> o d", o=1).broadcast_to([P, D])))
        for kc in range(8):
            for hh in range(2):
                idx = kc * 2 + hh
                par = idx % 2
                stg = self.wkstg[:, par * 768:(par + 1) * 768]
                if idx >= 2:
                    sy.wait_ge(V.sem, self._wk_v[idx - 2])
                sW.inc(sy.dma_start(out=stg, in_=self.wqkv_h[kc * P:(kc + 1) * P,
                                                             hh * 768:(hh + 1) * 768]))
                sW.wait(v)
                if idx == 0:
                    G.wait(v, self.dma_echo(sW, self.sW.n))
                self._wk_v = getattr(self, "_wk_v", [0] * 16)
                V.inc(v.tensor_copy(
                    self.wqkv_b[:, kc * 1536 + hh * 768: kc * 1536 + (hh + 1) * 768],
                    stg))
                self._wk_v[idx] = self.V.n
        for idx in (16, 17):
            kc, hh = 0, idx - 16
            par = idx % 2
            stg = self.wkstg[:, par * 768:(par + 1) * 768]
            sy.wait_ge(V.sem, self._wk_v[idx - 2])
            sW.inc(sy.dma_start(out=stg, in_=self.wqkv_h[kc * P:(kc + 1) * P,
                                                         hh * 768:(hh + 1) * 768]))
            sW.wait(v)
            V.inc(v.tensor_copy(
                self.wqkv_b[:, kc * 1536 + hh * 768: kc * 1536 + (hh + 1) * 768],
                stg))
        x_users = {}
        s_users = {}
        evac_qkv = {}
        evac_tr = {}
        c1_v_final = self.V.n
        c1_s_final = self.S.n
        c1_g_final = self.G.n
        for t in range(NT):
            par = t % 2
            xt = self.xf[:, par * D:(par + 1) * D]
            if t >= 2:
                sy.wait_ge(V.sem, x_users.get(t - 2, 0))
                sy.wait_ge(S.sem, s_users.get(t - 2, 0))
            else:
                sy.wait_ge(V.sem, c1_v_final)
                sy.wait_ge(S.sem, c1_s_final)
                sy.wait_ge(G.sem, c1_g_final)
            sX.inc(sy.dma_start(out=xt, in_=self.x_h[t * P:(t + 1) * P, :]))
            sX.wait(v)
            sX.wait(s)
            V.inc(v.tensor_scalar(self.xn, xt, self.mu_all[:, t:t + 1],
                                  self.rstd_all[:, t:t + 1], AL.subtract, AL.mult))
            x_users[t] = self.V.n
            V.wait(v)
            V.wait(s)
            V.inc(v.tensor_reduce(sm[:, 8:9], self.xn, AX.X, AL.add))
            S.inc(s.activation(self.sqs2, self.xn, AF.Square, accum_out=sm[:, 9:10]))
            s_users[t] = self.sfence()
            V.wait(v)
            V.inc(v.tensor_scalar(sm[:, 10:11], sm[:, 8:9], 1.0 / D, None, AL.mult))
            S.wait(v)
            V.inc(v.tensor_scalar(sm[:, 11:12], sm[:, 9:10], 1.0 / D, None, AL.mult))
            V.wait(v)
            V.inc(v.tensor_tensor(sm[:, 12:13], sm[:, 10:11], sm[:, 10:11], AL.mult))
            V.wait(v)
            V.inc(v.tensor_sub(sm[:, 11:12], sm[:, 11:12], sm[:, 12:13]))
            self.vfence()
            V.wait(s)
            S.inc(s.activation(sm[:, 12:13], sm[:, 11:12], AF.Ln, bias=LN_EPS))
            S.wait(s)
            S.inc(s.activation(sm[:, 13:14], sm[:, 12:13], AF.Exp, scale=-0.5))
            self.sfence()
            S.wait(v)
            V.inc(v.tensor_scalar(self.sqs2, self.xn, sm[:, 10:11], sm[:, 13:14],
                                  AL.subtract, AL.mult))
            V.wait(v)
            V.inc(v.tensor_tensor(self.sqs2, self.sqs2, self.lnw_rep, AL.mult))
            V.wait(v)
            V.inc(v.tensor_tensor(self.xn2b, self.sqs2, self.lnb_rep, AL.add))
            vm_xn2b = self.vfence()
            te.wait_ge(self.V.sem, vm_xn2b)
            for j in range(8):
                inst = te.transpose(trA[:, j * P:(j + 1) * P],
                                    self.xn2b[:, j * P:(j + 1) * P], self.ident)
            T.inc(inst)
            tm1 = self.T.n
            v.wait_ge(self.T.sem, tm1)
            V.inc(v.tensor_copy(self.xn2T, trA))
            vm_xn2T = self.vfence()
            banks = ("q", "k", "v") if par == 0 else ("oa", "ob", "r")
            te.wait_ge(self.V.sem, vm_xn2T)
            if t >= 2:
                te.wait_ge(self.V.sem, evac_qkv[t - 2])
            for bank_i, psn in enumerate(banks):
                for kc in range(8):
                    inst = nc.tensor.matmul(
                        ps[psn], self.xn2T[:, kc * P:(kc + 1) * P],
                        self.wqkv_b[:, kc * 1536 + bank_i * INNER:
                                    kc * 1536 + (bank_i + 1) * INNER],
                        start=(kc == 0), stop=(kc == 7))
            T.inc(inst)
            tqkv = self.T.n
            s.wait_ge(self.T.sem, tqkv)
            v.wait_ge(self.T.sem, tqkv)
            V.inc(v.tensor_copy(self.sqs, ps[banks[0]]))
            V.wait(v)
            V.inc(v.tensor_tensor(self.sqs, self.sqs, self.sqs, AL.mult))
            V.wait(v)
            V.inc(v.tensor_reduce(sm[:, 16:24],
                                  self.sqs.rearrange("p (h c) -> p h c", c=DH),
                                  AX.X, AL.add))
            self.vfence()
            V.wait(s)
            S.inc(s.activation(sm[:, 24:32], sm[:, 16:24], AF.Ln, bias=L2_EPS))
            S.wait(s)
            S.inc(s.activation(sm[:, 32:40], sm[:, 24:32], AF.Exp, scale=-0.5, bias=LOG8))
            self.sfence()
            S.wait(v)
            V.inc(v.tensor_tensor(self.qkb[:, 0:INNER].rearrange("p (h c) -> p h c", c=DH),
                                  ps[banks[0]].rearrange("p (h c) -> p h c", c=DH),
                                  sm[:, 32:40].broadcast_to([P, H, DH]), AL.mult))
            V.inc(v.tensor_copy(self.sqs, ps[banks[1]]))
            V.wait(v)
            V.inc(v.tensor_tensor(self.sqs, self.sqs, self.sqs, AL.mult))
            V.wait(v)
            V.inc(v.tensor_reduce(sm[:, 16:24],
                                  self.sqs.rearrange("p (h c) -> p h c", c=DH),
                                  AX.X, AL.add))
            self.vfence()
            V.wait(s)
            S.inc(s.activation(sm[:, 24:32], sm[:, 16:24], AF.Ln, bias=L2_EPS))
            S.wait(s)
            S.inc(s.activation(sm[:, 32:40], sm[:, 24:32], AF.Exp, scale=-0.5))
            s_users[t] = self.sfence()
            S.wait(v)
            V.inc(v.tensor_tensor(self.qkb[:, INNER:2 * INNER]
                                  .rearrange("p (h c) -> p h c", c=DH),
                                  ps[banks[1]].rearrange("p (h c) -> p h c", c=DH),
                                  sm[:, 32:40].broadcast_to([P, H, DH]), AL.mult))
            vdst = bass.AP(self._handles["vaug"], t * 520,
                           [[520 * NT, P], [65, 8], [1, DH]])
            V.inc(v.tensor_copy(vdst, ps[banks[2]].rearrange("p (h c) -> p h c", c=DH)))
            evac_qkv[t] = self.V.n
            vm_qkb = self.vfence()
            te.wait_ge(self.V.sem, vm_qkb)
            if t >= 1:
                te.wait_ge(self.V.sem, evac_tr[t - 1])
            for j in range(8):
                inst = te.transpose(trB[:, j * P:(j + 1) * P],
                                    self.qkb[:, j * P:(j + 1) * P], self.ident)
            T.inc(inst)
            tm2 = self.T.n
            v.wait_ge(self.T.sem, tm2)
            qdst = bass.AP(self._handles["qT"], t * P, [[4 * SEQ, P], [SEQ, 4], [1, P]])
            kdst = bass.AP(self._handles["kT"], t * P, [[4 * SEQ, P], [SEQ, 4], [1, P]])
            V.inc(v.tensor_copy(qdst, trB[:, 0:INNER].rearrange("p (c q) -> p c q", q=P)))
            V.inc(v.tensor_copy(kdst, trB[:, INNER:2 * INNER]
                                .rearrange("p (c q) -> p c q", q=P)))
            evac_tr[t] = self.V.n
            x_users[t] = self.V.n

    def emit_B(self):
        nc = self.nc
        v, s, te, gp, sy = nc.vector, nc.scalar, nc.tensor, nc.gpsimd, nc.sync
        V, S, T, G, sO = self.V, self.S, self.T, self.G, self.sO
        ps = self.ps
        for nm in ("oT_sb", "pt_a0", "pt_a1", "pt_b0", "pt_b1", "rsrow", "rsln",
                   "rrep", "oTs", "maskb"):
            self.sb(nm)
        self.T.wait(gp)
        self.V.wait(gp)
        self.S.wait(gp)
        G.inc(gp.memset(self.maskb, 1.0))
        G.wait(gp)
        G.inc(gp.affine_select(out=self.maskb, in_=self.maskb, compare_op=AL.is_ge,
                               fill=0.0, base=-INNER, pattern=[[1, 2 * INNER]],
                               channel_multiplier=-1))
        G.wait(v)
        G.wait(te)
        self.vfence()
        V.wait(te)
        S.wait(te)
        vm_b0 = self.V.n
        sa_banks = [ps["q"], ps["k"]]
        sb_banks = [ps["v"], ps["r"]]
        pta = [self.pt_a0, self.pt_a1]
        ptb = [self.pt_b0, self.pt_b1]
        trR = [ps["tA"], ps["tB"]]

        vm_epi = vm_b0
        self._b_pend_odd = None
        self._b_last_av = 0
        for hp in range(4):
            for ci in range(8):
                nj = 4 * ci + 4
                qs = self.qT[:, hp * SEQ + ci * INNER: hp * SEQ + (ci + 1) * INNER]
                tm_sp = {}
                sm_exp = {}
                vm_msk = {}
                tm_av = {}

                def emit_spair(j):
                    par = j % 2
                    ks = self.kT[:, hp * SEQ + j * P: hp * SEQ + (j + 1) * P]
                    if j >= 2:
                        te.wait_ge(self.S.sem, sm_exp[j - 2])
                    elif (hp, ci) != (0, 0):
                        te.wait_ge(self.S.sem, self._b_last_exp)
                    nc.tensor.matmul(sa_banks[par], ks[0:64, :], qs[0:64, :],
                                     start=True, stop=True, tile_position=(0, 0))
                    inst = nc.tensor.matmul(sb_banks[par], ks[64:128, :], qs[64:128, :],
                                            start=True, stop=True, tile_position=(64, 0))
                    T.inc(inst)
                    tm_sp[j] = self.T.n
                    s.wait_ge(self.T.sem, tm_sp[j])
                    if j >= 2:
                        s.wait_ge(self.T.sem, tm_av[j - 2])
                    elif self._b_last_av:
                        s.wait_ge(self.T.sem, self._b_last_av)
                    s.activation(pta[par], sa_banks[par], AF.Exp)
                    S.inc(s.activation(ptb[par], sb_banks[par], AF.Exp))
                    sm_exp[j] = self.sfence() if j == nj - 1 else self.S.n
                    if j >= 4 * ci:
                        r = j * P - ci * INNER
                        msl = self.maskb[:, INNER - r: 2 * INNER - r]
                        v.wait_ge(self.S.sem, sm_exp[j])
                        v.tensor_tensor(pta[par], pta[par], msl, AL.mult)
                        V.inc(v.tensor_tensor(ptb[par], ptb[par], msl, AL.mult))
                        vm_msk[j] = self.vfence()

                def emit_av(j):
                    par = j % 2
                    if j in vm_msk:
                        te.wait_ge(self.V.sem, vm_msk[j])
                    else:
                        te.wait_ge(self.S.sem, sm_exp[j])
                    if j == 0:
                        te.wait_ge(self.V.sem, vm_epi)
                    va = self.vaug[:, j * 520 + (2 * hp) * 65:
                                   j * 520 + (2 * hp) * 65 + 65]
                    vb = self.vaug[:, j * 520 + (2 * hp + 1) * 65:
                                   j * 520 + (2 * hp + 1) * 65 + 65]
                    nc.tensor.matmul(ps["oa"][0:65, :], va, pta[par],
                                     start=(j == 0), stop=(j == nj - 1))
                    inst = nc.tensor.matmul(ps["ob"][0:65, :], vb, ptb[par],
                                            start=(j == 0), stop=(j == nj - 1))
                    T.inc(inst)
                    tm_av[j] = self.T.n

                emit_spair(0)
                if self._b_pend_odd is not None:
                    coff_p, vm_p = self._b_pend_odd
                    self._b_pend_odd = None
                    sy.wait_ge(self.V.sem, vm_p)
                    sy.wait_ge(self.T.sem, tm_sp[0])
                    sO.inc(sy.dma_start(
                        out=self.oT_sb[64:128, coff_p:coff_p + INNER],
                        in_=self.oTs[:, 0:INNER]))
                for j in range(nj):
                    if j + 1 < nj:
                        emit_spair(j + 1)
                    emit_av(j)
                self._b_last_exp = sm_exp[nj - 1]
                self._b_last_av = tm_av[nj - 1]
                v.wait_ge(self.T.sem, tm_av[nj - 1])
                V.inc(v.tensor_copy(self.rsrow[64:65, 0:INNER], ps["oa"][64:65, :]))
                V.inc(v.tensor_copy(self.rsrow[64:65, INNER:], ps["ob"][64:65, :]))
                self.vfence()
                V.wait(s)
                S.inc(s.activation(self.rsln[64:65, :], self.rsrow[64:65, :], AF.Ln))
                S.wait(s)
                S.inc(s.activation(self.rsrow[64:65, :], self.rsln[64:65, :],
                                   AF.Exp, scale=-1.0))
                sm_rec = self.sfence()
                te.wait_ge(self.S.sem, sm_rec)
                nc.tensor.matmul(trR[0][0:64, :], self.ones65[64:65, 0:64],
                                 self.rsrow[64:65, 0:INNER], start=True, stop=True)
                inst = nc.tensor.matmul(trR[1][0:64, :], self.ones65[64:65, 0:64],
                                        self.rsrow[64:65, INNER:], start=True, stop=True)
                T.inc(inst)
                tm_rep = self.T.n
                v.wait_ge(self.T.sem, tm_rep)
                V.inc(v.tensor_copy(self.rrep[:, 0:INNER], trR[0][0:64, :]))
                V.inc(v.tensor_copy(self.rrep[:, INNER:], trR[1][0:64, :]))
                V.wait(v)
                coff = hp * SEQ + ci * INNER
                V.inc(v.tensor_tensor(
                    self.oT_sb[0:64, coff:coff + INNER], ps["oa"][0:64, :],
                    self.rrep[:, 0:INNER], AL.mult))
                v.wait_ge(self.sO.sem, self.sO.n)
                V.inc(v.tensor_tensor(self.oTs[:, 0:INNER], ps["ob"][0:64, :],
                                      self.rrep[:, INNER:], AL.mult))
                vm_epi = self.V.n
                self._b_pend_odd = (coff, vm_epi)
        coff_p, vm_p = self._b_pend_odd
        self._b_pend_odd = None
        gp.wait_ge(self.V.sem, vm_p)
        G.inc(gp.memset(self.smalls[:, 61:62], 0.0))
        sy.wait_ge(G.sem, self.G.n)
        sO.inc(sy.dma_start(out=self.oT_sb[64:128, coff_p:coff_p + INNER],
                            in_=self.oTs[:, 0:INNER]))

    def emit_C2(self):
        nc = self.nc
        v, s, te, gp, sy = nc.vector, nc.scalar, nc.tensor, nc.gpsimd, nc.sync
        V, S, T, sP, sY, sW = self.V, self.S, self.T, self.sP, self.sY, self.sW
        ps = self.ps
        for nm in ("pbuf", "obuf", "wout_b"):
            self.sb(nm)
        self.V.wait(sy)
        self.S.wait(sy)
        self.T.wait(sy)
        for kc in range(4):
            for hh in range(2):
                idx = kc * 2 + hh
                par = idx % 2
                stg = self.obuf[:, par * D: par * D + INNER]
                if idx >= 2:
                    sy.wait_ge(V.sem, self._wo_v[idx - 2])
                sW.inc(sy.dma_start(out=stg, in_=self.wout_h[kc * P:(kc + 1) * P,
                                                             hh * 512:(hh + 1) * 512]))
                sW.wait(v)
                if idx == 0:
                    self.G.wait(v, self.dma_echo(sW, self.sW.n))
                self._wo_v = getattr(self, "_wo_v", [0] * 8)
                V.inc(v.tensor_copy(
                    self.wout_b[:, kc * D + hh * 512: kc * D + (hh + 1) * 512], stg))
                self._wo_v[idx] = self.V.n
        for idx in (8, 9):
            kc, hh = 0, idx - 8
            par = idx % 2
            stg = self.obuf[:, par * D: par * D + INNER]
            sy.wait_ge(V.sem, self._wo_v[idx - 2])
            sW.inc(sy.dma_start(out=stg, in_=self.wout_h[kc * P:(kc + 1) * P,
                                                         hh * 512:(hh + 1) * 512]))
            sW.wait(v)
            V.inc(v.tensor_copy(
                self.wout_b[:, kc * D + hh * 512: kc * D + (hh + 1) * 512], stg))
        self.sO.wait(sy)
        self.sO.wait(te)
        self.vfence()
        V.wait(te)
        out_done = {}
        res_done = {}
        for t in range(NT):
            par = t % 2
            pt_ = self.pbuf[:, par * D:(par + 1) * D]
            ot_ = self.obuf[:, par * D:(par + 1) * D]
            if t >= 2:
                sy.wait_ge(V.sem, res_done[t - 2])
            sP.inc(sy.dma_start(out=pt_, in_=self.part_h[t * P:(t + 1) * P, :]))
            bpair = ("oa", "ob") if par == 0 else ("q", "k")
            if t >= 2:
                te.wait_ge(self.V.sem, res_done[t - 2])
            for nb, bank in enumerate(bpair):
                for cb in range(4):
                    inst = nc.tensor.matmul(
                        ps[bank], self.oT_sb[:, cb * SEQ + t * P: cb * SEQ + (t + 1) * P],
                        self.wout_b[:, cb * D + nb * INNER: cb * D + (nb + 1) * INNER],
                        start=(cb == 0), stop=(cb == 3))
            T.inc(inst)
            tm = self.T.n
            v.wait_ge(self.T.sem, tm)
            sP.wait(v)
            v.wait_ge(self.sY.sem, self.sY.n)
            if t >= 1:
                sy.wait_ge(self.V.sem, res_done[t - 1])
                sy.wait_ge(self.T.sem, tm)
                po_ = self.obuf[:, ((t - 1) % 2) * D:((t - 1) % 2 + 1) * D]
                sY.inc(sy.dma_start(out=self.y_h[(t - 1) * P:t * P, :], in_=po_))
                out_done[t - 1] = self.sY.n
            V.inc(v.tensor_tensor(ot_[:, 0:INNER], ps[bpair[0]], pt_[:, 0:INNER], AL.add))
            V.inc(v.tensor_tensor(ot_[:, INNER:D], ps[bpair[1]], pt_[:, INNER:D], AL.add))
            res_done[t] = self.V.n
        t = NT - 1
        gp = nc.gpsimd
        gp.wait_ge(self.V.sem, res_done[t])
        self.G.inc(gp.memset(self.smalls[:, 61:62], 0.0))
        sy.wait_ge(self.G.sem, self.G.n)
        sY.inc(sy.dma_start(out=self.y_h[t * P:(t + 1) * P, :],
                            in_=self.obuf[:, (t % 2) * D:(t % 2 + 1) * D]))
        out_done[t] = self.sY.n

    def emit_tail(self):
        gp = self.nc.gpsimd
        for c in (self.sY, self.sP, self.sO, self.sX, self.sW,
                  self.T, self.V, self.S, self.G):
            c.wait(gp)


def build():
    return MK2().build()


def kernel(L0_attn_x, L0_attn_ln_w, L0_attn_ln_b, L0_attn_w_qkv, L0_attn_w_out,
           L0_ffn_w_gate, L0_ffn_w_up, L0_ffn_w_down):
    from concourse.bass_utils import run_bass_kernel_spmd
    x = np.ascontiguousarray(np.asarray(L0_attn_x, dtype=np.float32))
    args = dict(ln_w=np.ascontiguousarray(np.asarray(L0_attn_ln_w, dtype=np.float32)),
                ln_b=np.ascontiguousarray(np.asarray(L0_attn_ln_b, dtype=np.float32)),
                w_qkv=np.ascontiguousarray(np.asarray(L0_attn_w_qkv, dtype=np.float32)),
                w_out=np.ascontiguousarray(np.asarray(L0_attn_w_out, dtype=np.float32)),
                w_gate=np.ascontiguousarray(np.asarray(L0_ffn_w_gate, dtype=np.float32)),
                w_up=np.ascontiguousarray(np.asarray(L0_ffn_w_up, dtype=np.float32)),
                w_down=np.ascontiguousarray(np.asarray(L0_ffn_w_down, dtype=np.float32)))
    nc = build()
    in_maps = [dict(x=np.ascontiguousarray(x[b]), **args) for b in range(8)]
    res = run_bass_kernel_spmd(nc, in_maps, core_ids=list(range(8)))
    return np.stack([np.asarray(res.results[b]["y"]) for b in range(8)], axis=0)


if __name__ == "__main__":
    nc = build()
    print("instructions:", sum(1 for _ in nc.all_instructions()))


# version 12
# speedup vs baseline: 1.4322x; 1.0244x over previous
import numpy as np

import concourse.bass as bass
import concourse.mybir as mybir

F32 = mybir.dt.float32
BF16 = mybir.dt.bfloat16
AL = mybir.AluOpType
AF = mybir.ActivationFunctionType
AX = mybir.AxisListType

P = 128
D = 1024
SEQ = 4096
NT = SEQ // P
H = 8
DH = 64
INNER = 512
IFF = 2816
QB = 16
LN_EPS = 1e-5
L2_EPS = 1e-12
LOG8 = float(np.log(np.float64(8.0)))
ARENA = 24576
PART_SZ = 229376


class Ctr:
    def __init__(self, nc, name, step=1):
        self._cm = nc.semaphore(name)
        self.sem = self._cm.__enter__()
        self.n = 0
        self.step = step

    def inc(self, inst, k=1):
        inst.then_inc(self.sem, k * self.step)
        self.n += k * self.step
        return self.n

    def wait(self, eng, n=None):
        v = self.n if n is None else n
        if v > 0:
            eng.wait_ge(self.sem, v)


def _layout():
    L = {}

    def put(name, off, shape, dtype):
        L[name] = (off, shape, dtype)
        nb = int(np.prod(shape[1:])) * (2 if dtype == BF16 else 4)
        return off + (nb + 31) // 32 * 32

    o = ARENA
    o = put("ident", o, [P, P], BF16)
    o = put("ones65", o, [P, 64], F32)
    o = put("mu_all", o, [P, NT], F32)
    o = put("rstd_all", o, [P, NT], F32)
    o = put("smalls", o, [P, 64], F32)
    o = put("consts", o, [P, 8], F32)
    r1 = o
    e = put("gateT", r1, [P, 8 * IFF], BF16)
    e = put("upT", e, [P, 8 * IFF], BF16)
    e = put("qT", r1, [P, 4 * SEQ], BF16)
    e = put("kT", e, [P, 4 * SEQ], BF16)
    r1_end = put("vaug", e, [P, NT * 520], BF16)
    r2 = r1_end
    e = put("downT", r2, [P, 22 * D], BF16)
    e = put("wqkv_b", r2, [P, 8 * 3 * INNER], BF16)
    e = put("lnw_rep", e, [P, D], F32)
    e = put("lnb_rep", e, [P, D], F32)
    assert e <= r2 + 45056
    e = put("oT_sb", r2, [P, 4 * SEQ], BF16)
    e = put("wout_b", e, [P, 4 * D], BF16)
    assert e <= r2 + 45056
    r3 = r2 + 45056
    e = put("xf", r3, [P, 2 * D], F32)
    e = put("xo", e, [P, D], F32)
    e = put("xq", e, [P, D], BF16)
    e = put("xqT", e, [P, 8 * P], BF16)
    e = put("hq", e, [P, 2 * INNER], BF16)
    e = put("hqT", e, [P, 2 * INNER], BF16)
    wr = e
    e = put("wstg", e, [P, 2 * 1408], F32)
    e2 = put("sg_sb", wr, [P, 2 * INNER], F32)
    e2 = put("hch", e2, [P, 2 * INNER], F32)
    assert e2 <= e
    e = put("qs_m", e, [P, 2 * 64], F32)
    e = put("qs_s", e, [P, 2 * 64], F32)
    e = put("qs_r", e, [P, 2 * 64], F32)
    e = put("qs_a", e, [P, 2 * D], F32)
    e = put("qs_u", e, [P, 2 * D], F32)
    e = put("qs_g", e, [P, 2 * D], BF16)
    e = put("wqb", e, [P, 2 * D], BF16)
    c1_end = e
    assert c1_end <= PART_SZ - 288, c1_end
    a0 = L["xo"][0]
    e = put("xn", a0, [P, D], F32)
    e = put("sqs2", e, [P, D], F32)
    e = put("xn2b", e, [P, D], BF16)
    e = put("xn2T", e, [P, 8 * P], BF16)
    e = put("qkb", e, [P, 2 * INNER], BF16)
    e = put("sqs", e, [P, INNER], F32)
    e = put("wkstg", e, [P, 2 * 768], F32)
    assert e <= c1_end
    b0 = r3
    e = put("pt_a0", b0, [P, INNER], BF16)
    e = put("pt_a1", e, [P, INNER], BF16)
    e = put("pt_b0", e, [P, INNER], BF16)
    e = put("pt_b1", e, [P, INNER], BF16)
    e = put("rsrow", e, [P, 2 * INNER], F32)
    e = put("rsln", e, [P, 2 * INNER], F32)
    e = put("rrep", e, [64, 2 * INNER], F32)
    e = put("oTs", e, [64, INNER], BF16)
    e = put("maskb", e, [P, 2 * INNER], BF16)
    assert e <= c1_end
    e = put("pbuf", b0, [P, 2 * D], F32)
    e = put("obuf", e, [P, 2 * D], F32)
    assert e <= c1_end
    return L


class MK2:
    def __init__(self):
        nc = bass.Bass("TRN2", target_bir_lowering=False,
                       detect_race_conditions=False)
        self.nc = nc
        self.x_h = nc.dram_tensor("x", [SEQ, D], F32, kind="ExternalInput").ap()
        self.lnw_h = nc.dram_tensor("ln_w", [D], F32, kind="ExternalInput").ap()
        self.lnb_h = nc.dram_tensor("ln_b", [D], F32, kind="ExternalInput").ap()
        self.wqkv_h = nc.dram_tensor("w_qkv", [D, 3 * INNER], F32, kind="ExternalInput").ap()
        self.wout_h = nc.dram_tensor("w_out", [INNER, D], F32, kind="ExternalInput").ap()
        self.wgate_h = nc.dram_tensor("w_gate", [IFF, D], F32, kind="ExternalInput").ap()
        self.wup_h = nc.dram_tensor("w_up", [IFF, D], F32, kind="ExternalInput").ap()
        self.wdown_h = nc.dram_tensor("w_down", [D, IFF], F32, kind="ExternalInput").ap()
        self.y_h = nc.dram_tensor("y", [SEQ, D], F32, kind="ExternalOutput").ap()
        self.part_h = nc.dram_tensor("part_scr", [SEQ, D], F32, kind="Internal").ap()
        self._handles = {}
        self._L = _layout()

    def sb(self, name):
        off, shape, dt = self._L[name]
        h = self.nc.alloc_sbuf_tensor_at(name, shape, dt, offset=off)
        self._handles[name] = h
        setattr(self, name, h.ap())
        return h.ap()

    def vfence(self):
        v = self.nc.vector
        self.V.wait(v)
        self.V.inc(v.memset(self.smalls[:, 62:63], 0.0))
        return self.V.n

    def sfence(self):
        s = self.nc.scalar
        self.S.wait(s)
        self.S.inc(s.activation(self.smalls[:, 63:64], self.smalls[:, 62:63],
                                AF.Abs))
        return self.S.n

    def dma_echo(self, ctr, n):
        gp = self.nc.gpsimd
        ctr.wait(gp, n)
        self.G.inc(gp.memset(self.smalls[:, 61:62], 0.0))
        return self.G.n

    def build(self):
        nc = self.nc
        self.sX = Ctr(nc, "c_sX", 16)
        self.sW = Ctr(nc, "c_sW", 16)
        self.sP = Ctr(nc, "c_sP", 16)
        self.sO = Ctr(nc, "c_sO", 16)
        self.sY = Ctr(nc, "c_sY", 16)
        self.T = Ctr(nc, "c_T")
        self.V = Ctr(nc, "c_V")
        self.S = Ctr(nc, "c_S")
        self.G = Ctr(nc, "c_G")

        with (nc.psum_tensor("pq", [P, INNER], F32) as pq,
              nc.psum_tensor("pk", [P, INNER], F32) as pk,
              nc.psum_tensor("pv", [P, INNER], F32) as pv,
              nc.psum_tensor("pr", [P, INNER], F32) as pr,
              nc.psum_tensor("poa", [P, INNER], F32) as poa,
              nc.psum_tensor("pob", [P, INNER], F32) as pob,
              nc.psum_tensor("ptA", [P, INNER], F32) as ptA,
              nc.psum_tensor("ptB", [P, INNER], F32) as ptB,
              nc.Block() as block):
            self.ps = dict(q=pq[:], k=pk[:], v=pv[:], r=pr[:],
                           oa=poa[:], ob=pob[:], tA=ptA[:], tB=ptB[:])

            @block.gpsimd
            def _(g):
                self.emit_setup()
                self.emit_prep()
                self.emit_C1()
                self.emit_A()
                self.emit_B()
                self.emit_C2()
                self.emit_tail()
        return nc

    def emit_setup(self):
        nc = self.nc
        v, s, gp, sy = nc.vector, nc.scalar, nc.gpsimd, nc.sync
        V, S, G = self.V, self.S, self.G
        for nm in ("ident", "ones65", "mu_all", "rstd_all", "smalls", "consts",
                   "gateT", "upT", "downT", "xf", "xo", "xq", "xqT", "hq", "hqT",
                   "sg_sb", "hch", "wstg", "qs_m", "qs_s", "qs_r", "qs_a", "qs_u",
                   "qs_g", "wqb"):
            self.sb(nm)
        self.qscr = dict(m=self.qs_m, sc2=self.qs_s, r6=self.qs_r,
                         a=self.qs_a, u=self.qs_u, sgn=self.qs_g)
        for i, val in enumerate([LN_EPS, L2_EPS, LOG8, L2_EPS / 64.0]):
            G.inc(gp.memset(self.consts[:, i:i + 1], val))
            nc.const_aps.register(F32, val, self.consts[:, i:i + 1])
        G.inc(gp.memset(self.ident, 0.0))
        G.wait(gp)
        G.inc(gp.affine_select(out=self.ident, in_=self.ident, compare_op=AL.not_equal,
                               fill=1.0, base=0, pattern=[[-1, P]], channel_multiplier=1))
        V.inc(v.memset(self.ones65, 1.0))
        G.wait(s)
        G.wait(v)
        G.wait(nc.tensor)

    def emit_qhead(self, x_ap, F):
        nc = self.nc
        v, s, gp = nc.vector, nc.scalar, nc.gpsimd
        V, S, G = self.V, self.S, self.G
        pp = self._qn & 1
        self._qn += 1
        r = self._qrec[pp]
        nb = F // QB
        o64, oD = pp * 64, pp * D
        m = self.qs_m[:, o64:o64 + nb]
        sc2 = self.qs_s[:, o64:o64 + nb]
        r6 = self.qs_r[:, o64:o64 + nb]
        a = self.qs_a[:, oD:oD + F]
        x3 = x_ap.rearrange("p (b s) -> p b s", s=QB)
        a3 = a.rearrange("p (b s) -> p b s", s=QB)
        V.wait(s, r["v"])
        G.wait(s, r["g"])
        s.activation(a, x_ap, AF.Abs)
        S.inc(s.activation(self.qs_g[:, oD:oD + F], x_ap, AF.Sign))
        s_hi = S.n
        G.wait(v, r["g"])
        V.wait(v)
        V.inc(v.tensor_reduce(m, x3, AX.X, AL.max, apply_absolute_value=True))
        V.wait(v)
        v.tensor_scalar(sc2, m, 1.0 / 12.0, 5e-13, AL.mult, AL.max)
        V.inc(v.tensor_scalar(r6, m, 1.0 / 6.0, 1e-12, AL.mult, AL.max))
        V.wait(v)
        V.inc(v.reciprocal(r6, r6))
        v_r6 = V.n
        S.wait(gp, s_hi)
        V.wait(gp, v_r6)
        G.inc(gp.tensor_tensor(a3, a3, r6.broadcast_to([P, nb, QB]), AL.mult))
        return dict(pp=pp, F=F, s_hi=s_hi, g_am=self.G.n)

    def emit_qtail(self, tok, out_ap):
        nc = self.nc
        v, gp = nc.vector, nc.gpsimd
        V, G = self.V, self.G
        pp, F = tok["pp"], tok["F"]
        nb = F // QB
        o64, oD = pp * 64, pp * D
        sc2 = self.qs_s[:, o64:o64 + nb]
        a = self.qs_a[:, oD:oD + F]
        u = self.qs_u[:, oD:oD + F]
        sgn = self.qs_g[:, oD:oD + F]
        G.wait(v, tok["g_am"])

        def st(inst):
            V.inc(inst)
            V.wait(v)

        st(v.tensor_scalar(u, a, 5.0, 2.0, AL.is_gt, AL.mult))
        st(v.scalar_tensor_tensor(u, a, 3.5, u, AL.is_gt, AL.add))
        st(v.scalar_tensor_tensor(u, a, 2.5, u, AL.is_gt, AL.add))
        st(v.tensor_scalar(u, u, 2.0, None, AL.mult))
        st(v.scalar_tensor_tensor(u, a, 1.75, u, AL.is_gt, AL.add))
        st(v.scalar_tensor_tensor(u, a, 1.25, u, AL.is_gt, AL.add))
        st(v.scalar_tensor_tensor(u, a, 0.75, u, AL.is_gt, AL.add))
        st(v.scalar_tensor_tensor(u, a, 0.25, u, AL.is_gt, AL.add))
        v_st = V.n
        V.wait(gp, v_st)
        G.inc(gp.tensor_tensor(u, u, sgn, AL.mult))
        G.wait(gp)
        G.inc(gp.tensor_tensor(out_ap.rearrange("p (b s) -> p b s", s=QB),
                               u.rearrange("p (b s) -> p b s", s=QB),
                               sc2.broadcast_to([P, nb, QB]), AL.mult))
        self._qrec[pp] = dict(v=v_st, s=tok["s_hi"], g=self.G.n)
        return self.G.n

    def emit_prep(self):
        nc = self.nc
        v, s, te, gp, sy = nc.vector, nc.scalar, nc.tensor, nc.gpsimd, nc.sync
        V, S, T, G, sW, sX = self.V, self.S, self.T, self.G, self.sW, self.sX
        ps = self.ps
        trA = ps["tA"].bitcast(BF16)
        self._qn = 0
        self._qrec = {0: dict(v=0, s=0, g=0), 1: dict(v=0, s=0, g=0)}
        self._tr_hist = []
        self._wq_evac = 0
        jobs = []
        for name, src_h in (("gateT", self.wgate_h), ("upT", self.wup_h)):
            for rc in range(22):
                dst = bass.AP(self._handles[name], rc * P,
                              [[8 * IFF, P], [IFF, 8], [1, P]])
                jobs.append((src_h[rc * P:(rc + 1) * P, :], D, [(0, D, dst)]))
        for dc in range(8):
            for hh in range(2):
                chains = []
                for c0, w in ((0, 768), (768, 640)):
                    kb0 = hh * 11 + c0 // P
                    dst = bass.AP(self._handles["downT"], dc * P + kb0 * D,
                                  [[22 * D, P], [D, w // P], [1, P]])
                    chains.append((c0, w, dst))
                jobs.append((self.wdown_h[dc * P:(dc + 1) * P,
                                          hh * 1408:(hh + 1) * 1408],
                             1408, chains))
        jrec = {}
        swm = {}
        sm = self.smalls
        sxm = {}
        st_rec = {0: dict(v=0, s=0), 1: dict(v=0, s=0)}
        stat_users = {}

        def issue_stat_dma(t):
            if t >= NT:
                return
            if t >= 2:
                sy.wait_ge(V.sem, stat_users[t - 2][0])
                sy.wait_ge(S.sem, stat_users[t - 2][1])
            sX.inc(sy.dma_start(out=self.xf[:, (t % 2) * D:(t % 2 + 1) * D],
                                in_=self.x_h[t * P:(t + 1) * P, :]))
            sxm[t] = sX.n

        def emit_stats_tile(t):
            par = t % 2
            so = par * 8
            xt = self.xf[:, par * D:(par + 1) * D]
            r = st_rec[par]
            sX.wait(v, sxm[t])
            sX.wait(s, sxm[t])
            S.wait(v, r["s"])
            V.wait(s, r["v"])
            V.inc(v.tensor_reduce(sm[:, so:so + 1], xt, AX.X, AL.add))
            V.wait(v)
            S.inc(s.activation(self.xo, xt, AF.Square,
                               accum_out=sm[:, so + 1:so + 2]))
            s_sq = S.n
            V.inc(v.tensor_scalar(self.mu_all[:, t:t + 1], sm[:, so:so + 1],
                                  1.0 / D, None, AL.mult))
            v.wait_ge(self.S.sem, s_sq)
            V.inc(v.tensor_scalar(sm[:, so + 2:so + 3], sm[:, so + 1:so + 2],
                                  1.0 / D, None, AL.mult))
            V.wait(v)
            V.inc(v.tensor_tensor(sm[:, so + 3:so + 4], self.mu_all[:, t:t + 1],
                                  self.mu_all[:, t:t + 1], AL.mult))
            V.wait(v)
            V.inc(v.tensor_sub(sm[:, so + 2:so + 3], sm[:, so + 2:so + 3],
                               sm[:, so + 3:so + 4]))
            v_var = self.vfence()
            s.wait_ge(self.V.sem, v_var)
            S.inc(s.activation(sm[:, so + 3:so + 4], sm[:, so + 2:so + 3],
                               AF.Ln, bias=LN_EPS))
            S.wait(s)
            S.inc(s.activation(self.rstd_all[:, t:t + 1], sm[:, so + 3:so + 4],
                               AF.Exp, scale=-0.5))
            s_exp = self.sfence()
            st_rec[par] = dict(v=v_var, s=s_exp)
            stat_users[t] = (v_var, s_exp)

        def issue_dma(j):
            src_ap, width, _ = jobs[j]
            if j >= 2:
                sy.wait_ge(V.sem, jrec[j - 2][0])
                sy.wait_ge(S.sem, jrec[j - 2][1])
            sW.inc(sy.dma_start(out=self.wstg[:, (j % 2) * 1408:
                                              (j % 2) * 1408 + width],
                                in_=src_ap))
            swm[j] = sW.n

        trB = ps["tB"].bitcast(BF16)
        pend = None
        pend_evac = None
        evac_hist = [0, 0]
        prcnt = [0]

        def flush_evac():
            nonlocal pend_evac
            if pend_evac is None:
                return
            trX, ntr, dst, tm = pend_evac
            pend_evac = None
            v.wait_ge(self.T.sem, tm)
            V.inc(v.tensor_copy(dst, trX[:, 0:ntr * P]
                                .rearrange("p (c q) -> p c q", q=P)))
            evac_hist.append(self.V.n)
            self._wq_evac = self.V.n

        def flush_pend():
            nonlocal pend, pend_evac
            if pend is None:
                return
            tok, pp, w, dst = pend
            pend = None
            tpm = self._tr_hist[-2] if len(self._tr_hist) >= 2 else 0
            self.T.wait(gp, tpm)
            gd = self.emit_qtail(tok, self.wqb[:, pp * D:pp * D + w])
            ntr = w // P
            trX = trA if prcnt[0] % 2 == 0 else trB
            prcnt[0] += 1
            G.wait(te, gd)
            te.wait_ge(V.sem, evac_hist[-2])
            for i in range(ntr):
                inst = te.transpose(trX[:, i * P:(i + 1) * P],
                                    self.wqb[:, pp * D + i * P:
                                             pp * D + (i + 1) * P],
                                    self.ident)
            T.inc(inst)
            self._tr_hist.append(self.T.n)
            flush_evac()
            pend_evac = (trX, ntr, dst, self.T.n)

        issue_dma(0)
        echo0 = self.dma_echo(sW, swm[0])
        issue_stat_dma(0)
        issue_stat_dma(1)
        stat_t = 0
        for j, (src_ap, width, chains) in enumerate(jobs):
            if j + 1 < len(jobs):
                issue_dma(j + 1)
            if j % 2 == 1 and stat_t < NT:
                emit_stats_tile(stat_t)
                issue_stat_dma(stat_t + 2)
                stat_t += 1
            stg = self.wstg[:, (j % 2) * 1408:(j % 2) * 1408 + width]
            for c0, w, dst in chains:
                pp = self._qn & 1
                sW.wait(v, swm[j])
                sW.wait(s, swm[j])
                if j == 0:
                    G.wait(v, echo0)
                    G.wait(s, echo0)
                tok = self.emit_qhead(stg[:, c0:c0 + w], w)
                flush_pend()
                pend = (tok, pp, w, dst)
            jrec[j] = (self.V.n, self.S.n)
        flush_pend()
        flush_evac()
        while stat_t < NT:
            emit_stats_tile(stat_t)
            issue_stat_dma(stat_t + 2)
            stat_t += 1
        self._stats_v_final = self.V.n
        self._stats_s_final = self.S.n

    def emit_C1(self):
        nc = self.nc
        v, s, te, gp, sy = nc.vector, nc.scalar, nc.tensor, nc.gpsimd, nc.sync
        V, S, T, G, sX, sP = self.V, self.S, self.T, self.G, self.sX, self.sP
        ps = self.ps
        trA = ps["tA"].bitcast(BF16)
        trB = ps["tB"].bitcast(BF16)
        NCH = 6
        sm = self.smalls
        x_users = {}
        part_done = {}
        p_dma = {}
        sxm = {}
        silu_hist = []
        hm_hist = []
        trb_hist = []
        xq_s_hi = self._stats_s_final
        tmx_prev = 0
        xqT_evac = self._wq_evac
        hqT_evac = self._wq_evac
        pend = None

        def issue_x_dma(t):
            if t >= NT:
                return
            if t <= 1:
                sy.wait_ge(V.sem, self._stats_v_final)
                sy.wait_ge(S.sem, self._stats_s_final)
            else:
                sy.wait_ge(V.sem, x_users[t - 2])
            sX.inc(sy.dma_start(out=self.xf[:, (t % 2) * D:(t % 2 + 1) * D],
                                in_=self.x_h[t * P:(t + 1) * P, :]))
            sxm[t] = sX.n

        def emit_stats_xn(t):
            par = t % 2
            xt = self.xf[:, par * D:(par + 1) * D]
            sX.wait(v, sxm[t])
            v.wait_ge(self.S.sem, xq_s_hi)
            V.inc(v.tensor_scalar(self.xo, xt, self.mu_all[:, t:t + 1],
                                  self.rstd_all[:, t:t + 1],
                                  AL.subtract, AL.mult))
            return self.vfence()

        def emit_trax(gq, t):
            nonlocal tmx_prev, xqT_evac
            G.wait(te, gq)
            te.wait_ge(self.V.sem, xqT_evac)
            for j in range(8):
                inst = te.transpose(trA[:, j * P:(j + 1) * P],
                                    self.xq[:, j * P:(j + 1) * P], self.ident)
            T.inc(inst)
            tmx_prev = self.T.n
            v.wait_ge(self.T.sem, tmx_prev)
            V.inc(v.tensor_copy(self.xqT, trA))
            xqT_evac = self.vfence()
            te.wait_ge(self.V.sem, xqT_evac)

        tgu_d = {}

        def emit_gu(ic, nlag):
            w = INNER if ic < 5 else (IFF - 5 * INNER)
            c0 = ic * INNER
            gb, ub = (("q", "k") if ic % 2 == 0 else ("v", "r"))
            if len(silu_hist) >= nlag:
                te.wait_ge(self.S.sem, silu_hist[-nlag])
                te.wait_ge(self.V.sem, hm_hist[-nlag])
            for kc in range(8):
                nc.tensor.matmul(ps[gb][:, 0:w],
                                 self.xqT[:, kc * P:(kc + 1) * P],
                                 self.gateT[:, kc * IFF + c0:
                                            kc * IFF + c0 + w],
                                 start=(kc == 0), stop=(kc == 7))
            for kc in range(8):
                inst = nc.tensor.matmul(ps[ub][:, 0:w],
                                        self.xqT[:, kc * P:(kc + 1) * P],
                                        self.upT[:, kc * IFF + c0:
                                                 kc * IFF + c0 + w],
                                        start=(kc == 0), stop=(kc == 7))
            T.inc(inst)
            tgu_d[ic] = self.T.n

        issue_x_dma(0)
        issue_x_dma(1)
        v_xn = emit_stats_xn(0)
        V.wait(s, v_xn)
        tokXQ = self.emit_qhead(self.xo, D)
        xq_s_hi = tokXQ["s_hi"]
        self.T.wait(gp, tmx_prev)
        gq = self.emit_qtail(tokXQ, self.xq)
        emit_trax(gq, 0)
        emit_gu(0, nlag=2)

        for t in range(NT):
            def flush_pending(t=t):
                nonlocal pend, hqT_evac
                if pend is None:
                    return
                tok2, icp, w2, ic2 = pend
                pend = None
                tbm = trb_hist[-2] if len(trb_hist) >= 2 else 0
                self.T.wait(gp, tbm)
                gq2 = self.emit_qtail(
                    tok2, self.hq[:, icp * INNER:icp * INNER + w2])
                ntr = (w2 + P - 1) // P
                G.wait(te, gq2)
                te.wait_ge(self.V.sem, hqT_evac)
                for j2 in range(ntr):
                    inst2 = te.transpose(trB[:, j2 * P:(j2 + 1) * P],
                                         self.hq[:, icp * INNER + j2 * P:
                                                 icp * INNER + (j2 + 1) * P],
                                         self.ident)
                T.inc(inst2)
                trb_hist.append(self.T.n)
                v.wait_ge(self.T.sem, self.T.n)
                V.inc(v.tensor_copy(self.hqT[:, icp * INNER:
                                             icp * INNER + ntr * P],
                                    trB[:, 0:ntr * P]))
                hqT_evac = self.vfence()
                te.wait_ge(self.V.sem, hqT_evac)
                if ic2 == 0 and t > 0:
                    te.wait_ge(self.V.sem, part_done[t - 1])
                for nb_, bank in enumerate(("oa", "ob")):
                    for kb in range(ntr):
                        kbg = ic2 * 4 + kb
                        inst2 = nc.tensor.matmul(
                            ps[bank],
                            self.hqT[:, icp * INNER + kb * P:
                                     icp * INNER + (kb + 1) * P],
                            self.downT[:, kbg * D + nb_ * INNER:
                                       kbg * D + (nb_ + 1) * INNER],
                            start=(ic2 == 0 and kb == 0),
                            stop=(ic2 == NCH - 1 and kb == ntr - 1))
                T.inc(inst2)

            par = t % 2
            xt = self.xf[:, par * D:(par + 1) * D]
            for ic in range(NCH):
                w = INNER if ic < 5 else (IFF - 5 * INNER)
                c0 = ic * INNER
                icpar = ic % 2
                gb, ub = (("q", "k") if icpar == 0 else ("v", "r"))
                if ic + 1 < NCH:
                    emit_gu(ic + 1, nlag=1)
                tgu = tgu_d[ic]
                s.wait_ge(self.T.sem, tgu)
                if len(hm_hist) >= 2:
                    s.wait_ge(self.V.sem, hm_hist[-2])
                sg = self.sg_sb[:, icpar * INNER: icpar * INNER + w]
                S.inc(s.activation(sg, ps[gb][:, 0:w], AF.Silu))
                silu_hist.append(self.S.n)
                hc = self.hch[:, icpar * INNER: icpar * INNER + w]
                v.wait_ge(self.T.sem, tgu)
                v.wait_ge(self.S.sem, silu_hist[-1])
                if ic == 0 and t >= 1:
                    sP.wait(v, p_dma[t - 1])
                V.inc(v.tensor_tensor(hc, ps[ub][:, 0:w], sg, AL.mult))
                hm_hist.append(self.V.n)
                V.wait(s, hm_hist[-1])
                tok = self.emit_qhead(hc, w)
                flush_pending()
                pend = (tok, icpar, w, ic)
            if t + 1 < NT:
                v_xn = emit_stats_xn(t + 1)
                V.wait(s, v_xn)
                tokXQ = self.emit_qhead(self.xo, D)
                xq_s_hi_t = tokXQ["s_hi"]
            s_hi5 = pend[0]["s_hi"]
            flush_pending()
            tdown = self.T.n
            v.wait_ge(self.T.sem, tdown)
            v.wait_ge(self.S.sem, s_hi5)
            V.inc(v.tensor_tensor(self.hch[:, 0:INNER], ps["oa"],
                                  xt[:, 0:INNER], AL.add))
            V.inc(v.tensor_tensor(self.hch[:, INNER:D], ps["ob"],
                                  xt[:, INNER:D], AL.add))
            part_done[t] = self.V.n
            x_users[t] = self.V.n
            sy.wait_ge(self.V.sem, part_done[t])
            sP.inc(sy.dma_start(out=self.part_h[t * P:(t + 1) * P, :],
                                in_=self.hch[:, 0:D]))
            p_dma[t] = self.sP.n
            issue_x_dma(t + 2)
            if t + 1 < NT:
                xq_s_hi = xq_s_hi_t
                self.T.wait(gp, tmx_prev)
                gq = self.emit_qtail(tokXQ, self.xq)
                emit_trax(gq, t + 1)
                emit_gu(0, nlag=2)

    def emit_A(self):
        nc = self.nc
        v, s, te, gp, sy = nc.vector, nc.scalar, nc.tensor, nc.gpsimd, nc.sync
        V, S, T, G, sX, sW = self.V, self.S, self.T, self.G, self.sX, self.sW
        ps = self.ps
        trA = ps["tA"].bitcast(BF16)
        trB = ps["tB"].bitcast(BF16)
        for nm in ("qT", "kT", "vaug", "wqkv_b", "lnw_rep", "lnb_rep", "xn",
                   "sqs2", "xn2b", "xn2T", "qkb", "sqs", "wkstg"):
            self.sb(nm)
        sm = self.smalls

        ones_ap = bass.AP(self._handles["vaug"], 64, [[520 * NT, P], [520, NT], [65, 8]])
        G.inc(gp.memset(ones_ap, 1.0))
        sX.inc(sy.dma_start(out=self.lnw_rep,
                            in_=self.lnw_h.rearrange("(o d) -> o d", o=1).broadcast_to([P, D])))
        sX.inc(sy.dma_start(out=self.lnb_rep,
                            in_=self.lnb_h.rearrange("(o d) -> o d", o=1).broadcast_to([P, D])))
        for kc in range(8):
            for hh in range(2):
                idx = kc * 2 + hh
                par = idx % 2
                stg = self.wkstg[:, par * 768:(par + 1) * 768]
                if idx >= 2:
                    sy.wait_ge(V.sem, self._wk_v[idx - 2])
                sW.inc(sy.dma_start(out=stg, in_=self.wqkv_h[kc * P:(kc + 1) * P,
                                                             hh * 768:(hh + 1) * 768]))
                sW.wait(v)
                if idx == 0:
                    G.wait(v, self.dma_echo(sW, self.sW.n))
                self._wk_v = getattr(self, "_wk_v", [0] * 16)
                V.inc(v.tensor_copy(
                    self.wqkv_b[:, kc * 1536 + hh * 768: kc * 1536 + (hh + 1) * 768],
                    stg))
                self._wk_v[idx] = self.V.n
        for idx in (16, 17):
            kc, hh = 0, idx - 16
            par = idx % 2
            stg = self.wkstg[:, par * 768:(par + 1) * 768]
            sy.wait_ge(V.sem, self._wk_v[idx - 2])
            sW.inc(sy.dma_start(out=stg, in_=self.wqkv_h[kc * P:(kc + 1) * P,
                                                         hh * 768:(hh + 1) * 768]))
            sW.wait(v)
            V.inc(v.tensor_copy(
                self.wqkv_b[:, kc * 1536 + hh * 768: kc * 1536 + (hh + 1) * 768],
                stg))
        x_users = {}
        s_users = {}
        evac_qkv = {}
        evac_tr = {}
        c1_v_final = self.V.n
        c1_s_final = self.S.n
        c1_g_final = self.G.n
        for t in range(NT):
            par = t % 2
            xt = self.xf[:, par * D:(par + 1) * D]
            if t >= 2:
                sy.wait_ge(V.sem, x_users.get(t - 2, 0))
                sy.wait_ge(S.sem, s_users.get(t - 2, 0))
            else:
                sy.wait_ge(V.sem, c1_v_final)
                sy.wait_ge(S.sem, c1_s_final)
                sy.wait_ge(G.sem, c1_g_final)
            sX.inc(sy.dma_start(out=xt, in_=self.x_h[t * P:(t + 1) * P, :]))
            sX.wait(v)
            sX.wait(s)
            V.inc(v.tensor_scalar(self.xn, xt, self.mu_all[:, t:t + 1],
                                  self.rstd_all[:, t:t + 1], AL.subtract, AL.mult))
            x_users[t] = self.V.n
            V.wait(v)
            V.wait(s)
            V.inc(v.tensor_reduce(sm[:, 8:9], self.xn, AX.X, AL.add))
            S.inc(s.activation(self.sqs2, self.xn, AF.Square, accum_out=sm[:, 9:10]))
            s_users[t] = self.sfence()
            V.wait(v)
            V.inc(v.tensor_scalar(sm[:, 10:11], sm[:, 8:9], 1.0 / D, None, AL.mult))
            S.wait(v)
            V.inc(v.tensor_scalar(sm[:, 11:12], sm[:, 9:10], 1.0 / D, None, AL.mult))
            V.wait(v)
            V.inc(v.tensor_tensor(sm[:, 12:13], sm[:, 10:11], sm[:, 10:11], AL.mult))
            V.wait(v)
            V.inc(v.tensor_sub(sm[:, 11:12], sm[:, 11:12], sm[:, 12:13]))
            self.vfence()
            V.wait(s)
            S.inc(s.activation(sm[:, 12:13], sm[:, 11:12], AF.Ln, bias=LN_EPS))
            S.wait(s)
            S.inc(s.activation(sm[:, 13:14], sm[:, 12:13], AF.Exp, scale=-0.5))
            self.sfence()
            S.wait(v)
            V.inc(v.tensor_scalar(self.sqs2, self.xn, sm[:, 10:11], sm[:, 13:14],
                                  AL.subtract, AL.mult))
            V.wait(v)
            V.inc(v.tensor_tensor(self.sqs2, self.sqs2, self.lnw_rep, AL.mult))
            V.wait(v)
            V.inc(v.tensor_tensor(self.xn2b, self.sqs2, self.lnb_rep, AL.add))
            vm_xn2b = self.vfence()
            te.wait_ge(self.V.sem, vm_xn2b)
            for j in range(8):
                inst = te.transpose(trA[:, j * P:(j + 1) * P],
                                    self.xn2b[:, j * P:(j + 1) * P], self.ident)
            T.inc(inst)
            tm1 = self.T.n
            v.wait_ge(self.T.sem, tm1)
            V.inc(v.tensor_copy(self.xn2T, trA))
            vm_xn2T = self.vfence()
            banks = ("q", "k", "v") if par == 0 else ("oa", "ob", "r")
            te.wait_ge(self.V.sem, vm_xn2T)
            if t >= 2:
                te.wait_ge(self.V.sem, evac_qkv[t - 2])
            for bank_i, psn in enumerate(banks):
                for kc in range(8):
                    inst = nc.tensor.matmul(
                        ps[psn], self.xn2T[:, kc * P:(kc + 1) * P],
                        self.wqkv_b[:, kc * 1536 + bank_i * INNER:
                                    kc * 1536 + (bank_i + 1) * INNER],
                        start=(kc == 0), stop=(kc == 7))
            T.inc(inst)
            tqkv = self.T.n
            s.wait_ge(self.T.sem, tqkv)
            v.wait_ge(self.T.sem, tqkv)
            V.inc(v.tensor_copy(self.sqs, ps[banks[0]]))
            V.wait(v)
            V.inc(v.tensor_tensor(self.sqs, self.sqs, self.sqs, AL.mult))
            V.wait(v)
            V.inc(v.tensor_reduce(sm[:, 16:24],
                                  self.sqs.rearrange("p (h c) -> p h c", c=DH),
                                  AX.X, AL.add))
            self.vfence()
            V.wait(s)
            S.inc(s.activation(sm[:, 24:32], sm[:, 16:24], AF.Ln, bias=L2_EPS))
            S.wait(s)
            S.inc(s.activation(sm[:, 32:40], sm[:, 24:32], AF.Exp, scale=-0.5, bias=LOG8))
            self.sfence()
            S.wait(v)
            V.inc(v.tensor_tensor(self.qkb[:, 0:INNER].rearrange("p (h c) -> p h c", c=DH),
                                  ps[banks[0]].rearrange("p (h c) -> p h c", c=DH),
                                  sm[:, 32:40].broadcast_to([P, H, DH]), AL.mult))
            V.inc(v.tensor_copy(self.sqs, ps[banks[1]]))
            V.wait(v)
            V.inc(v.tensor_tensor(self.sqs, self.sqs, self.sqs, AL.mult))
            V.wait(v)
            V.inc(v.tensor_reduce(sm[:, 16:24],
                                  self.sqs.rearrange("p (h c) -> p h c", c=DH),
                                  AX.X, AL.add))
            self.vfence()
            V.wait(s)
            S.inc(s.activation(sm[:, 24:32], sm[:, 16:24], AF.Ln, bias=L2_EPS))
            S.wait(s)
            S.inc(s.activation(sm[:, 32:40], sm[:, 24:32], AF.Exp, scale=-0.5))
            s_users[t] = self.sfence()
            S.wait(v)
            V.inc(v.tensor_tensor(self.qkb[:, INNER:2 * INNER]
                                  .rearrange("p (h c) -> p h c", c=DH),
                                  ps[banks[1]].rearrange("p (h c) -> p h c", c=DH),
                                  sm[:, 32:40].broadcast_to([P, H, DH]), AL.mult))
            vdst = bass.AP(self._handles["vaug"], t * 520,
                           [[520 * NT, P], [65, 8], [1, DH]])
            V.inc(v.tensor_copy(vdst, ps[banks[2]].rearrange("p (h c) -> p h c", c=DH)))
            evac_qkv[t] = self.V.n
            vm_qkb = self.vfence()
            te.wait_ge(self.V.sem, vm_qkb)
            if t >= 1:
                te.wait_ge(self.V.sem, evac_tr[t - 1])
            for j in range(8):
                inst = te.transpose(trB[:, j * P:(j + 1) * P],
                                    self.qkb[:, j * P:(j + 1) * P], self.ident)
            T.inc(inst)
            tm2 = self.T.n
            v.wait_ge(self.T.sem, tm2)
            qdst = bass.AP(self._handles["qT"], t * P, [[4 * SEQ, P], [SEQ, 4], [1, P]])
            kdst = bass.AP(self._handles["kT"], t * P, [[4 * SEQ, P], [SEQ, 4], [1, P]])
            V.inc(v.tensor_copy(qdst, trB[:, 0:INNER].rearrange("p (c q) -> p c q", q=P)))
            V.inc(v.tensor_copy(kdst, trB[:, INNER:2 * INNER]
                                .rearrange("p (c q) -> p c q", q=P)))
            evac_tr[t] = self.V.n
            x_users[t] = self.V.n

    def emit_B(self):
        nc = self.nc
        v, s, te, gp, sy = nc.vector, nc.scalar, nc.tensor, nc.gpsimd, nc.sync
        V, S, T, G, sO = self.V, self.S, self.T, self.G, self.sO
        ps = self.ps
        for nm in ("oT_sb", "pt_a0", "pt_a1", "pt_b0", "pt_b1", "rsrow", "rsln",
                   "rrep", "oTs", "maskb"):
            self.sb(nm)
        self.T.wait(gp)
        self.V.wait(gp)
        self.S.wait(gp)
        G.inc(gp.memset(self.maskb, 1.0))
        G.wait(gp)
        G.inc(gp.affine_select(out=self.maskb, in_=self.maskb, compare_op=AL.is_ge,
                               fill=0.0, base=-INNER, pattern=[[1, 2 * INNER]],
                               channel_multiplier=-1))
        G.wait(v)
        G.wait(te)
        self.vfence()
        V.wait(te)
        S.wait(te)
        vm_b0 = self.V.n
        sa_banks = [ps["q"], ps["k"]]
        sb_banks = [ps["v"], ps["r"]]
        pta = [self.pt_a0, self.pt_a1]
        ptb = [self.pt_b0, self.pt_b1]
        trR = [ps["tA"], ps["tB"]]

        vm_epi = vm_b0
        vm_rrep = 0
        pend_epi = None
        self._b_pend_odd = None
        self._b_last_av = 0

        def emit_epilogue():
            nonlocal vm_epi, vm_rrep, pend_epi
            if pend_epi is None:
                return
            tmav, coff = pend_epi
            pend_epi = None
            v.wait_ge(self.T.sem, tmav)
            V.inc(v.tensor_copy(self.rsrow[64:65, 0:INNER], ps["oa"][64:65, :]))
            V.inc(v.tensor_copy(self.rsrow[64:65, INNER:], ps["ob"][64:65, :]))
            self.vfence()
            V.wait(s)
            S.inc(s.activation(self.rsln[64:65, :], self.rsrow[64:65, :], AF.Ln))
            S.wait(s)
            S.inc(s.activation(self.rsrow[64:65, :], self.rsln[64:65, :],
                               AF.Exp, scale=-1.0))
            sm_rec = self.sfence()
            te.wait_ge(self.S.sem, sm_rec)
            if vm_rrep:
                te.wait_ge(self.V.sem, vm_rrep)
            nc.tensor.matmul(trR[0][0:64, :], self.ones65[64:65, 0:64],
                             self.rsrow[64:65, 0:INNER], start=True, stop=True)
            inst = nc.tensor.matmul(trR[1][0:64, :], self.ones65[64:65, 0:64],
                                    self.rsrow[64:65, INNER:], start=True, stop=True)
            T.inc(inst)
            tm_rep = self.T.n
            v.wait_ge(self.T.sem, tm_rep)
            V.inc(v.tensor_copy(self.rrep[:, 0:INNER], trR[0][0:64, :]))
            V.inc(v.tensor_copy(self.rrep[:, INNER:], trR[1][0:64, :]))
            vm_rrep = self.V.n
            V.wait(v)
            V.inc(v.tensor_tensor(
                self.oT_sb[0:64, coff:coff + INNER], ps["oa"][0:64, :],
                self.rrep[:, 0:INNER], AL.mult))
            v.wait_ge(self.sO.sem, self.sO.n)
            V.inc(v.tensor_tensor(self.oTs[:, 0:INNER], ps["ob"][0:64, :],
                                  self.rrep[:, INNER:], AL.mult))
            vm_epi = self.V.n
            self._b_pend_odd = (coff, vm_epi)
        for hp in range(4):
            for ci in range(8):
                nj = 4 * ci + 4
                qs = self.qT[:, hp * SEQ + ci * INNER: hp * SEQ + (ci + 1) * INNER]
                tm_sp = {}
                sm_exp = {}
                vm_msk = {}
                tm_av = {}

                def emit_spair(j):
                    par = j % 2
                    ks = self.kT[:, hp * SEQ + j * P: hp * SEQ + (j + 1) * P]
                    if j >= 2:
                        te.wait_ge(self.S.sem, sm_exp[j - 2])
                    elif (hp, ci) != (0, 0):
                        te.wait_ge(self.S.sem, self._b_last_exp)
                    nc.tensor.matmul(sa_banks[par], ks[0:64, :], qs[0:64, :],
                                     start=True, stop=True, tile_position=(0, 0))
                    inst = nc.tensor.matmul(sb_banks[par], ks[64:128, :], qs[64:128, :],
                                            start=True, stop=True, tile_position=(64, 0))
                    T.inc(inst)
                    tm_sp[j] = self.T.n
                    s.wait_ge(self.T.sem, tm_sp[j])
                    if j >= 2:
                        s.wait_ge(self.T.sem, tm_av[j - 2])
                    elif self._b_last_av:
                        s.wait_ge(self.T.sem, self._b_last_av)
                    s.activation(pta[par], sa_banks[par], AF.Exp)
                    S.inc(s.activation(ptb[par], sb_banks[par], AF.Exp))
                    sm_exp[j] = self.sfence() if j == nj - 1 else self.S.n
                    if j >= 4 * ci:
                        r = j * P - ci * INNER
                        msl = self.maskb[:, INNER - r: 2 * INNER - r]
                        v.wait_ge(self.S.sem, sm_exp[j])
                        v.tensor_tensor(pta[par], pta[par], msl, AL.mult)
                        V.inc(v.tensor_tensor(ptb[par], ptb[par], msl, AL.mult))
                        vm_msk[j] = self.vfence()

                def emit_av(j):
                    par = j % 2
                    if j in vm_msk:
                        te.wait_ge(self.V.sem, vm_msk[j])
                    else:
                        te.wait_ge(self.S.sem, sm_exp[j])
                    if j == 0:
                        te.wait_ge(self.V.sem, vm_epi)
                    va = self.vaug[:, j * 520 + (2 * hp) * 65:
                                   j * 520 + (2 * hp) * 65 + 65]
                    vb = self.vaug[:, j * 520 + (2 * hp + 1) * 65:
                                   j * 520 + (2 * hp + 1) * 65 + 65]
                    nc.tensor.matmul(ps["oa"][0:65, :], va, pta[par],
                                     start=(j == 0), stop=(j == nj - 1))
                    inst = nc.tensor.matmul(ps["ob"][0:65, :], vb, ptb[par],
                                            start=(j == 0), stop=(j == nj - 1))
                    T.inc(inst)
                    tm_av[j] = self.T.n

                emit_spair(0)
                if self._b_pend_odd is not None:
                    coff_p, vm_p = self._b_pend_odd
                    self._b_pend_odd = None
                    sy.wait_ge(self.V.sem, vm_p)
                    sy.wait_ge(self.T.sem, tm_sp[0])
                    sO.inc(sy.dma_start(
                        out=self.oT_sb[64:128, coff_p:coff_p + INNER],
                        in_=self.oTs[:, 0:INNER]))
                emit_epilogue()
                for j in range(nj):
                    if j + 1 < nj:
                        emit_spair(j + 1)
                    emit_av(j)
                self._b_last_exp = sm_exp[nj - 1]
                self._b_last_av = tm_av[nj - 1]
                pend_epi = (tm_av[nj - 1], hp * SEQ + ci * INNER)
        if self._b_pend_odd is not None:
            coff_p, vm_p = self._b_pend_odd
            self._b_pend_odd = None
            gp.wait_ge(self.V.sem, vm_p)
            G.inc(gp.memset(self.smalls[:, 61:62], 0.0))
            sy.wait_ge(G.sem, self.G.n)
            sO.inc(sy.dma_start(out=self.oT_sb[64:128, coff_p:coff_p + INNER],
                                in_=self.oTs[:, 0:INNER]))
        emit_epilogue()
        coff_p, vm_p = self._b_pend_odd
        self._b_pend_odd = None
        gp.wait_ge(self.V.sem, vm_p)
        G.inc(gp.memset(self.smalls[:, 61:62], 0.0))
        sy.wait_ge(G.sem, self.G.n)
        sO.inc(sy.dma_start(out=self.oT_sb[64:128, coff_p:coff_p + INNER],
                            in_=self.oTs[:, 0:INNER]))

    def emit_C2(self):
        nc = self.nc
        v, s, te, gp, sy = nc.vector, nc.scalar, nc.tensor, nc.gpsimd, nc.sync
        V, S, T, sP, sY, sW = self.V, self.S, self.T, self.sP, self.sY, self.sW
        ps = self.ps
        for nm in ("pbuf", "obuf", "wout_b"):
            self.sb(nm)
        self.V.wait(sy)
        self.S.wait(sy)
        self.T.wait(sy)
        for kc in range(4):
            for hh in range(2):
                idx = kc * 2 + hh
                par = idx % 2
                stg = self.obuf[:, par * D: par * D + INNER]
                if idx >= 2:
                    sy.wait_ge(V.sem, self._wo_v[idx - 2])
                sW.inc(sy.dma_start(out=stg, in_=self.wout_h[kc * P:(kc + 1) * P,
                                                             hh * 512:(hh + 1) * 512]))
                sW.wait(v)
                if idx == 0:
                    self.G.wait(v, self.dma_echo(sW, self.sW.n))
                self._wo_v = getattr(self, "_wo_v", [0] * 8)
                V.inc(v.tensor_copy(
                    self.wout_b[:, kc * D + hh * 512: kc * D + (hh + 1) * 512], stg))
                self._wo_v[idx] = self.V.n
        for idx in (8, 9):
            kc, hh = 0, idx - 8
            par = idx % 2
            stg = self.obuf[:, par * D: par * D + INNER]
            sy.wait_ge(V.sem, self._wo_v[idx - 2])
            sW.inc(sy.dma_start(out=stg, in_=self.wout_h[kc * P:(kc + 1) * P,
                                                         hh * 512:(hh + 1) * 512]))
            sW.wait(v)
            V.inc(v.tensor_copy(
                self.wout_b[:, kc * D + hh * 512: kc * D + (hh + 1) * 512], stg))
        self.sO.wait(sy)
        self.sO.wait(te)
        self.vfence()
        V.wait(te)
        out_done = {}
        res_done = {}
        for t in range(NT):
            par = t % 2
            pt_ = self.pbuf[:, par * D:(par + 1) * D]
            ot_ = self.obuf[:, par * D:(par + 1) * D]
            if t >= 2:
                sy.wait_ge(V.sem, res_done[t - 2])
            sP.inc(sy.dma_start(out=pt_, in_=self.part_h[t * P:(t + 1) * P, :]))
            bpair = ("oa", "ob") if par == 0 else ("q", "k")
            if t >= 2:
                te.wait_ge(self.V.sem, res_done[t - 2])
            for nb, bank in enumerate(bpair):
                for cb in range(4):
                    inst = nc.tensor.matmul(
                        ps[bank], self.oT_sb[:, cb * SEQ + t * P: cb * SEQ + (t + 1) * P],
                        self.wout_b[:, cb * D + nb * INNER: cb * D + (nb + 1) * INNER],
                        start=(cb == 0), stop=(cb == 3))
            T.inc(inst)
            tm = self.T.n
            v.wait_ge(self.T.sem, tm)
            sP.wait(v)
            v.wait_ge(self.sY.sem, self.sY.n)
            if t >= 1:
                sy.wait_ge(self.V.sem, res_done[t - 1])
                sy.wait_ge(self.T.sem, tm)
                po_ = self.obuf[:, ((t - 1) % 2) * D:((t - 1) % 2 + 1) * D]
                sY.inc(sy.dma_start(out=self.y_h[(t - 1) * P:t * P, :], in_=po_))
                out_done[t - 1] = self.sY.n
            V.inc(v.tensor_tensor(ot_[:, 0:INNER], ps[bpair[0]], pt_[:, 0:INNER], AL.add))
            V.inc(v.tensor_tensor(ot_[:, INNER:D], ps[bpair[1]], pt_[:, INNER:D], AL.add))
            res_done[t] = self.V.n
        t = NT - 1
        gp = nc.gpsimd
        gp.wait_ge(self.V.sem, res_done[t])
        self.G.inc(gp.memset(self.smalls[:, 61:62], 0.0))
        sy.wait_ge(self.G.sem, self.G.n)
        sY.inc(sy.dma_start(out=self.y_h[t * P:(t + 1) * P, :],
                            in_=self.obuf[:, (t % 2) * D:(t % 2 + 1) * D]))
        out_done[t] = self.sY.n

    def emit_tail(self):
        gp = self.nc.gpsimd
        for c in (self.sY, self.sP, self.sO, self.sX, self.sW,
                  self.T, self.V, self.S, self.G):
            c.wait(gp)


def build():
    return MK2().build()


def kernel(L0_attn_x, L0_attn_ln_w, L0_attn_ln_b, L0_attn_w_qkv, L0_attn_w_out,
           L0_ffn_w_gate, L0_ffn_w_up, L0_ffn_w_down):
    from concourse.bass_utils import run_bass_kernel_spmd
    x = np.ascontiguousarray(np.asarray(L0_attn_x, dtype=np.float32))
    args = dict(ln_w=np.ascontiguousarray(np.asarray(L0_attn_ln_w, dtype=np.float32)),
                ln_b=np.ascontiguousarray(np.asarray(L0_attn_ln_b, dtype=np.float32)),
                w_qkv=np.ascontiguousarray(np.asarray(L0_attn_w_qkv, dtype=np.float32)),
                w_out=np.ascontiguousarray(np.asarray(L0_attn_w_out, dtype=np.float32)),
                w_gate=np.ascontiguousarray(np.asarray(L0_ffn_w_gate, dtype=np.float32)),
                w_up=np.ascontiguousarray(np.asarray(L0_ffn_w_up, dtype=np.float32)),
                w_down=np.ascontiguousarray(np.asarray(L0_ffn_w_down, dtype=np.float32)))
    nc = build()
    in_maps = [dict(x=np.ascontiguousarray(x[b]), **args) for b in range(8)]
    res = run_bass_kernel_spmd(nc, in_maps, core_ids=list(range(8)))
    return np.stack([np.asarray(res.results[b]["y"]) for b in range(8)], axis=0)


if __name__ == "__main__":
    nc = build()
    print("instructions:", sum(1 for _ in nc.all_instructions()))


# version 13 (final)
# speedup vs baseline: 1.4886x; 1.0394x over previous
import numpy as np

import concourse.bass as bass
import concourse.mybir as mybir

F32 = mybir.dt.float32
BF16 = mybir.dt.bfloat16
AL = mybir.AluOpType
AF = mybir.ActivationFunctionType
AX = mybir.AxisListType

P = 128
D = 1024
SEQ = 4096
NT = SEQ // P
H = 8
DH = 64
INNER = 512
IFF = 2816
QB = 16
LN_EPS = 1e-5
L2_EPS = 1e-12
LOG8 = float(np.log(np.float64(8.0)))
ARENA = 24576
PART_SZ = 229376


class Ctr:
    def __init__(self, nc, name, step=1):
        self._cm = nc.semaphore(name)
        self.sem = self._cm.__enter__()
        self.n = 0
        self.step = step

    def inc(self, inst, k=1):
        inst.then_inc(self.sem, k * self.step)
        self.n += k * self.step
        return self.n

    def wait(self, eng, n=None):
        v = self.n if n is None else n
        if v > 0:
            eng.wait_ge(self.sem, v)


def _layout():
    L = {}

    def put(name, off, shape, dtype):
        L[name] = (off, shape, dtype)
        nb = int(np.prod(shape[1:])) * (2 if dtype == BF16 else 4)
        return off + (nb + 31) // 32 * 32

    o = ARENA
    o = put("ident", o, [P, P], BF16)
    o = put("ones65", o, [P, 64], F32)
    o = put("mu_all", o, [P, NT], F32)
    o = put("rstd_all", o, [P, NT], F32)
    o = put("smalls", o, [P, 64], F32)
    o = put("consts", o, [P, 8], F32)
    r1 = o
    e = put("gateT", r1, [P, 8 * IFF], BF16)
    e = put("upT", e, [P, 8 * IFF], BF16)
    e = put("qT", r1, [P, 4 * SEQ], BF16)
    e = put("kT", e, [P, 4 * SEQ], BF16)
    r1_end = put("vaug", e, [P, NT * 520], BF16)
    r2 = r1_end
    e = put("downT", r2, [P, 22 * D], BF16)
    e = put("wqkv_b", r2, [P, 8 * 3 * INNER], BF16)
    e = put("lnw_rep", e, [P, D], F32)
    e = put("lnb_rep", e, [P, D], F32)
    assert e <= r2 + 45056
    e = put("oT_sb", r2, [P, 4 * SEQ], BF16)
    e = put("wout_b", e, [P, 4 * D], BF16)
    assert e <= r2 + 45056
    r3 = r2 + 45056
    e = put("xf", r3, [P, 2 * D], F32)
    e = put("xo", e, [P, D], F32)
    e = put("xq", e, [P, D], BF16)
    e = put("xqT", e, [P, 8 * P], BF16)
    e = put("hq", e, [P, 2 * INNER], BF16)
    e = put("hqT", e, [P, 2 * INNER], BF16)
    wr = e
    e = put("wstg", e, [P, 2 * 1408], F32)
    e2 = put("sg_sb", wr, [P, 2 * INNER], F32)
    e2 = put("hch", e2, [P, 2 * INNER], F32)
    assert e2 <= e
    e = put("qs_m", e, [P, 2 * 64], F32)
    e = put("qs_s", e, [P, 2 * 64], F32)
    e = put("qs_r", e, [P, 2 * 64], F32)
    e = put("qs_a", e, [P, 2 * D], F32)
    e = put("qs_u", e, [P, 2 * D], F32)
    e = put("qs_g", e, [P, 2 * D], BF16)
    e = put("wqb", e, [P, 2 * D], BF16)
    c1_end = e
    assert c1_end <= PART_SZ - 288, c1_end
    a0 = L["xo"][0]
    e = put("xn", a0, [P, D], F32)
    e = put("sqs2", e, [P, D], F32)
    e = put("xn2b", e, [P, D], BF16)
    e = put("xn2T", e, [P, 8 * P], BF16)
    e = put("qkb", e, [P, 2 * INNER], BF16)
    e = put("sqs", e, [P, INNER], F32)
    e = put("wkstg", e, [P, 2 * 768], F32)
    assert e <= c1_end
    b0 = r3
    e = put("pt_a0", b0, [P, INNER], BF16)
    e = put("pt_a1", e, [P, INNER], BF16)
    e = put("pt_b0", e, [P, INNER], BF16)
    e = put("pt_b1", e, [P, INNER], BF16)
    e = put("rsrow", e, [P, 2 * INNER], F32)
    e = put("rsln", e, [P, 2 * INNER], F32)
    e = put("rrep", e, [64, 2 * INNER], F32)
    e = put("oTs", e, [64, INNER], BF16)
    e = put("maskb", e, [P, 2 * INNER], BF16)
    assert e <= c1_end
    e = put("pbuf", b0, [P, 2 * D], F32)
    e = put("obuf", e, [P, 2 * D], F32)
    assert e <= c1_end
    return L


class MK2:
    def __init__(self):
        nc = bass.Bass("TRN2", target_bir_lowering=False,
                       detect_race_conditions=False)
        self.nc = nc
        self.x_h = nc.dram_tensor("x", [SEQ, D], F32, kind="ExternalInput").ap()
        self.lnw_h = nc.dram_tensor("ln_w", [D], F32, kind="ExternalInput").ap()
        self.lnb_h = nc.dram_tensor("ln_b", [D], F32, kind="ExternalInput").ap()
        self.wqkv_h = nc.dram_tensor("w_qkv", [D, 3 * INNER], F32, kind="ExternalInput").ap()
        self.wout_h = nc.dram_tensor("w_out", [INNER, D], F32, kind="ExternalInput").ap()
        self.wgate_h = nc.dram_tensor("w_gate", [IFF, D], F32, kind="ExternalInput").ap()
        self.wup_h = nc.dram_tensor("w_up", [IFF, D], F32, kind="ExternalInput").ap()
        self.wdown_h = nc.dram_tensor("w_down", [D, IFF], F32, kind="ExternalInput").ap()
        self.y_h = nc.dram_tensor("y", [SEQ, D], F32, kind="ExternalOutput").ap()
        self.part_h = nc.dram_tensor("part_scr", [SEQ, D], F32, kind="Internal").ap()
        self._handles = {}
        self._L = _layout()

    def sb(self, name):
        off, shape, dt = self._L[name]
        h = self.nc.alloc_sbuf_tensor_at(name, shape, dt, offset=off)
        self._handles[name] = h
        setattr(self, name, h.ap())
        return h.ap()

    def vfence(self):
        v = self.nc.vector
        self.V.wait(v)
        self.V.inc(v.memset(self.smalls[:, 62:63], 0.0))
        return self.V.n

    def sfence(self):
        s = self.nc.scalar
        self.S.wait(s)
        self.S.inc(s.activation(self.smalls[:, 63:64], self.smalls[:, 62:63],
                                AF.Abs))
        return self.S.n

    def dma_echo(self, ctr, n):
        gp = self.nc.gpsimd
        ctr.wait(gp, n)
        self.G.inc(gp.memset(self.smalls[:, 61:62], 0.0))
        return self.G.n

    def build(self):
        nc = self.nc
        self.sX = Ctr(nc, "c_sX", 16)
        self.sW = Ctr(nc, "c_sW", 16)
        self.sP = Ctr(nc, "c_sP", 16)
        self.sO = Ctr(nc, "c_sO", 16)
        self.sY = Ctr(nc, "c_sY", 16)
        self.T = Ctr(nc, "c_T")
        self.V = Ctr(nc, "c_V")
        self.S = Ctr(nc, "c_S")
        self.G = Ctr(nc, "c_G")

        with (nc.psum_tensor("pq", [P, INNER], F32) as pq,
              nc.psum_tensor("pk", [P, INNER], F32) as pk,
              nc.psum_tensor("pv", [P, INNER], F32) as pv,
              nc.psum_tensor("pr", [P, INNER], F32) as pr,
              nc.psum_tensor("poa", [P, INNER], F32) as poa,
              nc.psum_tensor("pob", [P, INNER], F32) as pob,
              nc.psum_tensor("ptA", [P, INNER], F32) as ptA,
              nc.psum_tensor("ptB", [P, INNER], F32) as ptB,
              nc.Block() as block):
            self.ps = dict(q=pq[:], k=pk[:], v=pv[:], r=pr[:],
                           oa=poa[:], ob=pob[:], tA=ptA[:], tB=ptB[:])

            @block.gpsimd
            def _(g):
                self.emit_setup()
                self.emit_prep()
                self.emit_C1()
                self.emit_A()
                self.emit_B()
                self.emit_C2()
                self.emit_tail()
        return nc

    def emit_setup(self):
        nc = self.nc
        v, s, gp, sy = nc.vector, nc.scalar, nc.gpsimd, nc.sync
        V, S, G = self.V, self.S, self.G
        for nm in ("ident", "ones65", "mu_all", "rstd_all", "smalls", "consts",
                   "gateT", "upT", "downT", "xf", "xo", "xq", "xqT", "hq", "hqT",
                   "sg_sb", "hch", "wstg", "qs_m", "qs_s", "qs_r", "qs_a", "qs_u",
                   "qs_g", "wqb"):
            self.sb(nm)
        self.qscr = dict(m=self.qs_m, sc2=self.qs_s, r6=self.qs_r,
                         a=self.qs_a, u=self.qs_u, sgn=self.qs_g)
        for i, val in enumerate([LN_EPS, L2_EPS, LOG8, L2_EPS / 64.0]):
            G.inc(gp.memset(self.consts[:, i:i + 1], val))
            nc.const_aps.register(F32, val, self.consts[:, i:i + 1])
        G.inc(gp.memset(self.ident, 0.0))
        G.wait(gp)
        G.inc(gp.affine_select(out=self.ident, in_=self.ident, compare_op=AL.not_equal,
                               fill=1.0, base=0, pattern=[[-1, P]], channel_multiplier=1))
        V.inc(v.memset(self.ones65, 1.0))
        G.wait(s)
        G.wait(v)
        G.wait(nc.tensor)

    def emit_qhead(self, x_ap, F):
        nc = self.nc
        v, s, gp = nc.vector, nc.scalar, nc.gpsimd
        V, S, G = self.V, self.S, self.G
        pp = self._qn & 1
        self._qn += 1
        r = self._qrec[pp]
        nb = F // QB
        o64, oD = pp * 64, pp * D
        m = self.qs_m[:, o64:o64 + nb]
        sc2 = self.qs_s[:, o64:o64 + nb]
        r6 = self.qs_r[:, o64:o64 + nb]
        a = self.qs_a[:, oD:oD + F]
        x3 = x_ap.rearrange("p (b s) -> p b s", s=QB)
        a3 = a.rearrange("p (b s) -> p b s", s=QB)
        V.wait(s, r["v"])
        G.wait(s, r["g"])
        s.activation(a, x_ap, AF.Abs)
        S.inc(s.activation(self.qs_g[:, oD:oD + F], x_ap, AF.Sign))
        s_hi = S.n
        G.wait(v, r["g"])
        V.wait(v)
        V.inc(v.tensor_reduce(m, x3, AX.X, AL.max, apply_absolute_value=True))
        V.wait(v)
        v.tensor_scalar(sc2, m, 1.0 / 12.0, 5e-13, AL.mult, AL.max)
        V.inc(v.tensor_scalar(r6, m, 1.0 / 6.0, 1e-12, AL.mult, AL.max))
        V.wait(v)
        V.inc(v.reciprocal(r6, r6))
        v_r6 = V.n
        S.wait(gp, s_hi)
        V.wait(gp, v_r6)
        G.inc(gp.tensor_tensor(a3, a3, r6.broadcast_to([P, nb, QB]), AL.mult))
        return dict(pp=pp, F=F, s_hi=s_hi, g_am=self.G.n)

    def emit_qtail(self, tok, out_ap):
        nc = self.nc
        v, gp = nc.vector, nc.gpsimd
        V, G = self.V, self.G
        pp, F = tok["pp"], tok["F"]
        nb = F // QB
        o64, oD = pp * 64, pp * D
        sc2 = self.qs_s[:, o64:o64 + nb]
        a = self.qs_a[:, oD:oD + F]
        u = self.qs_u[:, oD:oD + F]
        sgn = self.qs_g[:, oD:oD + F]
        G.wait(v, tok["g_am"])

        def st(inst):
            V.inc(inst)
            V.wait(v)

        st(v.tensor_scalar(u, a, 5.0, 2.0, AL.is_gt, AL.mult))
        st(v.scalar_tensor_tensor(u, a, 3.5, u, AL.is_gt, AL.add))
        st(v.scalar_tensor_tensor(u, a, 2.5, u, AL.is_gt, AL.add))
        st(v.tensor_scalar(u, u, 2.0, None, AL.mult))
        st(v.scalar_tensor_tensor(u, a, 1.75, u, AL.is_gt, AL.add))
        st(v.scalar_tensor_tensor(u, a, 1.25, u, AL.is_gt, AL.add))
        st(v.scalar_tensor_tensor(u, a, 0.75, u, AL.is_gt, AL.add))
        st(v.scalar_tensor_tensor(u, a, 0.25, u, AL.is_gt, AL.add))
        v_st = V.n
        V.wait(gp, v_st)
        G.inc(gp.tensor_tensor(u, u, sgn, AL.mult))
        G.wait(gp)
        G.inc(gp.tensor_tensor(out_ap.rearrange("p (b s) -> p b s", s=QB),
                               u.rearrange("p (b s) -> p b s", s=QB),
                               sc2.broadcast_to([P, nb, QB]), AL.mult))
        self._qrec[pp] = dict(v=v_st, s=tok["s_hi"], g=self.G.n)
        return self.G.n

    def emit_prep(self):
        nc = self.nc
        v, s, te, gp, sy = nc.vector, nc.scalar, nc.tensor, nc.gpsimd, nc.sync
        V, S, T, G, sW, sX = self.V, self.S, self.T, self.G, self.sW, self.sX
        ps = self.ps
        trA = ps["tA"].bitcast(BF16)
        self._qn = 0
        self._qrec = {0: dict(v=0, s=0, g=0), 1: dict(v=0, s=0, g=0)}
        self._tr_hist = []
        self._wq_evac = 0
        jobs = []
        for name, src_h in (("gateT", self.wgate_h), ("upT", self.wup_h)):
            for rc in range(22):
                dst = bass.AP(self._handles[name], rc * P,
                              [[8 * IFF, P], [IFF, 8], [1, P]])
                jobs.append((src_h[rc * P:(rc + 1) * P, :], D, [(0, D, dst)]))
        for dc in range(8):
            for hh in range(2):
                chains = []
                for c0, w in ((0, 768), (768, 640)):
                    kb0 = hh * 11 + c0 // P
                    dst = bass.AP(self._handles["downT"], dc * P + kb0 * D,
                                  [[22 * D, P], [D, w // P], [1, P]])
                    chains.append((c0, w, dst))
                jobs.append((self.wdown_h[dc * P:(dc + 1) * P,
                                          hh * 1408:(hh + 1) * 1408],
                             1408, chains))
        jrec = {}
        swm = {}
        sm = self.smalls
        sxm = {}
        st_rec = {0: dict(v=0, s=0), 1: dict(v=0, s=0)}
        stat_users = {}

        def issue_stat_dma(t):
            if t >= NT:
                return
            if t >= 2:
                sy.wait_ge(V.sem, stat_users[t - 2][0])
                sy.wait_ge(S.sem, stat_users[t - 2][1])
            sX.inc(sy.dma_start(out=self.xf[:, (t % 2) * D:(t % 2 + 1) * D],
                                in_=self.x_h[t * P:(t + 1) * P, :]))
            sxm[t] = sX.n

        def emit_stats_tile(t):
            par = t % 2
            so = par * 8
            xt = self.xf[:, par * D:(par + 1) * D]
            r = st_rec[par]
            sX.wait(v, sxm[t])
            sX.wait(s, sxm[t])
            S.wait(v, r["s"])
            V.wait(s, r["v"])
            V.inc(v.tensor_reduce(sm[:, so:so + 1], xt, AX.X, AL.add))
            V.wait(v)
            S.inc(s.activation(self.xo, xt, AF.Square,
                               accum_out=sm[:, so + 1:so + 2]))
            s_sq = S.n
            V.inc(v.tensor_scalar(self.mu_all[:, t:t + 1], sm[:, so:so + 1],
                                  1.0 / D, None, AL.mult))
            v.wait_ge(self.S.sem, s_sq)
            V.inc(v.tensor_scalar(sm[:, so + 2:so + 3], sm[:, so + 1:so + 2],
                                  1.0 / D, None, AL.mult))
            V.wait(v)
            V.inc(v.tensor_tensor(sm[:, so + 3:so + 4], self.mu_all[:, t:t + 1],
                                  self.mu_all[:, t:t + 1], AL.mult))
            V.wait(v)
            V.inc(v.tensor_sub(sm[:, so + 2:so + 3], sm[:, so + 2:so + 3],
                               sm[:, so + 3:so + 4]))
            v_var = self.vfence()
            s.wait_ge(self.V.sem, v_var)
            S.inc(s.activation(sm[:, so + 3:so + 4], sm[:, so + 2:so + 3],
                               AF.Ln, bias=LN_EPS))
            S.wait(s)
            S.inc(s.activation(self.rstd_all[:, t:t + 1], sm[:, so + 3:so + 4],
                               AF.Exp, scale=-0.5))
            s_exp = self.sfence()
            st_rec[par] = dict(v=v_var, s=s_exp)
            stat_users[t] = (v_var, s_exp)

        def issue_dma(j):
            src_ap, width, _ = jobs[j]
            if j >= 2:
                sy.wait_ge(V.sem, jrec[j - 2][0])
                sy.wait_ge(S.sem, jrec[j - 2][1])
            sW.inc(sy.dma_start(out=self.wstg[:, (j % 2) * 1408:
                                              (j % 2) * 1408 + width],
                                in_=src_ap))
            swm[j] = sW.n

        trB = ps["tB"].bitcast(BF16)
        pend = None
        pend_evac = None
        evac_hist = [0, 0]
        prcnt = [0]

        def flush_evac():
            nonlocal pend_evac
            if pend_evac is None:
                return
            trX, ntr, dst, tm = pend_evac
            pend_evac = None
            v.wait_ge(self.T.sem, tm)
            V.inc(v.tensor_copy(dst, trX[:, 0:ntr * P]
                                .rearrange("p (c q) -> p c q", q=P)))
            evac_hist.append(self.V.n)
            self._wq_evac = self.V.n

        def flush_pend():
            nonlocal pend, pend_evac
            if pend is None:
                return
            tok, pp, w, dst = pend
            pend = None
            tpm = self._tr_hist[-2] if len(self._tr_hist) >= 2 else 0
            self.T.wait(gp, tpm)
            gd = self.emit_qtail(tok, self.wqb[:, pp * D:pp * D + w])
            ntr = w // P
            trX = trA if prcnt[0] % 2 == 0 else trB
            prcnt[0] += 1
            G.wait(te, gd)
            te.wait_ge(V.sem, evac_hist[-2])
            for i in range(ntr):
                inst = te.transpose(trX[:, i * P:(i + 1) * P],
                                    self.wqb[:, pp * D + i * P:
                                             pp * D + (i + 1) * P],
                                    self.ident)
            T.inc(inst)
            self._tr_hist.append(self.T.n)
            flush_evac()
            pend_evac = (trX, ntr, dst, self.T.n)

        issue_dma(0)
        echo0 = self.dma_echo(sW, swm[0])
        issue_stat_dma(0)
        issue_stat_dma(1)
        stat_t = 0
        for j, (src_ap, width, chains) in enumerate(jobs):
            if j + 1 < len(jobs):
                issue_dma(j + 1)
            if j % 2 == 1 and stat_t < NT:
                emit_stats_tile(stat_t)
                issue_stat_dma(stat_t + 2)
                stat_t += 1
            stg = self.wstg[:, (j % 2) * 1408:(j % 2) * 1408 + width]
            for c0, w, dst in chains:
                pp = self._qn & 1
                sW.wait(v, swm[j])
                sW.wait(s, swm[j])
                if j == 0:
                    G.wait(v, echo0)
                    G.wait(s, echo0)
                tok = self.emit_qhead(stg[:, c0:c0 + w], w)
                flush_pend()
                pend = (tok, pp, w, dst)
            jrec[j] = (self.V.n, self.S.n)
        flush_pend()
        flush_evac()
        while stat_t < NT:
            emit_stats_tile(stat_t)
            issue_stat_dma(stat_t + 2)
            stat_t += 1
        self._stats_v_final = self.V.n
        self._stats_s_final = self.S.n

    def emit_C1(self):
        nc = self.nc
        v, s, te, gp, sy = nc.vector, nc.scalar, nc.tensor, nc.gpsimd, nc.sync
        V, S, T, G, sX, sP = self.V, self.S, self.T, self.G, self.sX, self.sP
        ps = self.ps
        trA = ps["tA"].bitcast(BF16)
        trB = ps["tB"].bitcast(BF16)
        NCH = 6
        sm = self.smalls
        x_users = {}
        part_done = {}
        p_dma = {}
        sxm = {}
        silu_hist = []
        hm_hist = []
        trb_hist = []
        xq_s_hi = self._stats_s_final
        tmx_prev = 0
        xqT_evac = self._wq_evac
        hqT_evac = self._wq_evac
        pend = None
        pend_ed = None
        ehist = [self._wq_evac, self._wq_evac]
        c1ecnt = [0]

        def issue_x_dma(t):
            if t >= NT:
                return
            if t <= 1:
                sy.wait_ge(V.sem, self._stats_v_final)
                sy.wait_ge(S.sem, self._stats_s_final)
            else:
                sy.wait_ge(V.sem, x_users[t - 2])
            sX.inc(sy.dma_start(out=self.xf[:, (t % 2) * D:(t % 2 + 1) * D],
                                in_=self.x_h[t * P:(t + 1) * P, :]))
            sxm[t] = sX.n

        def emit_stats_xn(t):
            par = t % 2
            xt = self.xf[:, par * D:(par + 1) * D]
            sX.wait(v, sxm[t])
            v.wait_ge(self.S.sem, xq_s_hi)
            V.inc(v.tensor_scalar(self.xo, xt, self.mu_all[:, t:t + 1],
                                  self.rstd_all[:, t:t + 1],
                                  AL.subtract, AL.mult))
            return self.vfence()

        def emit_trax(gq, t):
            nonlocal tmx_prev, xqT_evac
            G.wait(te, gq)
            te.wait_ge(self.V.sem, xqT_evac)
            for j in range(8):
                inst = te.transpose(trA[:, j * P:(j + 1) * P],
                                    self.xq[:, j * P:(j + 1) * P], self.ident)
            T.inc(inst)
            tmx_prev = self.T.n
            v.wait_ge(self.T.sem, tmx_prev)
            V.inc(v.tensor_copy(self.xqT, trA))
            xqT_evac = self.vfence()
            te.wait_ge(self.V.sem, xqT_evac)

        tgu_d = {}

        def emit_gu(ic, nlag):
            w = INNER if ic < 5 else (IFF - 5 * INNER)
            c0 = ic * INNER
            gb, ub = (("q", "k") if ic % 2 == 0 else ("v", "r"))
            if len(silu_hist) >= nlag:
                te.wait_ge(self.S.sem, silu_hist[-nlag])
                te.wait_ge(self.V.sem, hm_hist[-nlag])
            for kc in range(8):
                nc.tensor.matmul(ps[gb][:, 0:w],
                                 self.xqT[:, kc * P:(kc + 1) * P],
                                 self.gateT[:, kc * IFF + c0:
                                            kc * IFF + c0 + w],
                                 start=(kc == 0), stop=(kc == 7))
            for kc in range(8):
                inst = nc.tensor.matmul(ps[ub][:, 0:w],
                                        self.xqT[:, kc * P:(kc + 1) * P],
                                        self.upT[:, kc * IFF + c0:
                                                 kc * IFF + c0 + w],
                                        start=(kc == 0), stop=(kc == 7))
            T.inc(inst)
            tgu_d[ic] = self.T.n

        issue_x_dma(0)
        issue_x_dma(1)
        v_xn = emit_stats_xn(0)
        V.wait(s, v_xn)
        tokXQ = self.emit_qhead(self.xo, D)
        xq_s_hi = tokXQ["s_hi"]
        self.T.wait(gp, tmx_prev)
        gq = self.emit_qtail(tokXQ, self.xq)
        emit_trax(gq, 0)
        emit_gu(0, nlag=2)

        for t in range(NT):
            def flush_evac_down(t=t):
                nonlocal pend_ed, hqT_evac
                if pend_ed is None:
                    return
                icp, w2, ic2, half, tm2 = pend_ed
                pend_ed = None
                ntr = (w2 + P - 1) // P
                v.wait_ge(self.T.sem, tm2)
                V.inc(v.tensor_copy(self.hqT[:, icp * INNER:
                                             icp * INNER + ntr * P],
                                    trB[:, half * INNER:half * INNER + ntr * P]))
                hqT_evac = self.vfence()
                te.wait_ge(self.V.sem, hqT_evac)
                if ic2 == 0 and t > 0:
                    te.wait_ge(self.V.sem, part_done[t - 1])
                for nb_, bank in enumerate(("oa", "ob")):
                    for kb in range(ntr):
                        kbg = ic2 * 4 + kb
                        inst2 = nc.tensor.matmul(
                            ps[bank],
                            self.hqT[:, icp * INNER + kb * P:
                                     icp * INNER + (kb + 1) * P],
                            self.downT[:, kbg * D + nb_ * INNER:
                                       kbg * D + (nb_ + 1) * INNER],
                            start=(ic2 == 0 and kb == 0),
                            stop=(ic2 == NCH - 1 and kb == ntr - 1))
                T.inc(inst2)

            def flush_pending(t=t):
                nonlocal pend, pend_ed
                if pend is None:
                    return
                tok2, icp, w2, ic2 = pend
                pend = None
                tbm = trb_hist[-2] if len(trb_hist) >= 2 else 0
                self.T.wait(gp, tbm)
                gq2 = self.emit_qtail(
                    tok2, self.hq[:, icp * INNER:icp * INNER + w2])
                ntr = (w2 + P - 1) // P
                half = c1ecnt[0] % 2
                c1ecnt[0] += 1
                G.wait(te, gq2)
                te.wait_ge(self.V.sem, ehist[-1])
                for j2 in range(ntr):
                    inst2 = te.transpose(trB[:, half * INNER + j2 * P:
                                             half * INNER + (j2 + 1) * P],
                                         self.hq[:, icp * INNER + j2 * P:
                                                 icp * INNER + (j2 + 1) * P],
                                         self.ident)
                T.inc(inst2)
                trb_hist.append(self.T.n)
                flush_evac_down(t)
                pend_ed = (icp, w2, ic2, half, self.T.n)
                ehist.append(hqT_evac)

            par = t % 2
            xt = self.xf[:, par * D:(par + 1) * D]
            for ic in range(NCH):
                w = INNER if ic < 5 else (IFF - 5 * INNER)
                c0 = ic * INNER
                icpar = ic % 2
                gb, ub = (("q", "k") if icpar == 0 else ("v", "r"))
                if ic + 1 < NCH:
                    emit_gu(ic + 1, nlag=1)
                tgu = tgu_d[ic]
                s.wait_ge(self.T.sem, tgu)
                if len(hm_hist) >= 2:
                    s.wait_ge(self.V.sem, hm_hist[-2])
                sg = self.sg_sb[:, icpar * INNER: icpar * INNER + w]
                S.inc(s.activation(sg, ps[gb][:, 0:w], AF.Silu))
                silu_hist.append(self.S.n)
                hc = self.hch[:, icpar * INNER: icpar * INNER + w]
                v.wait_ge(self.T.sem, tgu)
                v.wait_ge(self.S.sem, silu_hist[-1])
                if ic == 0 and t >= 1:
                    sP.wait(v, p_dma[t - 1])
                V.inc(v.tensor_tensor(hc, ps[ub][:, 0:w], sg, AL.mult))
                hm_hist.append(self.V.n)
                V.wait(s, hm_hist[-1])
                tok = self.emit_qhead(hc, w)
                flush_pending()
                pend = (tok, icpar, w, ic)
            if t + 1 < NT:
                v_xn = emit_stats_xn(t + 1)
                V.wait(s, v_xn)
                tokXQ = self.emit_qhead(self.xo, D)
                xq_s_hi_t = tokXQ["s_hi"]
            s_hi5 = pend[0]["s_hi"]
            flush_pending()
            flush_evac_down(t)
            tdown = self.T.n
            v.wait_ge(self.T.sem, tdown)
            v.wait_ge(self.S.sem, s_hi5)
            V.inc(v.tensor_tensor(self.hch[:, 0:INNER], ps["oa"],
                                  xt[:, 0:INNER], AL.add))
            V.inc(v.tensor_tensor(self.hch[:, INNER:D], ps["ob"],
                                  xt[:, INNER:D], AL.add))
            part_done[t] = self.V.n
            x_users[t] = self.V.n
            sy.wait_ge(self.V.sem, part_done[t])
            sP.inc(sy.dma_start(out=self.part_h[t * P:(t + 1) * P, :],
                                in_=self.hch[:, 0:D]))
            p_dma[t] = self.sP.n
            issue_x_dma(t + 2)
            if t + 1 < NT:
                xq_s_hi = xq_s_hi_t
                self.T.wait(gp, tmx_prev)
                gq = self.emit_qtail(tokXQ, self.xq)
                emit_trax(gq, t + 1)
                emit_gu(0, nlag=2)

    def emit_A(self):
        nc = self.nc
        v, s, te, gp, sy = nc.vector, nc.scalar, nc.tensor, nc.gpsimd, nc.sync
        V, S, T, G, sX, sW = self.V, self.S, self.T, self.G, self.sX, self.sW
        ps = self.ps
        trA = ps["tA"].bitcast(BF16)
        trB = ps["tB"].bitcast(BF16)
        for nm in ("qT", "kT", "vaug", "wqkv_b", "lnw_rep", "lnb_rep", "xn",
                   "sqs2", "xn2b", "xn2T", "qkb", "sqs", "wkstg"):
            self.sb(nm)
        sm = self.smalls

        ones_ap = bass.AP(self._handles["vaug"], 64, [[520 * NT, P], [520, NT], [65, 8]])
        G.inc(gp.memset(ones_ap, 1.0))
        sX.inc(sy.dma_start(out=self.lnw_rep,
                            in_=self.lnw_h.rearrange("(o d) -> o d", o=1).broadcast_to([P, D])))
        sX.inc(sy.dma_start(out=self.lnb_rep,
                            in_=self.lnb_h.rearrange("(o d) -> o d", o=1).broadcast_to([P, D])))
        for kc in range(8):
            for hh in range(2):
                idx = kc * 2 + hh
                par = idx % 2
                stg = self.wkstg[:, par * 768:(par + 1) * 768]
                if idx >= 2:
                    sy.wait_ge(V.sem, self._wk_v[idx - 2])
                sW.inc(sy.dma_start(out=stg, in_=self.wqkv_h[kc * P:(kc + 1) * P,
                                                             hh * 768:(hh + 1) * 768]))
                sW.wait(v)
                if idx == 0:
                    G.wait(v, self.dma_echo(sW, self.sW.n))
                self._wk_v = getattr(self, "_wk_v", [0] * 16)
                V.inc(v.tensor_copy(
                    self.wqkv_b[:, kc * 1536 + hh * 768: kc * 1536 + (hh + 1) * 768],
                    stg))
                self._wk_v[idx] = self.V.n
        for idx in (16, 17):
            kc, hh = 0, idx - 16
            par = idx % 2
            stg = self.wkstg[:, par * 768:(par + 1) * 768]
            sy.wait_ge(V.sem, self._wk_v[idx - 2])
            sW.inc(sy.dma_start(out=stg, in_=self.wqkv_h[kc * P:(kc + 1) * P,
                                                         hh * 768:(hh + 1) * 768]))
            sW.wait(v)
            V.inc(v.tensor_copy(
                self.wqkv_b[:, kc * 1536 + hh * 768: kc * 1536 + (hh + 1) * 768],
                stg))
        x_users = {}
        s_users = {}
        evac_qkv = {}
        evac_tr = {}
        c1_v_final = self.V.n
        c1_s_final = self.S.n
        c1_g_final = self.G.n
        for t in range(NT):
            par = t % 2
            xt = self.xf[:, par * D:(par + 1) * D]
            if t >= 2:
                sy.wait_ge(V.sem, x_users.get(t - 2, 0))
                sy.wait_ge(S.sem, s_users.get(t - 2, 0))
            else:
                sy.wait_ge(V.sem, c1_v_final)
                sy.wait_ge(S.sem, c1_s_final)
                sy.wait_ge(G.sem, c1_g_final)
            sX.inc(sy.dma_start(out=xt, in_=self.x_h[t * P:(t + 1) * P, :]))
            sX.wait(v)
            sX.wait(s)
            V.inc(v.tensor_scalar(self.xn, xt, self.mu_all[:, t:t + 1],
                                  self.rstd_all[:, t:t + 1], AL.subtract, AL.mult))
            x_users[t] = self.V.n
            V.wait(v)
            V.wait(s)
            V.inc(v.tensor_reduce(sm[:, 8:9], self.xn, AX.X, AL.add))
            S.inc(s.activation(self.sqs2, self.xn, AF.Square, accum_out=sm[:, 9:10]))
            s_users[t] = self.sfence()
            V.wait(v)
            V.inc(v.tensor_scalar(sm[:, 10:11], sm[:, 8:9], 1.0 / D, None, AL.mult))
            S.wait(v)
            V.inc(v.tensor_scalar(sm[:, 11:12], sm[:, 9:10], 1.0 / D, None, AL.mult))
            V.wait(v)
            V.inc(v.tensor_tensor(sm[:, 12:13], sm[:, 10:11], sm[:, 10:11], AL.mult))
            V.wait(v)
            V.inc(v.tensor_sub(sm[:, 11:12], sm[:, 11:12], sm[:, 12:13]))
            self.vfence()
            V.wait(s)
            S.inc(s.activation(sm[:, 12:13], sm[:, 11:12], AF.Ln, bias=LN_EPS))
            S.wait(s)
            S.inc(s.activation(sm[:, 13:14], sm[:, 12:13], AF.Exp, scale=-0.5))
            self.sfence()
            S.wait(v)
            V.inc(v.tensor_scalar(self.sqs2, self.xn, sm[:, 10:11], sm[:, 13:14],
                                  AL.subtract, AL.mult))
            V.wait(v)
            V.inc(v.tensor_tensor(self.sqs2, self.sqs2, self.lnw_rep, AL.mult))
            V.wait(v)
            V.inc(v.tensor_tensor(self.xn2b, self.sqs2, self.lnb_rep, AL.add))
            vm_xn2b = self.vfence()
            te.wait_ge(self.V.sem, vm_xn2b)
            for j in range(8):
                inst = te.transpose(trA[:, j * P:(j + 1) * P],
                                    self.xn2b[:, j * P:(j + 1) * P], self.ident)
            T.inc(inst)
            tm1 = self.T.n
            v.wait_ge(self.T.sem, tm1)
            V.inc(v.tensor_copy(self.xn2T, trA))
            vm_xn2T = self.vfence()
            banks = ("q", "k", "v") if par == 0 else ("oa", "ob", "r")
            te.wait_ge(self.V.sem, vm_xn2T)
            if t >= 2:
                te.wait_ge(self.V.sem, evac_qkv[t - 2])
            for bank_i, psn in enumerate(banks):
                for kc in range(8):
                    inst = nc.tensor.matmul(
                        ps[psn], self.xn2T[:, kc * P:(kc + 1) * P],
                        self.wqkv_b[:, kc * 1536 + bank_i * INNER:
                                    kc * 1536 + (bank_i + 1) * INNER],
                        start=(kc == 0), stop=(kc == 7))
            T.inc(inst)
            tqkv = self.T.n
            s.wait_ge(self.T.sem, tqkv)
            v.wait_ge(self.T.sem, tqkv)
            V.inc(v.tensor_copy(self.sqs, ps[banks[0]]))
            V.wait(v)
            V.inc(v.tensor_tensor(self.sqs, self.sqs, self.sqs, AL.mult))
            V.wait(v)
            V.inc(v.tensor_reduce(sm[:, 16:24],
                                  self.sqs.rearrange("p (h c) -> p h c", c=DH),
                                  AX.X, AL.add))
            self.vfence()
            V.wait(s)
            S.inc(s.activation(sm[:, 24:32], sm[:, 16:24], AF.Ln, bias=L2_EPS))
            S.wait(s)
            S.inc(s.activation(sm[:, 32:40], sm[:, 24:32], AF.Exp, scale=-0.5, bias=LOG8))
            self.sfence()
            S.wait(v)
            V.inc(v.tensor_tensor(self.qkb[:, 0:INNER].rearrange("p (h c) -> p h c", c=DH),
                                  ps[banks[0]].rearrange("p (h c) -> p h c", c=DH),
                                  sm[:, 32:40].broadcast_to([P, H, DH]), AL.mult))
            V.inc(v.tensor_copy(self.sqs, ps[banks[1]]))
            V.wait(v)
            V.inc(v.tensor_tensor(self.sqs, self.sqs, self.sqs, AL.mult))
            V.wait(v)
            V.inc(v.tensor_reduce(sm[:, 16:24],
                                  self.sqs.rearrange("p (h c) -> p h c", c=DH),
                                  AX.X, AL.add))
            self.vfence()
            V.wait(s)
            S.inc(s.activation(sm[:, 24:32], sm[:, 16:24], AF.Ln, bias=L2_EPS))
            S.wait(s)
            S.inc(s.activation(sm[:, 32:40], sm[:, 24:32], AF.Exp, scale=-0.5))
            s_users[t] = self.sfence()
            S.wait(v)
            V.inc(v.tensor_tensor(self.qkb[:, INNER:2 * INNER]
                                  .rearrange("p (h c) -> p h c", c=DH),
                                  ps[banks[1]].rearrange("p (h c) -> p h c", c=DH),
                                  sm[:, 32:40].broadcast_to([P, H, DH]), AL.mult))
            vdst = bass.AP(self._handles["vaug"], t * 520,
                           [[520 * NT, P], [65, 8], [1, DH]])
            V.inc(v.tensor_copy(vdst, ps[banks[2]].rearrange("p (h c) -> p h c", c=DH)))
            evac_qkv[t] = self.V.n
            vm_qkb = self.vfence()
            te.wait_ge(self.V.sem, vm_qkb)
            if t >= 1:
                te.wait_ge(self.V.sem, evac_tr[t - 1])
            for j in range(8):
                inst = te.transpose(trB[:, j * P:(j + 1) * P],
                                    self.qkb[:, j * P:(j + 1) * P], self.ident)
            T.inc(inst)
            tm2 = self.T.n
            v.wait_ge(self.T.sem, tm2)
            qdst = bass.AP(self._handles["qT"], t * P, [[4 * SEQ, P], [SEQ, 4], [1, P]])
            kdst = bass.AP(self._handles["kT"], t * P, [[4 * SEQ, P], [SEQ, 4], [1, P]])
            V.inc(v.tensor_copy(qdst, trB[:, 0:INNER].rearrange("p (c q) -> p c q", q=P)))
            V.inc(v.tensor_copy(kdst, trB[:, INNER:2 * INNER]
                                .rearrange("p (c q) -> p c q", q=P)))
            evac_tr[t] = self.V.n
            x_users[t] = self.V.n

    def emit_B(self):
        nc = self.nc
        v, s, te, gp, sy = nc.vector, nc.scalar, nc.tensor, nc.gpsimd, nc.sync
        V, S, T, G, sO = self.V, self.S, self.T, self.G, self.sO
        ps = self.ps
        for nm in ("oT_sb", "pt_a0", "pt_a1", "pt_b0", "pt_b1", "rsrow", "rsln",
                   "rrep", "oTs", "maskb"):
            self.sb(nm)
        self.T.wait(gp)
        self.V.wait(gp)
        self.S.wait(gp)
        G.inc(gp.memset(self.maskb, 1.0))
        G.wait(gp)
        G.inc(gp.affine_select(out=self.maskb, in_=self.maskb, compare_op=AL.is_ge,
                               fill=0.0, base=-INNER, pattern=[[1, 2 * INNER]],
                               channel_multiplier=-1))
        G.wait(v)
        G.wait(te)
        self.vfence()
        V.wait(te)
        S.wait(te)
        vm_b0 = self.V.n
        sa_banks = [ps["q"], ps["k"]]
        sb_banks = [ps["v"], ps["r"]]
        pta = [self.pt_a0, self.pt_a1]
        ptb = [self.pt_b0, self.pt_b1]
        trR = [ps["tA"], ps["tB"]]

        vm_epi = vm_b0
        vm_rrep = 0
        pend_epi = None
        self._b_pend_odd = None
        self._b_last_av = 0

        def emit_epilogue():
            nonlocal vm_epi, vm_rrep, pend_epi
            if pend_epi is None:
                return
            tmav, coff = pend_epi
            pend_epi = None
            v.wait_ge(self.T.sem, tmav)
            V.inc(v.tensor_copy(self.rsrow[64:65, 0:INNER], ps["oa"][64:65, :]))
            V.inc(v.tensor_copy(self.rsrow[64:65, INNER:], ps["ob"][64:65, :]))
            self.vfence()
            V.wait(s)
            S.inc(s.activation(self.rsln[64:65, :], self.rsrow[64:65, :], AF.Ln))
            S.wait(s)
            S.inc(s.activation(self.rsrow[64:65, :], self.rsln[64:65, :],
                               AF.Exp, scale=-1.0))
            sm_rec = self.sfence()
            te.wait_ge(self.S.sem, sm_rec)
            if vm_rrep:
                te.wait_ge(self.V.sem, vm_rrep)
            nc.tensor.matmul(trR[0][0:64, :], self.ones65[64:65, 0:64],
                             self.rsrow[64:65, 0:INNER], start=True, stop=True)
            inst = nc.tensor.matmul(trR[1][0:64, :], self.ones65[64:65, 0:64],
                                    self.rsrow[64:65, INNER:], start=True, stop=True)
            T.inc(inst)
            tm_rep = self.T.n
            v.wait_ge(self.T.sem, tm_rep)
            V.inc(v.tensor_copy(self.rrep[:, 0:INNER], trR[0][0:64, :]))
            V.inc(v.tensor_copy(self.rrep[:, INNER:], trR[1][0:64, :]))
            vm_rrep = self.V.n
            V.wait(v)
            V.inc(v.tensor_tensor(
                self.oT_sb[0:64, coff:coff + INNER], ps["oa"][0:64, :],
                self.rrep[:, 0:INNER], AL.mult))
            v.wait_ge(self.sO.sem, self.sO.n)
            V.inc(v.tensor_tensor(self.oTs[:, 0:INNER], ps["ob"][0:64, :],
                                  self.rrep[:, INNER:], AL.mult))
            vm_epi = self.V.n
            self._b_pend_odd = (coff, vm_epi)
        for hp in range(4):
            for ci in range(8):
                nj = 4 * ci + 4
                qs = self.qT[:, hp * SEQ + ci * INNER: hp * SEQ + (ci + 1) * INNER]
                tm_sp = {}
                sm_exp = {}
                vm_msk = {}
                tm_av = {}

                def emit_spair(j):
                    par = j % 2
                    ks = self.kT[:, hp * SEQ + j * P: hp * SEQ + (j + 1) * P]
                    if j >= 2:
                        te.wait_ge(self.S.sem, sm_exp[j - 2])
                    elif (hp, ci) != (0, 0):
                        te.wait_ge(self.S.sem, self._b_last_exp)
                    nc.tensor.matmul(sa_banks[par], ks[0:64, :], qs[0:64, :],
                                     start=True, stop=True, tile_position=(0, 0))
                    inst = nc.tensor.matmul(sb_banks[par], ks[64:128, :], qs[64:128, :],
                                            start=True, stop=True, tile_position=(64, 0))
                    T.inc(inst)
                    tm_sp[j] = self.T.n
                    s.wait_ge(self.T.sem, tm_sp[j])
                    if j >= 2:
                        s.wait_ge(self.T.sem, tm_av[j - 2])
                    elif self._b_last_av:
                        s.wait_ge(self.T.sem, self._b_last_av)
                    s.activation(pta[par], sa_banks[par], AF.Exp)
                    S.inc(s.activation(ptb[par], sb_banks[par], AF.Exp))
                    sm_exp[j] = self.sfence() if j == nj - 1 else self.S.n
                    if j >= 4 * ci:
                        r = j * P - ci * INNER
                        msl = self.maskb[:, INNER - r: 2 * INNER - r]
                        v.wait_ge(self.S.sem, sm_exp[j])
                        v.tensor_tensor(pta[par], pta[par], msl, AL.mult)
                        V.inc(v.tensor_tensor(ptb[par], ptb[par], msl, AL.mult))
                        vm_msk[j] = self.vfence()

                def emit_av(j):
                    par = j % 2
                    if j in vm_msk:
                        te.wait_ge(self.V.sem, vm_msk[j])
                    else:
                        te.wait_ge(self.S.sem, sm_exp[j])
                    if j == 0:
                        te.wait_ge(self.V.sem, vm_epi)
                    va = self.vaug[:, j * 520 + (2 * hp) * 65:
                                   j * 520 + (2 * hp) * 65 + 65]
                    vb = self.vaug[:, j * 520 + (2 * hp + 1) * 65:
                                   j * 520 + (2 * hp + 1) * 65 + 65]
                    nc.tensor.matmul(ps["oa"][0:65, :], va, pta[par],
                                     start=(j == 0), stop=(j == nj - 1))
                    inst = nc.tensor.matmul(ps["ob"][0:65, :], vb, ptb[par],
                                            start=(j == 0), stop=(j == nj - 1))
                    T.inc(inst)
                    tm_av[j] = self.T.n

                emit_spair(0)
                if self._b_pend_odd is not None:
                    coff_p, vm_p = self._b_pend_odd
                    self._b_pend_odd = None
                    sy.wait_ge(self.V.sem, vm_p)
                    sy.wait_ge(self.T.sem, tm_sp[0])
                    sO.inc(sy.dma_start(
                        out=self.oT_sb[64:128, coff_p:coff_p + INNER],
                        in_=self.oTs[:, 0:INNER]))
                emit_epilogue()
                for j in range(nj):
                    if j + 1 < nj:
                        emit_spair(j + 1)
                    emit_av(j)
                self._b_last_exp = sm_exp[nj - 1]
                self._b_last_av = tm_av[nj - 1]
                pend_epi = (tm_av[nj - 1], hp * SEQ + ci * INNER)
        if self._b_pend_odd is not None:
            coff_p, vm_p = self._b_pend_odd
            self._b_pend_odd = None
            gp.wait_ge(self.V.sem, vm_p)
            G.inc(gp.memset(self.smalls[:, 61:62], 0.0))
            sy.wait_ge(G.sem, self.G.n)
            sO.inc(sy.dma_start(out=self.oT_sb[64:128, coff_p:coff_p + INNER],
                                in_=self.oTs[:, 0:INNER]))
        emit_epilogue()
        coff_p, vm_p = self._b_pend_odd
        self._b_pend_odd = None
        gp.wait_ge(self.V.sem, vm_p)
        G.inc(gp.memset(self.smalls[:, 61:62], 0.0))
        sy.wait_ge(G.sem, self.G.n)
        sO.inc(sy.dma_start(out=self.oT_sb[64:128, coff_p:coff_p + INNER],
                            in_=self.oTs[:, 0:INNER]))

    def emit_C2(self):
        nc = self.nc
        v, s, te, gp, sy = nc.vector, nc.scalar, nc.tensor, nc.gpsimd, nc.sync
        V, S, T, sP, sY, sW = self.V, self.S, self.T, self.sP, self.sY, self.sW
        ps = self.ps
        for nm in ("pbuf", "obuf", "wout_b"):
            self.sb(nm)
        self.V.wait(sy)
        self.S.wait(sy)
        self.T.wait(sy)
        for kc in range(4):
            for hh in range(2):
                idx = kc * 2 + hh
                par = idx % 2
                stg = self.obuf[:, par * D: par * D + INNER]
                if idx >= 2:
                    sy.wait_ge(V.sem, self._wo_v[idx - 2])
                sW.inc(sy.dma_start(out=stg, in_=self.wout_h[kc * P:(kc + 1) * P,
                                                             hh * 512:(hh + 1) * 512]))
                sW.wait(v)
                if idx == 0:
                    self.G.wait(v, self.dma_echo(sW, self.sW.n))
                self._wo_v = getattr(self, "_wo_v", [0] * 8)
                V.inc(v.tensor_copy(
                    self.wout_b[:, kc * D + hh * 512: kc * D + (hh + 1) * 512], stg))
                self._wo_v[idx] = self.V.n
        for idx in (8, 9):
            kc, hh = 0, idx - 8
            par = idx % 2
            stg = self.obuf[:, par * D: par * D + INNER]
            sy.wait_ge(V.sem, self._wo_v[idx - 2])
            sW.inc(sy.dma_start(out=stg, in_=self.wout_h[kc * P:(kc + 1) * P,
                                                         hh * 512:(hh + 1) * 512]))
            sW.wait(v)
            V.inc(v.tensor_copy(
                self.wout_b[:, kc * D + hh * 512: kc * D + (hh + 1) * 512], stg))
        self.sO.wait(sy)
        self.sO.wait(te)
        self.vfence()
        V.wait(te)
        out_done = {}
        res_done = {}
        for t in range(NT):
            par = t % 2
            pt_ = self.pbuf[:, par * D:(par + 1) * D]
            ot_ = self.obuf[:, par * D:(par + 1) * D]
            if t >= 2:
                sy.wait_ge(V.sem, res_done[t - 2])
            sP.inc(sy.dma_start(out=pt_, in_=self.part_h[t * P:(t + 1) * P, :]))
            bpair = ("oa", "ob") if par == 0 else ("q", "k")
            if t >= 2:
                te.wait_ge(self.V.sem, res_done[t - 2])
            for nb, bank in enumerate(bpair):
                for cb in range(4):
                    inst = nc.tensor.matmul(
                        ps[bank], self.oT_sb[:, cb * SEQ + t * P: cb * SEQ + (t + 1) * P],
                        self.wout_b[:, cb * D + nb * INNER: cb * D + (nb + 1) * INNER],
                        start=(cb == 0), stop=(cb == 3))
            T.inc(inst)
            tm = self.T.n
            v.wait_ge(self.T.sem, tm)
            sP.wait(v)
            v.wait_ge(self.sY.sem, self.sY.n)
            if t >= 1:
                sy.wait_ge(self.V.sem, res_done[t - 1])
                sy.wait_ge(self.T.sem, tm)
                po_ = self.obuf[:, ((t - 1) % 2) * D:((t - 1) % 2 + 1) * D]
                sY.inc(sy.dma_start(out=self.y_h[(t - 1) * P:t * P, :], in_=po_))
                out_done[t - 1] = self.sY.n
            V.inc(v.tensor_tensor(ot_[:, 0:INNER], ps[bpair[0]], pt_[:, 0:INNER], AL.add))
            V.inc(v.tensor_tensor(ot_[:, INNER:D], ps[bpair[1]], pt_[:, INNER:D], AL.add))
            res_done[t] = self.V.n
        t = NT - 1
        gp = nc.gpsimd
        gp.wait_ge(self.V.sem, res_done[t])
        self.G.inc(gp.memset(self.smalls[:, 61:62], 0.0))
        sy.wait_ge(self.G.sem, self.G.n)
        sY.inc(sy.dma_start(out=self.y_h[t * P:(t + 1) * P, :],
                            in_=self.obuf[:, (t % 2) * D:(t % 2 + 1) * D]))
        out_done[t] = self.sY.n

    def emit_tail(self):
        gp = self.nc.gpsimd
        for c in (self.sY, self.sP, self.sO, self.sX, self.sW,
                  self.T, self.V, self.S, self.G):
            c.wait(gp)


def build():
    return MK2().build()


def kernel(L0_attn_x, L0_attn_ln_w, L0_attn_ln_b, L0_attn_w_qkv, L0_attn_w_out,
           L0_ffn_w_gate, L0_ffn_w_up, L0_ffn_w_down):
    from concourse.bass_utils import run_bass_kernel_spmd
    x = np.ascontiguousarray(np.asarray(L0_attn_x, dtype=np.float32))
    args = dict(ln_w=np.ascontiguousarray(np.asarray(L0_attn_ln_w, dtype=np.float32)),
                ln_b=np.ascontiguousarray(np.asarray(L0_attn_ln_b, dtype=np.float32)),
                w_qkv=np.ascontiguousarray(np.asarray(L0_attn_w_qkv, dtype=np.float32)),
                w_out=np.ascontiguousarray(np.asarray(L0_attn_w_out, dtype=np.float32)),
                w_gate=np.ascontiguousarray(np.asarray(L0_ffn_w_gate, dtype=np.float32)),
                w_up=np.ascontiguousarray(np.asarray(L0_ffn_w_up, dtype=np.float32)),
                w_down=np.ascontiguousarray(np.asarray(L0_ffn_w_down, dtype=np.float32)))
    nc = build()
    in_maps = [dict(x=np.ascontiguousarray(x[b]), **args) for b in range(8)]
    res = run_bass_kernel_spmd(nc, in_maps, core_ids=list(range(8)))
    return np.stack([np.asarray(res.results[b]["y"]) for b in range(8)], axis=0)


if __name__ == "__main__":
    nc = build()
    print("instructions:", sum(1 for _ in nc.all_instructions()))
